# version 01
# baseline (speedup 1.0000x reference)
import numpy as np
import ml_dtypes
import concourse.bass as bass
import concourse.mybir as mybir
from concourse.bass_utils import run_bass_kernel_spmd

F32 = mybir.dt.float32
BF16 = mybir.dt.bfloat16
I32 = mybir.dt.int32
FP8 = mybir.dt.float8e4
AF = mybir.ActivationFunctionType
ALU = mybir.AluOpType
AX = mybir.AxisListType

D = 2048
T = 1024
NCH = 16
NH0 = 16
NIC = 64
SC = float(1.0 / np.sqrt(np.float32(128.0)))
SCH = float(np.sqrt(SC))
EPS = 1e-5

ENGS = ("t", "v", "s", "g", "y")
ARENA_BYTES = 210944


class Prog:
    def __init__(self, nc):
        self.nc = nc
        self.q = {e: [] for e in ENGS}
        self.sems = {}
        self.cnt = {}

    def sem(self, name):
        if name not in self.sems:
            self.sems[name] = self.nc.alloc_semaphore(name)
            self.cnt[name] = 0
        return self.sems[name]

    def emit(self, eng, fn=None, wait=(), inc=None, inc_by=1):
        tok = None
        if inc is not None:
            self.sem(inc)
            self.cnt[inc] += inc_by
            tok = self.cnt[inc]
        for (sn, _v) in wait:
            self.sem(sn)
        waits = tuple((sn, v) for (sn, v) in wait if v > 0)
        sems = self.sems

        def job(e, waits=waits, fn=fn, inc=inc, inc_by=inc_by):
            for (sn, val) in waits:
                e.wait_ge(sems[sn], val)
            if fn is not None:
                inst = fn(e)
                if inc is not None:
                    inst.then_inc(sems[inc], inc_by)
            elif inc is not None:
                e.engine_nop().then_inc(sems[inc], inc_by)
        self.q[eng].append(job)
        return tok

    def dma(self, eng, out, in_, inc, wait=(), slow=False):
        if slow:
            return self.emit(eng, lambda e, o=out, i=in_: e.dma_start(
                out=o, in_=i, allow_slow_non_contiguous=True), wait=wait, inc=inc, inc_by=16)
        return self.emit(eng, lambda e, o=out, i=in_: e.dma_start(out=o, in_=i),
                         wait=wait, inc=inc, inc_by=16)

    def tok(self, name):
        self.sem(name)
        return self.cnt[name]

    def replay(self):
        nc = self.nc
        with nc.Block() as block:
            @block.tensor
            def _(t):
                for job in self.q["t"]:
                    job(t)

            @block.vector
            def _(v):
                for job in self.q["v"]:
                    job(v)

            @block.scalar
            def _(s):
                for job in self.q["s"]:
                    job(s)

            @block.gpsimd
            def _(g):
                for job in self.q["g"]:
                    job(g)

            @block.sync
            def _(y):
                for job in self.q["y"]:
                    job(y)


def build_nc(dbg_mode=None):
    nc = bass.Bass("TRN2", target_bir_lowering=False, detect_race_conditions=False)
    p = Prog(nc)
    DT = nc.dram_tensor
    A = nc.alloc_sbuf_tensor
    P_ = nc.alloc_psum_tensor

    x_own = DT("x_own", [D, T], F32, kind="ExternalInput")
    x_part = DT("x_part", [D, T], F32, kind="ExternalInput")
    wq0 = DT("wq0", [NH0, 128, NCH * 128], F32, kind="ExternalInput")
    wk0 = DT("wk0", [NH0, 128, NCH * 128], F32, kind="ExternalInput")
    wv0 = DT("wv0", [4, NCH, 128, 512], F32, kind="ExternalInput")
    bq0 = DT("bq0", [NH0, 128], F32, kind="ExternalInput")
    bk0 = DT("bk0", [NH0, 128], F32, kind="ExternalInput")
    bv0 = DT("bv0", [1, D], F32, kind="ExternalInput")
    wp0 = DT("wp0", [NCH, 128, NCH * 128], F32, kind="ExternalInput")
    bp0 = DT("bp0", [NCH, 128], F32, kind="ExternalInput")
    wgd = DT("wg", [NIC, 128, NCH * 128], I32, kind="ExternalInput")
    wud = DT("wu", [NIC, 128, NCH * 128], I32, kind="ExternalInput")
    wdd = DT("wd", [4, NCH, 128, 16 * 128], I32, kind="ExternalInput")
    gsc = DT("gsc", [NIC, 128], F32, kind="ExternalInput")
    usc = DT("usc", [NIC, 128], F32, kind="ExternalInput")
    dsc = DT("dsc", [NCH, 128], F32, kind="ExternalInput")
    ident = DT("ident", [128, 128], BF16, kind="ExternalInput")
    wpq = DT("wpq", [16, 128, NCH * 128], F32, kind="ExternalInput")
    wpk = DT("wpk", [8, 128, NCH * 128], F32, kind="ExternalInput")
    wpv = DT("wpv", [8, 128, NCH * 128], F32, kind="ExternalInput")
    wod = DT("wod", [NCH, 128, NCH * 128], F32, kind="ExternalInput")
    w1d = DT("w1d", [NCH, 128, NCH * 128], FP8, kind="ExternalInput")
    cosd = DT("cosd", [128, T], F32, kind="ExternalInput")
    sind = DT("sind", [128, T], F32, kind="ExternalInput")
    protd = DT("protd", [128, 128], BF16, kind="ExternalInput")
    maskd = DT("maskd", [128, 8 * 9 * 128], BF16, kind="ExternalInput")
    convd = DT("convd", [1, 32], F32, kind="ExternalInput")
    flagd = DT("flagd", [128, 2], F32, kind="ExternalInput")
    iscl = DT("iscl", [1, 1], BF16, kind="ExternalInput")
    wscl = DT("wscl", [1, 1], F32, kind="ExternalInput")
    kv_out = DT("kv_out", [2, 8, 128, T], BF16)
    kv_gth = DT("kv_gth", [2, 2, 8, 128, T], BF16)

    y = DT("y", [128, NCH * T], F32, kind="ExternalOutput")
    dbg = DT("dbg", [128, 2 * NCH * T], F32, kind="ExternalOutput") if dbg_mode else None

    AR = A("AR", [128, ARENA_BYTES // 2], BF16)

    def BV(off, cols):
        assert off % 2 == 0 and off + 2 * cols <= ARENA_BYTES, (off, cols)
        return AR[:, off // 2: off // 2 + cols]

    def FV(off, cols):
        assert off % 4 == 0 and off + 4 * cols <= ARENA_BYTES, (off, cols)
        return AR[:, off // 2: off // 2 + 2 * cols].bitcast(F32)

    ones_c = A("ones_c", [128, 1], F32)
    ones_cb = A("ones_cb", [128, 1], BF16)
    ones_r = A("ones_r", [1, 128], F32)
    idn = A("idn", [128, 128], BF16)
    biasqk = A("biasqk", [128, 4], F32)
    atm = A("atm", [128, 3 * 128], BF16)
    dnorm = A("dnorm", [128, 2], F32)
    bpt = A("bpt", [128, NCH], F32)
    fsc = A("fsc", [128, 40], F32)

    xn = BV(0, 32768)
    x_acc = FV(65536, 16384)
    XF = FV(65536, 16384)
    probT = BV(65536, 8192)
    kh_t = BV(81920, 4096)
    qh_t = BV(90112, 2048)
    sT = FV(94208, 1024)
    SRQ = FV(98304, 1024)
    SRM = FV(102400, 1024)
    SRK = FV(106496, 1024)
    SRS = FV(110592, 8)
    mub = FV(110624, 1024)
    ksq = BV(114720, 1024)
    qsq = BV(116768, 1024)
    bvb = FV(118816, 2048)
    xw = FV(131072, 2048)
    sq2 = FV(139264, 2048)
    mb = FV(147456, 2048)
    ST_mo = FV(155648, 1024)
    ST_ro = FV(159744, 1024)
    ST_mp = FV(163840, 1024)
    ST_rp = FV(167936, 1024)
    ST_v = FV(172032, 1024)
    att0 = BV(131072, 16384)
    vg_t = BV(163840, 16 * 4 * 129)
    wvg = BV(180352, 8192)
    whead = BV(196736, 4096)
    wpt = BV(163840, 4096)
    hbuf = BV(131072, 16384)
    wgt = BV(163840, 4096)
    wut = BV(172032, 4096)
    wdt = BV(180224, 4096)
    rbc = FV(188416, 1024)
    ffa = FV(192512, 2048)
    ffm = FV(200704, 1024)
    SBC = FV(204800, 1024)
    SBCn = FV(208896, 512)

    pA = P_("pA", [128, 1024], F32)
    pB = P_("pB", [128, 1024], F32)
    pC = P_("pC", [128, 1024], F32)
    pD = P_("pD", [128, 512], F32)
    pT = P_("pT", [128, 1024], BF16)

    p.dma("y", idn[:, :], ident[:, :], inc="xc")
    p.emit("v", lambda e: e.memset(ones_c[:, :], 1.0), inc="vcst")
    p.emit("v", lambda e: e.memset(ones_cb[:, :], 1.0), inc="vcst")
    cst_tok = p.emit("v", lambda e: e.memset(ones_r[:, :], 1.0), inc="vcst")

    def bcast_row(row_ap, out_ap, n, psum, wait, incname):
        mts = None
        for s0 in range(0, n, 512):
            s1 = min(s0 + 512, n)
            mts = p.emit("t", lambda e, o=psum[:, s0:s1], r=row_ap[:, s0:s1]:
                         e.matmul(o, ones_r[0:1, :], r, start=True, stop=True),
                         wait=wait if s0 == 0 else (), inc="mm")
        return p.emit("v", lambda e, o=out_ap[:, 0:n], i=psum[:, 0:n]: e.tensor_copy(o, i),
                      wait=[("mm", mts)], inc=incname)

    def ln0(src_dram, which, start_wait, mrow, rrow):
        dts = []
        chunk_mm = {}
        for ci in range(NCH):
            sem = "xl%d" % (ci % 2)
            w = list(start_wait)
            if ci >= 2:
                w.append(("mm", 0))
            if which == 0 and ci == 0:
                w.append(("vcst", cst_tok))
            w = [x for x in w if x[1] != 0]
            if ci >= 2:
                w.append(("mm", chunk_mm[ci - 2]))
            dts.append(p.dma("y", XF[:, ci * T:(ci + 1) * T],
                             src_dram[128 * ci:128 * (ci + 1), :], inc=sem, wait=w))
            sl = ci % 2
            sv = sq2[:, sl * T:(sl + 1) * T]
            xv = XF[:, ci * T:(ci + 1) * T]
            w = [(sem, p.tok(sem))]
            vt = p.emit("v", lambda e, a=sv, b=xv: e.tensor_mul(a, b, b), wait=w, inc="vln")
            for nh in range(2):
                s0, s1 = nh * 512, (nh + 1) * 512
                w0 = []
                if ci == 0 and nh == 0 and which == 1:
                    w0 = [("vln", ln0.stats_rd)]
                p.emit("t", lambda e, o=pA[0:1, s0:s1], r=xv[:, s0:s1], st=(ci == 0), sp=(ci == NCH - 1):
                       e.matmul(o, ones_c[:, 0:1], r, start=st, stop=sp),
                       wait=w0 + [(sem, p.tok(sem))])
            mt = None
            for nh in range(2):
                s0, s1 = nh * 512, (nh + 1) * 512
                mt = p.emit("t", lambda e, o=pB[0:1, s0:s1], r=sv[:, s0:s1], st=(ci == 0), sp=(ci == NCH - 1):
                            e.matmul(o, ones_c[:, 0:1], r, start=st, stop=sp),
                            wait=[("vln", vt)] if nh == 0 else (), inc="mm")
            chunk_mm[ci] = mt
        mmtok = chunk_mm[NCH - 1]
        vrow = ST_v[0:1, :]
        p.emit("v", lambda e, o=mrow[0:1, :]: e.tensor_scalar(o, pA[0:1, :], 1.0 / D, None, ALU.mult),
               wait=[("mm", mmtok)], inc="vln")
        p.emit("v", lambda e: e.tensor_scalar(vrow, pB[0:1, :], 1.0 / D, EPS, ALU.mult, ALU.add), inc="vln")
        p.emit("v", lambda e, o=rrow[0:1, :], m=mrow[0:1, :]: e.tensor_mul(o, m, m), inc="vln")
        v1 = p.emit("v", lambda e, r=rrow[0:1, :]: e.tensor_sub(vrow, vrow, r), inc="vln")
        s1_ = p.emit("s", lambda e, o=rrow[0:1, :]: e.activation(out=o, in_=vrow, func=AF.Sqrt),
                     wait=[("vln", v1)], inc="sln")
        v2 = p.emit("v", lambda e, r=rrow[0:1, :]: e.reciprocal(r, r), wait=[("sln", s1_)], inc="vln")
        ln0.stats_rd = p.tok("vln")
        b1 = bcast_row(mrow[0:1, :], mb[:, 0:T], T, pC, [("vln", v2)], "vln")
        b2 = bcast_row(rrow[0:1, :], mb[:, T:2 * T], T, pC, [("vln", b1)], "vln")
        last = None
        for ci in range(NCH):
            xv = XF[:, ci * T:(ci + 1) * T]
            xc = xn[:, ci * 2 * T + which * T: ci * 2 * T + (which + 1) * T]
            p.emit("v", lambda e, a=xv: e.tensor_sub(a, a, mb[:, 0:T]),
                   wait=[("vln", b2)] if ci == 0 else (), inc="vln")
            last = p.emit("v", lambda e, o=xc, a=xv: e.tensor_mul(o, a, mb[:, T:2 * T]), inc="vln")
        return last, ln0.stats_rd

    ln_own, stats_rd0 = ln0(x_own, 0, [], ST_mo, ST_ro)
    ln_part, _ = ln0(x_part, 1, [("vln", ln_own)], ST_mp, ST_rp)
    xn_ready = ln_part

    def dump_dbg(get_src, n_tiles, wait_sem, wait_tok, bounce=None):
        bounce = bounce if bounce is not None else sT
        for i in range(n_tiles):
            src = get_src(i)
            tk2 = p.emit("v", lambda e, o=bounce[:, 0:512], i_=src[:, 0:512]: e.tensor_copy(o, i_),
                         wait=[(wait_sem, wait_tok), ("st", p.tok("st"))], inc="vdbg")
            p.dma("y", dbg[:, i * T:i * T + 512], bounce[:, 0:512], inc="st", wait=[("vdbg", tk2)])
            tk3 = p.emit("v", lambda e, o=bounce[:, 0:512], i_=src[:, 512:1024]: e.tensor_copy(o, i_),
                         wait=[("st", p.tok("st"))], inc="vdbg")
            p.dma("y", dbg[:, i * T + 512:(i + 1) * T], bounce[:, 0:512], inc="st", wait=[("vdbg", tk3)])
        p.emit("y", None, wait=[("st", p.tok("st"))])

    if dbg_mode == "xn0":
        dump_dbg(lambda i: xn[:, (i // 2) * 2 * T + (i % 2) * T:(i // 2) * 2 * T + (i % 2 + 1) * T],
                 32, "vln", xn_ready)
        p.replay()
        return nc

    tkbv = p.dma("y", SRQ[0:1, :], bv0[0:1, 0:1024], inc="xc", wait=[("vln", xn_ready)])
    tkbv2 = p.dma("y", SRK[0:1, :], bv0[0:1, 1024:2048], inc="xc")
    b1 = bcast_row(SRQ[0:1, :], bvb[:, 0:1024], 1024, pC, [("xc", p.tok("xc")), ("vln", xn_ready)], "vln")
    vb_ready = bcast_row(SRK[0:1, :], bvb[:, 1024:2048], 1024, pC, [("vln", b1)], "vln")
    vg4 = vg_t.rearrange("p (t h d) -> p t h d", t=16, h=4)
    cst2 = p.emit("v", lambda e: e.memset(vg4[:, :, :, 128:129], 1.0), inc="vat")

    vgrp_evac = {}

    def emit_vgroup(g):
        wtoks = []
        for fc in range(NCH):
            dst = wvg[:, fc * 512:(fc + 1) * 512]
            w = [("mm", p.tok("mm"))] if fc == 0 and g >= 1 else ()
            wtoks.append(p.dma("g", dst, wv0[g, fc, :, :], inc="wv", wait=w))
        ev_last = None
        for tc in range(16):
            which, tloc = tc // 8, tc % 8
            for fc in range(NCH):
                lhs = xn[:, fc * 2 * T + which * T + tloc * 128: fc * 2 * T + which * T + (tloc + 1) * 128]
                rhs = wvg[:, fc * 512:(fc + 1) * 512]
                w = []
                if fc == 0:
                    w = [("wv", p.tok("wv")), ("vat", p.tok("vat"))]
                    if g == 0 and tc == 0:
                        w.append(("vln", vb_ready))
                p.emit("t", lambda e, o=pC[:, 0:512], l=lhs, r=rhs, st=(fc == 0), sp=(fc == NCH - 1):
                       e.matmul(o, l, r, start=st, stop=sp),
                       wait=w, inc="mm" if fc == NCH - 1 else None)
            mmt = p.tok("mm")
            vg_view = vg4[:, tc, :, 0:128]
            ps_view = pC.ap().rearrange("p (h d) -> p h d", h=8)[:, 0:4, :]
            bb_view = bvb.rearrange("p (h d) -> p h d", h=NCH)[:, 4 * g:4 * g + 4, :]
            ev_last = p.emit("v", lambda e, o=vg_view, i=ps_view, b=bb_view: e.tensor_add(o, i, b),
                             wait=[("mm", mmt)], inc="vat")
        vgrp_evac[g] = ev_last

    att_fin = None
    import os
    NHRUN = int(os.environ.get('NHRUN', str(NH0)))
    for h in range(1 if dbg_mode == 'qk0' else NHRUN):
        if h % 4 == 0:
            emit_vgroup(h // 4)
        hs = h % 2
        wsb = hs * NCH * 128

        wk_tok = p.dma("g", whead[:, wsb:wsb + NCH * 128], wk0[h, :, :], inc="wk",
                       wait=[("mm", p.tok("mm"))] if h >= 1 else ())
        bk_tok = p.dma("g", biasqk[:, hs * 2 + 1:hs * 2 + 2],
                       bk0[h:h + 1, :].rearrange("a b -> b a"), inc="wbk",
                       wait=[("vqk", p.tok("vqk"))] if h >= 1 else ())
        kevs = []
        for half2 in range(2):
            mmt = None
            for fc in range(NCH):
                lhs = whead[:, wsb + fc * 128:wsb + (fc + 1) * 128]
                for nb in range(2):
                    rhs = xn[:, fc * 2 * T + half2 * T + nb * 512: fc * 2 * T + half2 * T + nb * 512 + 512]
                    w = []
                    if fc == 0 and nb == 0:
                        w = [("wk", p.tok("wk")), ("vqk", p.tok("vqk")), ("vsc", p.tok("vsc"))]
                        if h == 0 and half2 == 0:
                            w.append(("vln", xn_ready))
                    mmt = p.emit("t", lambda e, o=pA[:, nb * 512:nb * 512 + 512], l=lhs, r=rhs,
                                 st=(fc == 0), sp=(fc == NCH - 1):
                                 e.matmul(o, l, r, start=st, stop=sp),
                                 wait=w, inc="mm" if fc == NCH - 1 else None)
            kdst = kh_t[:, hs * 2 * T + half2 * T: hs * 2 * T + (half2 + 1) * T]
            kev = p.emit("v", lambda e, o=kdst, b=biasqk[:, hs * 2 + 1:hs * 2 + 2]:
                         e.tensor_scalar(o, pA[:, :], b, SCH, ALU.add, ALU.mult),
                         wait=[("mm", mmt), ("wbk", p.tok("wbk"))], inc="vqk")
            kevs.append(kev)

        if h >= 1 and os.environ.get('STOPAT') == 'k':
            continue
        wq_tok = p.dma("g", whead[:, wsb:wsb + NCH * 128], wq0[h, :, :], inc="wq",
                       wait=[("mm", p.tok("mm"))])
        bq_tok = p.dma("g", biasqk[:, hs * 2:hs * 2 + 1],
                       bq0[h:h + 1, :].rearrange("a b -> b a"), inc="wbq",
                       wait=[("vqk", p.tok("vqk"))] if h >= 1 else ())
        mmq = None
        for fc in range(NCH):
            lhs = whead[:, wsb + fc * 128:wsb + (fc + 1) * 128]
            for nb in range(2):
                rhs = xn[:, fc * 2 * T + nb * 512: fc * 2 * T + nb * 512 + 512]
                w = []
                if fc == 0 and nb == 0:
                    w = [("wq", p.tok("wq")), ("vqk", p.tok("vqk")), ("vsc", p.tok("vsc")),
                         ("vst2", p.tok("vst2"))]
                mmq = p.emit("t", lambda e, o=pB[:, nb * 512:nb * 512 + 512], l=lhs, r=rhs,
                             st=(fc == 0), sp=(fc == NCH - 1):
                             e.matmul(o, l, r, start=st, stop=sp),
                             wait=w, inc="mm" if fc == NCH - 1 else None)
        qdst = qh_t[:, hs * T:(hs + 1) * T]
        vq = p.emit("v", lambda e, o=qdst, b=biasqk[:, hs * 2:hs * 2 + 1]:
                    e.tensor_scalar(o, pB[:, :], b, SCH, ALU.add, ALU.mult),
                    wait=[("mm", mmq), ("wbq", p.tok("wbq"))], inc="vqk")

        vq2 = p.emit("v", lambda e, q=qdst: e.tensor_mul(qsq[:, :], q, q),
                     wait=[("vqk", vq)], inc="vqk")
        for nb in range(2):
            mt = p.emit("t", lambda e, r=qsq[:, nb * 512:nb * 512 + 512]:
                        e.matmul(pD[0:1, 0:512], ones_cb[:, 0:1], r, start=True, stop=True),
                        wait=[("vqk", vq2), ("vst2", p.tok("vst2")), ("vpv", p.tok("vpv"))], inc="mm")
            p.emit("v", lambda e, o=SRQ[0:1, nb * 512:nb * 512 + 512]:
                   e.tensor_copy(o, pD[0:1, 0:512]), wait=[("mm", mt)], inc="vst2")
        ksrc = kh_t[:, hs * 2 * T:(hs + 1) * 2 * T]
        for kk in range(2):
            vk2 = p.emit("v", lambda e, i=ksrc[:, kk * T:(kk + 1) * T]: e.tensor_mul(ksq[:, :], i, i),
                         wait=[("vqk", kevs[kk]), ("vst2", p.tok("vst2"))], inc="vqk")
            for nb in range(2):
                mt = p.emit("t", lambda e, r=ksq[:, nb * 512:nb * 512 + 512]:
                            e.matmul(pD[0:1, 0:512], ones_cb[:, 0:1], r, start=True, stop=True),
                            wait=[("vqk", vk2), ("vst2", p.tok("vst2"))], inc="mm")
                p.emit("v", lambda e, o=SRK[0:1, nb * 512:nb * 512 + 512]:
                       e.tensor_copy(o, pD[0:1, 0:512]), wait=[("mm", mt)], inc="vst2")
            p.emit("v", lambda e, o=SRS[0:1, kk:kk + 1]: e.reduce_max(o, SRK[0:1, :], axis=AX.X),
                   inc="vst2")
        vkm = p.emit("v", lambda e: e.tensor_max(SRS[0:1, 2:3], SRS[0:1, 0:1], SRS[0:1, 1:2]),
                     inc="vst2")
        vm = p.emit("v", lambda e: e.tensor_scalar(SRM[0:1, :], SRQ[0:1, :],
                                                   SRS[0:1, 2:3], 0.5, ALU.add, ALU.mult),
                    wait=[("vst2", vkm)], inc="vst2")
        bmu = bcast_row(SRM[0:1, :], mub, T, pB,
                        [("vst2", vm), ("vqk", vq), ("vsc", p.tok("vsc"))], "vst2")

        hh = h % 4
        for qhf in range(2):
            qsl = qdst[:, qhf * 512:(qhf + 1) * 512]
            exp_last = None
            for kc in range(16):
                ps_ = [pA[:, 0:512], pA[:, 512:1024], pB[:, 0:512], pB[:, 512:1024]][kc % 4]
                lhs = ksrc[:, kc * 128:(kc + 1) * 128]
                w = [("vsc", p.tok("vsc"))]
                if kc == 0 and qhf == 0:
                    w.append(("vst2", bmu))
                if kc <= 3 and qhf == 0:
                    w.append(("vst2", bmu))
                mm_last = p.emit("t", lambda e, o=ps_, l=lhs, r=qsl:
                                 e.matmul(o, l, r, start=True, stop=True),
                                 wait=w, inc="mm")
                stv = sT[:, (kc % 2) * 512:((kc % 2) + 1) * 512]
                vsub = p.emit("v", lambda e, o=stv, i=ps_, m=mub[:, qhf * 512:(qhf + 1) * 512]:
                              e.tensor_sub(o, i, m),
                              wait=[("mm", mm_last), ("sex", p.tok("sex"))], inc="vsc")
                exp_last = p.emit("s", lambda e, o=probT[:, kc * 512:(kc + 1) * 512], i=stv:
                                  e.activation(out=o, in_=i, func=AF.Exp),
                                  wait=[("vsc", vsub)], inc="sex")
            for qc4 in range(4):
                qc = qhf * 4 + qc4
                mmt = None
                for kc in range(16):
                    lhs = probT[:, kc * 512 + qc4 * 128: kc * 512 + (qc4 + 1) * 128]
                    rhs = vg4[:, kc, hh, :]
                    w = []
                    if kc == 0:
                        w = [("sex", exp_last), ("vat", vgrp_evac[h // 4]), ("vpv", p.tok("vpv"))]
                    mmt = p.emit("t", lambda e, o=pD[:, 0:129], l=lhs, r=rhs, st=(kc == 0), sp=(kc == 15):
                                 e.matmul(o, l, r, start=st, stop=sp),
                                 wait=w, inc="mm" if kc == 15 else None)
                qs_ = qc % 2
                asl = atm[:, qs_ * 128:(qs_ + 1) * 128]
                vr = p.emit("v", lambda e, d=dnorm[:, qs_:qs_ + 1]: e.reciprocal(d, pD[:, 128:129]),
                            wait=[("mm", mmt)], inc="vpv")
                vn = p.emit("v", lambda e, o=asl, d=dnorm[:, qs_:qs_ + 1]:
                            e.tensor_scalar(o, pD[:, 0:128], d, None, ALU.mult),
                            wait=[("vpv", vr), ("sat", p.tok("sat"))], inc="vpv")
                trs = pT[:, (qc % 4) * 128:((qc % 4) + 1) * 128]
                w_tr = [("vpv", vn), ("sat", p.tok("sat"))]
                if h == 0 and qc == 0:
                    w_tr.append(("xc", 16))
                mtt = p.emit("t", lambda e, o=trs, i=asl: e.transpose(o, i, idn[:, :]),
                             wait=w_tr, inc="mm")
                att_fin = p.emit("v", lambda e, o=att0[:, h * T + qc * 128: h * T + (qc + 1) * 128],
                                 i=trs: e.tensor_copy(o, i), wait=[("mm", mtt)], inc="sat")

    if dbg_mode == "qk0":
        tiles = []
        tiles.append((qh_t[:, 0:T], "vqk"))
        tiles.append((kh_t[:, 0:T], "vqk"))
        tiles.append((kh_t[:, T:2 * T], "vqk"))
        tiles.append((mub[:, 0:T], "vst2"))
        vgh0 = vg4[:, :, 0, 0:128]
        fin_tok = {"vqk": p.tok("vqk"), "vst2": p.tok("vst2"), "sex": p.tok("sex"), "vat": p.tok("vat"),
                   "vpv": p.tok("vpv"), "sat": p.tok("sat")}
        idx = 0
        def put(src_ap, sem):
            nonlocal idx
            tk2 = p.emit("v", lambda e, o=sT[:, 0:512], i_=src_ap: e.tensor_copy(o, i_),
                         wait=[(sem, fin_tok[sem]), ("st", p.tok("st"))], inc="vdbg")
            p.dma("y", dbg[:, idx * 512:(idx + 1) * 512], sT[:, 0:512], inc="st", wait=[("vdbg", tk2)])
            idx += 1
        for (ap, sem) in tiles:
            put(ap[:, 0:512], sem)
            put(ap[:, 512:1024], sem)
        for tc2 in range(16):
            src_ap = vgh0[:, tc2, :]
            tk2 = p.emit("v", lambda e, o=sT[:, 0:128], i_=src_ap: e.tensor_copy(o, i_),
                         wait=[("vat", fin_tok["vat"]), ("st", p.tok("st"))], inc="vdbg")
            p.dma("y", dbg[:, idx * 512 + (tc2 % 4) * 128: idx * 512 + (tc2 % 4) * 128 + 128],
                  sT[:, 0:128], inc="st", wait=[("vdbg", tk2)])
            if tc2 % 4 == 3:
                idx += 1
        for kc in range(16):
            put(probT[:, kc * 512:(kc + 1) * 512], "sex")
        for i2 in range(2):
            put(att0[:, i2 * 512:(i2 + 1) * 512], "sat")
        mmt2 = None
        for fc in range(NCH):
            lhs = xn[:, fc * 2 * T + 0 * T + 0 * 128: fc * 2 * T + 0 * T + 128]
            mmt2 = p.emit("t", lambda e, l=lhs, r=wvg[:, fc * 512:(fc + 1) * 512],
                          st_=(fc == 0), sp=(fc == NCH - 1):
                          e.matmul(pD[:, 0:512], l, r, start=st_, stop=sp),
                          wait=[("st", p.tok("st")), ("vpv", p.tok("vpv"))] if fc == 0 else (),
                          inc="mm" if fc == NCH - 1 else None)
        tkr = p.emit("v", lambda e: e.tensor_copy(sT[:, 0:512], pD[:, 0:512]),
                     wait=[("mm", mmt2)], inc="vdbg")
        p.dma("y", dbg[:, idx * 512:(idx + 1) * 512], sT[:, 0:512], inc="st", wait=[("vdbg", tkr)])
        idx += 1
        p.emit("y", None, wait=[("st", p.tok("st"))])
        p.replay()
        return nc

    if dbg_mode == "attnraw":
        dump_dbg(lambda i: att0[:, i * T:(i + 1) * T], NCH, "sat", att_fin, bounce=sT)
        p.replay()
        return nc

    xacc_tks = []
    for ci in range(NCH):
        xacc_tks.append(p.dma("y", x_acc[:, ci * T:(ci + 1) * T],
                              x_own[128 * ci:128 * (ci + 1), :], inc="xc",
                              wait=[("sat", att_fin)]))
    bptk = p.dma("g", bpt[:, :], bp0[:, :].rearrange("c b -> b c"), inc="wpb",
                 wait=[("sat", att_fin)], slow=True)

    proj_last = None
    for oc in range(NCH):
        osl = (oc % 2) * NCH * 128
        wtk = p.dma("g", wpt[:, osl:osl + NCH * 128], wp0[oc, :, :], inc="wp",
                    wait=[("mm", p.tok("mm"))] if oc >= 2 else [("sat", att_fin)])
        mmt = None
        for ic in range(NCH):
            lhs = wpt[:, osl + ic * 128:osl + (ic + 1) * 128]
            for nb in range(2):
                rhs = att0[:, ic * T + nb * 512: ic * T + nb * 512 + 512]
                w = []
                if ic == 0 and nb == 0:
                    w = [("wp", p.tok("wp")), ("vac", p.tok("vac")), ("vsc", p.tok("vsc"))]
                    if oc == 0:
                        w.append(("sat", att_fin))
                mmt = p.emit("t", lambda e, o=pA[:, nb * 512:nb * 512 + 512], l=lhs, r=rhs,
                             st=(ic == 0), sp=(ic == NCH - 1):
                             e.matmul(o, l, r, start=st, stop=sp),
                             wait=w, inc="mm" if ic == NCH - 1 else None)
        xa = x_acc[:, oc * T:(oc + 1) * T]
        w = [("mm", mmt), ("xc", p.tok("xc"))]
        if oc == 0:
            w.append(("wpb", bptk))
        proj_last = p.emit("v", lambda e, o=xa, b=bpt[:, oc:oc + 1]:
                           e.scalar_tensor_tensor(o, pA[:, :], b, o, ALU.add, ALU.add),
                           wait=w, inc="vac")

    if dbg_mode == "attn0":
        dump_dbg(lambda i: x_acc[:, i * T:(i + 1) * T], NCH, "vac", proj_last, bounce=ffa)
        p.replay()
        return nc

    rrowf = rbc[0:1, 0:1024]
    for nb in range(2):
        mmt = None
        for ci in range(NCH):
            rhs = xn[:, ci * 2 * T + nb * 512: ci * 2 * T + nb * 512 + 512]
            w = []
            if ci == 0:
                w = [("vr", p.tok("vr")), ("vpv", p.tok("vpv")), ("vst2", p.tok("vst2"))]
            mmt = p.emit("t", lambda e, r=rhs, st=(ci == 0), sp=(ci == NCH - 1):
                         e.matmul(pD[0:1, 0:512], ones_cb[:, 0:1], r, start=st, stop=sp),
                         wait=w, inc="mm" if ci == NCH - 1 else None)
        p.emit("v", lambda e, o=rbc[0:1, nb * 512:nb * 512 + 512]:
               e.tensor_copy(o, pD[0:1, 0:512]), wait=[("mm", mmt)], inc="vr")
    v128 = p.emit("v", lambda e: e.tensor_scalar(rrowf, rrowf, 128.0, None, ALU.mult),
                  wait=[("vr", p.tok("vr"))], inc="vr")
    brt = bcast_row(rrowf, rbc, T, pB, [("vr", v128), ("vsc", p.tok("vsc"))], "vr")
    dst_tk = p.dma("g", fsc[:, 4:4 + NCH], dsc[:, :].rearrange("c b -> b c"), inc="wds",
                   wait=[("sat", att_fin)], slow=True)
    nds_tok = p.emit("v", lambda e: e.tensor_scalar(fsc[:, 4 + NCH:4 + 2 * NCH],
                                                    fsc[:, 4:4 + NCH], -128.0, None, ALU.mult),
                     wait=[("wds", dst_tk)], inc="vr")

    dv = None
    for grp in range(4):
        hlast = None
        for il in range(16):
            ic = grp * 16 + il
            wslot = ic % 2
            wgt_d = wgt[:, wslot * NCH * 128 // 2:wslot * NCH * 128 // 2 + NCH * 128]
            wut_d = wut[:, wslot * NCH * 128 // 2:wslot * NCH * 128 // 2 + NCH * 128]
            wgt_d = wgt[:, wslot * 2048:wslot * 2048 + 2048]
            wut_d = wut[:, wslot * 2048:wslot * 2048 + 2048]
            wgk = p.dma("g", wgt_d, wgd[ic, :, :], inc="wg",
                        wait=[("mm", p.tok("mm"))] if ic >= 2 else [("vac", p.tok("vac"))])
            wuk = p.dma("g", wut_d, wud[ic, :, :], inc="wu",
                        wait=[("mm", p.tok("mm"))] if ic >= 2 else [("vac", p.tok("vac"))])
            sctk = p.dma("g", fsc[:, wslot * 2:wslot * 2 + 1],
                         gsc[ic:ic + 1, :].rearrange("a b -> b a"), inc="wsg",
                         wait=[("vff", p.tok("vff"))])
            sctk2 = p.dma("g", fsc[:, wslot * 2 + 1:wslot * 2 + 2],
                          usc[ic:ic + 1, :].rearrange("a b -> b a"), inc="wsu",
                          wait=[("vff", p.tok("vff"))])
            mg = mu = None
            for fc in range(NCH):
                lg = wgt_d[:, fc * 128:(fc + 1) * 128]
                for nb in range(2):
                    rhs = xn[:, fc * 2 * T + nb * 512: fc * 2 * T + nb * 512 + 512]
                    w = []
                    if fc == 0 and nb == 0:
                        w = [("wg", p.tok("wg")), ("vff", p.tok("vff")), ("vdn", p.tok("vdn")),
                             ("vr", p.tok("vr"))]
                        if ic == 0:
                            w.append(("vr", brt))
                            w.append(("vac", p.tok("vac")))
                    mg = p.emit("t", lambda e, o=pA[:, nb * 512:nb * 512 + 512], l=lg, r=rhs,
                                st=(fc == 0), sp=(fc == NCH - 1):
                                e.matmul(o, l, r, start=st, stop=sp),
                                wait=w, inc="mm" if fc == NCH - 1 else None)
            for fc in range(NCH):
                lu = wut_d[:, fc * 128:(fc + 1) * 128]
                for nb in range(2):
                    rhs = xn[:, fc * 2 * T + nb * 512: fc * 2 * T + nb * 512 + 512]
                    w = [("wu", p.tok("wu"))] if (fc == 0 and nb == 0) else ()
                    mu = p.emit("t", lambda e, o=pB[:, nb * 512:nb * 512 + 512], l=lu, r=rhs,
                                st=(fc == 0), sp=(fc == NCH - 1):
                                e.matmul(o, l, r, start=st, stop=sp),
                                wait=w, inc="mm" if fc == NCH - 1 else None)
            ag = ffa[:, 0:T]
            au = ffa[:, T:2 * T]
            v1 = p.emit("v", lambda e, o=ag: e.tensor_sub(o, pA[:, :], rbc[:, :]),
                        wait=[("mm", mg)], inc="vff")
            v2 = p.emit("v", lambda e, o=au: e.tensor_sub(o, pB[:, :], rbc[:, :]),
                        wait=[("mm", mu)], inc="vff")
            s1_ = p.emit("s", lambda e, o=ffm[:, :], i=ag, sc=fsc[:, wslot * 2:wslot * 2 + 1]:
                         e.activation(out=o, in_=i, func=AF.Sigmoid, scale=sc),
                         wait=[("vff", v1), ("wsg", p.tok("wsg"))], inc="sff")
            p.emit("v", lambda e: e.tensor_mul(ffa[:, 0:T], ffa[:, 0:T], ffm[:, :]),
                   wait=[("sff", s1_)], inc="vff")
            v4 = p.emit("v", lambda e: e.tensor_mul(ffa[:, 0:T], ffa[:, 0:T], ffa[:, T:2 * T]),
                        inc="vff")
            v5 = p.emit("v", lambda e, a=wslot: e.tensor_mul(fsc[:, a * 2:a * 2 + 1],
                                                             fsc[:, a * 2:a * 2 + 1],
                                                             fsc[:, a * 2 + 1:a * 2 + 2]),
                        wait=[("wsu", p.tok("wsu")), ("sff", s1_)], inc="vff")
            hlast = p.emit("v", lambda e, o=hbuf[:, il * T:(il + 1) * T], a=wslot:
                           e.tensor_scalar(o, ffa[:, 0:T], fsc[:, a * 2:a * 2 + 1], None, ALU.mult),
                           wait=[("vff", v4), ("vff", v5)], inc="vff")
        for nb in range(2):
            mmt = None
            for il in range(16):
                rhs = hbuf[:, il * T + nb * 512: il * T + nb * 512 + 512]
                w = []
                if il == 0:
                    w = [("vff", hlast), ("vr", p.tok("vr"))]
                mmt = p.emit("t", lambda e, o=pB[0:1, nb * 512:nb * 512 + 512], r=rhs,
                             st=(il == 0), sp=(il == 15):
                             e.matmul(o, ones_cb[:, 0:1], r, start=st, stop=sp),
                             wait=w, inc="mm" if il == 15 else None)
            p.emit("v", lambda e, nb=nb: e.tensor_copy(SBC[0:1, nb * 512:nb * 512 + 512],
                                                       pB[0:1, nb * 512:nb * 512 + 512]),
                   wait=[("mm", mmt)], inc="vr")
        sbc_tok = bcast_row(SBC[0:1, :], SBC, T, pB, [("vr", p.tok("vr"))], "vr")
        for hc in range(NCH):
            wslot = hc % 2
            wdt_d = wdt[:, wslot * 2048:wslot * 2048 + 2048]
            wdk = p.dma("g", wdt_d, wdd[grp, hc, :, :], inc="wd",
                        wait=[("mm", p.tok("mm"))] if hc >= 2 else [("vac", p.tok("vac"))])
            ps = pC if hc % 2 == 0 else pA
            mmt = None
            for il in range(16):
                lhs = wdt_d[:, il * 128:(il + 1) * 128]
                for nb in range(2):
                    rhs = hbuf[:, il * T + nb * 512: il * T + nb * 512 + 512]
                    w = []
                    if il == 0 and nb == 0:
                        w = [("wd", p.tok("wd")), ("vff", hlast), ("vdn", p.tok("vdn"))]
                        if grp == 0 and hc <= 1:
                            w.append(("vat", p.tok("vat")))
                    mmt = p.emit("t", lambda e, o=ps[:, nb * 512:nb * 512 + 512], l=lhs, r=rhs,
                                 st=(il == 0), sp=(il == 15):
                                 e.matmul(o, l, r, start=st, stop=sp),
                                 wait=w, inc="mm" if il == 15 else None)
            xa = x_acc[:, hc * T:(hc + 1) * T]
            w = [("mm", mmt), ("vac", proj_last if grp == 0 else 0)]
            if grp == 0 and hc == 0:
                w.append(("wds", dst_tk))
            dv = p.emit("v", lambda e, o=xa, s=fsc[:, 4 + hc:4 + hc + 1], ps_=ps:
                        e.scalar_tensor_tensor(o, ps_[:, :], s, o, ALU.mult, ALU.add),
                        wait=w, inc="vdn")
            dv = p.emit("v", lambda e, o=xa, s=fsc[:, 4 + NCH + hc:4 + NCH + hc + 1]:
                        e.scalar_tensor_tensor(o, SBC[:, :], s, o, ALU.mult, ALU.add),
                        wait=[("vr", sbc_tok), ("vr", nds_tok)], inc="vdn")


    xn1 = BV(0, 16384)
    qx = BV(32768, 16384)
    QS0 = FV(131072, 1024)
    QS1 = FV(135168, 1024)
    QS2 = FV(139264, 1024)
    QS3 = FV(143360, 1024)
    KVT = BV(147456, 2048)
    CS1 = FV(155648, 2048)
    WPS = BV(163840, 4096)
    W1S = BV(172032, 4096)
    LR_m = FV(180224, 1024)
    LR_r = FV(184320, 1024)
    LR_v = FV(188416, 1024)
    MB1 = FV(192512, 2048)
    SCL = FV(200704, 16)
    RSB = FV(200768, 1024)
    protv = atm[:, 256:384]
    isb = atm[:, 384:385] if False else None

    tokP3 = [("vdn", p.tok("vdn")), ("vr", p.tok("vr")), ("vac", p.tok("vac"))]
    p.dma("y", protv, protd[:, :], inc="xc", wait=tokP3)
    p.dma("y", CS1[:, 0:T], cosd[:, :], inc="xc")
    p.dma("y", CS1[:, T:2 * T], sind[:, :], inc="xc")
    p.dma("y", fsc[:, 2:3], flagd[:, 0:1], inc="xc", slow=True)
    p.dma("y", fsc[:, 3:4], flagd[:, 1:2], inc="xc", slow=True)
    isld = p.dma("y", mb[0:1, 0:1].bitcast(BF16)[0:1, 0:1] if False else qh_t[0:1, 0:1], iscl[0:1, :], inc="xc")
    wsld = p.dma("y", SCL[0:1, 1:2], wscl[0:1, :], inc="xc")

    prev_mm = None
    for ci in range(NCH):
        xv = x_acc[:, ci * T:(ci + 1) * T]
        w0 = list(tokP3) if ci == 0 else ([("mm", prev_mm)] if prev_mm else [])
        vt = p.emit("v", lambda e, a=QS0[:, :], b=xv: e.tensor_mul(a, b, b), wait=w0, inc="vl1")
        for nh in range(2):
            s0, s1 = nh * 512, (nh + 1) * 512
            p.emit("t", lambda e, o=pA[0:1, s0:s1], r=xv[:, s0:s1], st=(ci == 0), sp=(ci == NCH - 1):
                   e.matmul(o, ones_c[:, 0:1], r, start=st, stop=sp),
                   wait=[("vl1", vt)] if nh == 0 else ())
        mt = None
        for nh in range(2):
            s0, s1 = nh * 512, (nh + 1) * 512
            mt = p.emit("t", lambda e, o=pB[0:1, s0:s1], r=QS0[:, s0:s1], st=(ci == 0), sp=(ci == NCH - 1):
                        e.matmul(o, ones_c[:, 0:1], r, start=st, stop=sp),
                        wait=[("vl1", vt)] if nh == 0 else (), inc="mm")
        prev_mm = mt
    p.emit("v", lambda e: e.tensor_scalar(LR_m[0:1, :], pA[0:1, :], 1.0 / D, None, ALU.mult),
           wait=[("mm", prev_mm)], inc="vl1")
    p.emit("v", lambda e: e.tensor_scalar(LR_v[0:1, :], pB[0:1, :], 1.0 / D, EPS, ALU.mult, ALU.add), inc="vl1")
    p.emit("v", lambda e: e.tensor_mul(LR_r[0:1, :], LR_m[0:1, :], LR_m[0:1, :]), inc="vl1")
    v1 = p.emit("v", lambda e: e.tensor_sub(LR_v[0:1, :], LR_v[0:1, :], LR_r[0:1, :]), inc="vl1")
    s1_ = p.emit("s", lambda e: e.activation(out=LR_r[0:1, :], in_=LR_v[0:1, :], func=AF.Sqrt),
                 wait=[("vl1", v1)], inc="sl1")
    v2 = p.emit("v", lambda e: e.reciprocal(LR_r[0:1, :], LR_r[0:1, :]), wait=[("sl1", s1_)], inc="vl1")
    vcv = p.emit("v", lambda e: e.tensor_copy(SCL[0:1, 0:1], qh_t[0:1, 0:1]),
                 wait=[("xc", p.tok("xc"))], inc="vl1")
    vinv = p.emit("v", lambda e: e.reciprocal(SCL[0:1, 2:3], SCL[0:1, 0:1]), wait=[("vl1", vcv)], inc="vl1")
    vsw = p.emit("v", lambda e: e.tensor_mul(SCL[0:1, 3:4], SCL[0:1, 0:1], SCL[0:1, 1:2]),
                 wait=[("vl1", vcv)], inc="vl1")
    vrsi = p.emit("v", lambda e: e.tensor_scalar(LR_v[0:1, 0:T], LR_r[0:1, :], SCL[0:1, 2:3], None, ALU.mult),
                  wait=[("vl1", v2), ("vl1", vinv)], inc="vl1")
    b1 = bcast_row(LR_m[0:1, :], MB1[:, 0:T], T, pC, [("vl1", v2)], "vl1")
    b2 = bcast_row(LR_r[0:1, :], MB1[:, T:2 * T], T, pC, [("vl1", b1)], "vl1")
    rsb_tok = bcast_row(LR_v[0:1, 0:T], RSB, T, pC, [("vl1", vrsi), ("vl1", b2)], "vl1")
    cv2 = p.dma("y", LR_m[0:1, 0:32], convd[0:1, :], inc="xc", wait=[("vl1", b2)])
    bcv = bcast_row(LR_m[0:1, 0:32], fsc[:, 4:36], 32, pD, [("xc", p.tok("xc"))], "vl1")
    msw = p.emit("t", lambda e: e.matmul(pD[:, 0:1], ones_r[0:1, :], SCL[0:1, 3:4], start=True, stop=True),
                 wait=[("vl1", vsw), ("vl1", bcv)], inc="mm")
    bsw = p.emit("v", lambda e: e.tensor_copy(fsc[:, 0:1], pD[:, 0:1]), wait=[("mm", msw)], inc="vl1")

    QC = float(2 ** 20 + 1)
    QG = float(3 * 2 ** 13)
    QTH = float((1.5 * 2.0 ** -6) ** 2)
    qx_done = xn1_done = None
    for ci in range(NCH):
        xv = x_acc[:, ci * T:(ci + 1) * T]
        p.emit("v", lambda e, a=xv: e.tensor_sub(QS0[:, :], a, MB1[:, 0:T]),
               wait=[("vl1", rsb_tok), ("sl1", p.tok("sl1"))] if ci == 0 else (), inc="vl1")
        xn1_done = p.emit("v", lambda e, o=xn1[:, ci * T:(ci + 1) * T]:
                          e.tensor_mul(o, QS0[:, :], MB1[:, T:2 * T]), inc="vl1")
        p.emit("v", lambda e: e.tensor_mul(QS0[:, :], QS0[:, :], RSB[:, :]), inc="vl1")
        p.emit("v", lambda e: e.tensor_scalar(QS0[:, :], QS0[:, :], -448.0, 448.0, ALU.max, ALU.min), inc="vl1")
        p.emit("v", lambda e: e.tensor_scalar(QS1[:, :], QS0[:, :], QG, QG, ALU.add, ALU.subtract), inc="vl1")
        st_ = p.emit("s", lambda e: e.mul(QS2[:, :], QS0[:, :], QC), wait=[("vl1", p.tok("vl1"))], inc="sl1")
        p.emit("v", lambda e: e.tensor_sub(QS3[:, :], QS2[:, :], QS0[:, :]), wait=[("sl1", st_)], inc="vl1")
        p.emit("v", lambda e: e.tensor_sub(QS2[:, :], QS2[:, :], QS3[:, :]), inc="vl1")
        p.emit("v", lambda e: e.tensor_mul(QS3[:, :], QS0[:, :], QS0[:, :]), inc="vl1")
        p.emit("v", lambda e: e.tensor_scalar(QS3[:, :], QS3[:, :], QTH, 1e30, ALU.subtract, ALU.mult), inc="vl1")
        p.emit("v", lambda e: e.tensor_scalar(QS3[:, :], QS3[:, :], 0.0, 1.0, ALU.max, ALU.min), inc="vl1")
        p.emit("v", lambda e: e.tensor_sub(QS2[:, :], QS2[:, :], QS1[:, :]), inc="vl1")
        p.emit("v", lambda e: e.tensor_mul(QS2[:, :], QS2[:, :], QS3[:, :]), inc="vl1")
        qx_done = p.emit("v", lambda e, o=qx[:, ci * T:(ci + 1) * T]:
                         e.tensor_add(o, QS2[:, :], QS1[:, :]), inc="vl1")

    pub_last = None
    for kv in range(8):
        wsl = (kv % 2) * NCH * 128
        wkt = p.dma("g", WPS[:, wsl:wsl + NCH * 128], wpk[kv, :, :], inc="wk1",
                    wait=[("mm", p.tok("mm"))] if kv >= 1 else [("vl1", qx_done)])
        mmt = None
        for fc in range(NCH):
            lhs = WPS[:, wsl + fc * 128:wsl + (fc + 1) * 128]
            for nb in range(2):
                rhs = xn1[:, fc * T + nb * 512: fc * T + nb * 512 + 512]
                w = []
                if fc == 0 and nb == 0:
                    w = [("wk1", p.tok("wk1")), ("vkv", p.tok("vkv"))]
                    if kv == 0:
                        w.append(("vl1", xn1_done))
                mmt = p.emit("t", lambda e, o=pA[:, nb * 512:nb * 512 + 512], l=lhs, r=rhs,
                             st=(fc == 0), sp=(fc == NCH - 1):
                             e.matmul(o, l, r, start=st, stop=sp),
                             wait=w, inc="mm" if fc == NCH - 1 else None)
        kev = p.emit("v", lambda e: e.tensor_copy(KVT[:, 0:T], pA[:, :]),
                     wait=[("mm", mmt), ("pub", p.tok("pub"))], inc="vkv")
        rt = None
        for nb in range(2):
            rt = p.emit("t", lambda e, o=pB[:, nb * 512:nb * 512 + 512],
                        r=KVT[:, nb * 512:nb * 512 + 512]:
                        e.matmul(o, protv, r, start=True, stop=True),
                        wait=[("vkv", kev), ("vrp", p.tok("vrp"))] if nb == 0 else (), inc="mm")
        r1 = p.emit("v", lambda e: e.tensor_mul(QS0[:, :], KVT[:, 0:T], CS1[:, 0:T]),
                    wait=[("vkv", kev)], inc="vrp")
        r2 = p.emit("v", lambda e: e.tensor_mul(QS1[:, :], pB[:, :], CS1[:, T:2 * T]),
                    wait=[("mm", rt)], inc="vrp")
        r3 = p.emit("v", lambda e: e.tensor_add(KVT[:, T:2 * T], QS0[:, :], QS1[:, :]),
                    wait=[("vrp", r2)], inc="vrp")
        pub_last = p.dma("y", kv_out[0, kv, :, :], KVT[:, T:2 * T], inc="pub", wait=[("vrp", r3)])
        wvt_ = p.dma("g", WPS[:, wsl:wsl + NCH * 128], wpv[kv, :, :], inc="wk1",
                     wait=[("mm", p.tok("mm"))])
        mmt = None
        for fc in range(NCH):
            lhs = WPS[:, wsl + fc * 128:wsl + (fc + 1) * 128]
            for nb in range(2):
                rhs = xn1[:, fc * T + nb * 512: fc * T + nb * 512 + 512]
                w = [("wk1", p.tok("wk1")), ("vkv", p.tok("vkv"))] if (fc == 0 and nb == 0) else ()
                mmt = p.emit("t", lambda e, o=pA[:, nb * 512:nb * 512 + 512], l=lhs, r=rhs,
                             st=(fc == 0), sp=(fc == NCH - 1):
                             e.matmul(o, l, r, start=st, stop=sp),
                             wait=w, inc="mm" if fc == NCH - 1 else None)
        vev = p.emit("v", lambda e: e.tensor_copy(KVT[:, 0:T], pA[:, :]),
                     wait=[("mm", mmt), ("pub", p.tok("pub"))], inc="vkv")
        pub_last = p.dma("y", kv_out[1, kv, :, :], KVT[:, 0:T], inc="pub", wait=[("vkv", vev)])

    cc_tok = p.emit("g", lambda e: e.collective_compute(
        "AllGather", ALU.bypass, replica_groups=[[0, 1], [2, 3], [4, 5], [6, 7]],
        ins=[kv_out.ap().opt()], outs=[kv_gth.ap().opt()]),
        wait=[("pub", p.tok("pub"))], inc="cc")

    ffn1_last = None
    for oc in range(NCH):
        wsl = (oc % 2) * NCH * 128
        wft = p.dma("g", W1S[:, wsl:wsl + NCH * 128], w1d[oc, :, :], inc="wf1",
                    wait=[("mm", p.tok("mm"))] if oc >= 2 else [("vl1", qx_done)])
        ps = pA if oc % 2 == 0 else pB
        mmt = None
        for ic in range(NCH):
            lhs = W1S[:, wsl + ic * 128:wsl + (ic + 1) * 128]
            for nb in range(2):
                rhs = qx[:, ic * T + nb * 512: ic * T + nb * 512 + 512]
                w = []
                if ic == 0 and nb == 0:
                    w = [("wf1", p.tok("wf1")), ("vf1", p.tok("vf1")), ("vkv", p.tok("vkv")),
                         ("vrp", p.tok("vrp"))]
                mmt = p.emit("t", lambda e, o=ps[:, nb * 512:nb * 512 + 512], l=lhs, r=rhs,
                             st=(ic == 0), sp=(ic == NCH - 1):
                             e.matmul(o, l, r, start=st, stop=sp),
                             wait=w, inc="mm" if ic == NCH - 1 else None)
        ffn1_last = p.emit("v", lambda e, o=x_acc[:, oc * T:(oc + 1) * T], ps_=ps:
                           e.scalar_tensor_tensor(o, ps_[:, :], fsc[:, 0:1], o, ALU.mult, ALU.add),
                           wait=[("mm", mmt), ("vl1", bsw)], inc="vf1")

    masks = BV(32768, 9216)
    CS2 = FV(51200, 2048)
    ST2 = BV(59392, 1024)
    SS2 = FV(61440, 1024)
    KB = BV(131072, 8192)
    VB = BV(147456, 8256)
    PT1 = BV(164352, 4608)
    WQS = BV(173568, 4096)
    QCUR = BV(181760, 2048)
    WOS = BV(185856, 4096)
    A1Q = BV(194048, 4096)
    KB4 = KB.rearrange("p (v t) -> p v t", v=4)
    VB4 = VB.rearrange("p (v b c) -> p v b c", v=4, b=16)

    mld = p.dma("y", masks[:, :], maskd[:, :], inc="xc",
                wait=[("vf1", p.tok("vf1"))])
    p.dma("y", CS2[:, 0:T], cosd[:, :], inc="xc")
    csl2 = p.dma("y", CS2[:, T:2 * T], sind[:, :], inc="xc")

    att1_fin = None
    for hf in range(2):
        rb0w = [("cc", cc_tok), ("xc", p.tok("xc")), ("vf1", p.tok("vf1"))]
        if hf == 1:
            rb0w.append(("mm", p.tok("mm")))
            rb0w.append(("vb2", p.tok("vb2")))
        for (reg, base_src) in ((0, 0), (1, 1)):
            dst = KB4[:, :, reg * T:(reg + 1) * T]
            d1 = p.dma("y", dst, kv_gth[base_src, 0, 4 * hf:4 * hf + 4, :, :].rearrange("v p t -> p v t"),
                       inc="xr", wait=rb0w if reg == 0 else ())
            b1_ = p.emit("v", lambda e, o=dst: e.tensor_scalar(o, o, fsc[:, 3:4], None, ALU.mult),
                         wait=[("xr", p.tok("xr"))], inc="vb2")
            for kvl in range(4):
                d2 = p.dma("y", ST2[:, :], kv_gth[1 - base_src, 0, 4 * hf + kvl, :, :], inc="xr",
                           wait=[("vb2", p.tok("vb2"))])
                b1_ = p.emit("v", lambda e, o=KB4[:, kvl, reg * T:(reg + 1) * T]:
                             e.scalar_tensor_tensor(o, ST2[:, :], fsc[:, 2:3], o, ALU.mult, ALU.add),
                             wait=[("xr", p.tok("xr"))], inc="vb2")
        for kvl in range(4):
            kvg = 4 * hf + kvl
            w0c = fsc[:, 4 + 2 * kvg:5 + 2 * kvg]
            w1c = fsc[:, 5 + 2 * kvg:6 + 2 * kvg]
            kown = KB4[:, kvl, 0:T]
            kpar = KB4[:, kvl, T:2 * T]
            bb = p.emit("v", lambda e, o=ST2[:, 0:1], i=KB4[:, kvl, 2 * T - 1:2 * T]:
                        e.tensor_scalar(o, i, fsc[:, 2:3], None, ALU.mult),
                        wait=[("vb2", p.tok("vb2"))], inc="vb2")
            p.emit("v", lambda e, o=ST2[:, 1:T], i=kown[:, 0:T - 1], w=w0c:
                   e.tensor_scalar(o, i, w, None, ALU.mult), inc="vb2")
            p.emit("v", lambda e, o=ST2[:, 0:1], w=w0c:
                   e.tensor_scalar(o, ST2[:, 0:1], w, None, ALU.mult), inc="vb2")
            p.emit("v", lambda e, o=kown, w=w1c:
                   e.scalar_tensor_tensor(o, kown, w, ST2[:, 0:T], ALU.mult, ALU.add), inc="vb2")
            p.emit("v", lambda e, o=ST2[:, 1:T], i=kpar[:, 0:T - 1], w=w0c:
                   e.tensor_scalar(o, i, w, None, ALU.mult), inc="vb2")
            p.emit("v", lambda e, o=ST2[:, 0:1]: e.memset(o, 0.0), inc="vb2")
            p.emit("v", lambda e, o=kpar, w=w1c:
                   e.scalar_tensor_tensor(o, kpar, w, ST2[:, 0:T], ALU.mult, ALU.add), inc="vb2")
        VFM = PT1[:, 0:2048]
        on1 = p.emit("v", lambda e: e.memset(VB4[:, :, :, 128:129], 1.0),
                     wait=[("vb2", p.tok("vb2"))] + ([("mm", p.tok("mm"))] if hf == 1 else []), inc="vb2")
        for kvl in range(4):
            kvg = 4 * hf + kvl
            w0c = fsc[:, 4 + 32 + 2 * kvg - 28:5 + 32 + 2 * kvg - 28] if False else fsc[:, 20 + 2 * kvg:21 + 2 * kvg]
            w1c = fsc[:, 21 + 2 * kvg:22 + 2 * kvg]
            dv1 = p.dma("y", VFM[:, 0:T], kv_gth[0, 1, kvg, :, :], inc="xr",
                        wait=[("mm", p.tok("mm")), ("vb2", p.tok("vb2"))] if kvl == 0 else [("vb2", p.tok("vb2"))])
            p.emit("v", lambda e, o=VFM[:, 0:T]: e.tensor_scalar(o, o, fsc[:, 3:4], None, ALU.mult),
                   wait=[("xr", p.tok("xr"))], inc="vb2")
            dv2 = p.dma("y", ST2[:, :], kv_gth[1, 1, kvg, :, :], inc="xr", wait=[("vb2", p.tok("vb2"))])
            p.emit("v", lambda e, o=VFM[:, 0:T]:
                   e.scalar_tensor_tensor(o, ST2[:, :], fsc[:, 2:3], o, ALU.mult, ALU.add),
                   wait=[("xr", p.tok("xr"))], inc="vb2")
            dv3 = p.dma("y", VFM[:, T:2 * T], kv_gth[1, 1, kvg, :, :], inc="xr", wait=[("vb2", p.tok("vb2"))])
            p.emit("v", lambda e, o=VFM[:, T:2 * T]: e.tensor_scalar(o, o, fsc[:, 3:4], None, ALU.mult),
                   wait=[("xr", p.tok("xr"))], inc="vb2")
            dv4 = p.dma("y", ST2[:, :], kv_gth[0, 1, kvg, :, :], inc="xr", wait=[("vb2", p.tok("vb2"))])
            p.emit("v", lambda e, o=VFM[:, T:2 * T]:
                   e.scalar_tensor_tensor(o, ST2[:, :], fsc[:, 2:3], o, ALU.mult, ALU.add),
                   wait=[("xr", p.tok("xr"))], inc="vb2")
            bb = p.emit("v", lambda e, o=ST2[:, 0:1], i=VFM[:, 2 * T - 1:2 * T]:
                        e.tensor_scalar(o, i, fsc[:, 2:3], None, ALU.mult), inc="vb2")
            p.emit("v", lambda e, o=ST2[:, 1:T], i=VFM[:, 0:T - 1], w=w0c:
                   e.tensor_scalar(o, i, w, None, ALU.mult), inc="vb2")
            p.emit("v", lambda e, o=ST2[:, 0:1], w=w0c:
                   e.tensor_scalar(o, ST2[:, 0:1], w, None, ALU.mult), inc="vb2")
            p.emit("v", lambda e, o=VFM[:, 0:T], w=w1c:
                   e.scalar_tensor_tensor(o, VFM[:, 0:T], w, ST2[:, 0:T], ALU.mult, ALU.add), inc="vb2")
            p.emit("v", lambda e, o=ST2[:, 1:T], i=VFM[:, T:2 * T - 1], w=w0c:
                   e.tensor_scalar(o, i, w, None, ALU.mult), inc="vb2")
            p.emit("v", lambda e, o=ST2[:, 0:1]: e.memset(o, 0.0), inc="vb2")
            vcv_ = p.emit("v", lambda e, o=VFM[:, T:2 * T], w=w1c:
                          e.scalar_tensor_tensor(o, VFM[:, T:2 * T], w, ST2[:, 0:T], ALU.mult, ALU.add),
                          inc="vb2")
            for blk in range(16):
                trs = pT[:, (blk % 4) * 128:((blk % 4) + 1) * 128]
                mtt = p.emit("t", lambda e, o=trs, i=VFM[:, blk * 128:(blk + 1) * 128]:
                             e.transpose(o, i, idn[:, :]),
                             wait=[("vb2", vcv_ if blk == 0 else 0), ("vb2", p.tok("vb2"))], inc="mm")
                p.emit("v", lambda e, o=VB4[:, kvl, blk, 0:128], i=trs: e.tensor_copy(o, i),
                       wait=[("mm", mtt)], inc="vb2")
        for qi in range(8):
            qh = 8 * hf + qi
            kvl = qi // 2
            qsl = (qi % 2) * T
            wqt = p.dma("g", WQS[:, (qi % 2) * NCH * 128:(qi % 2) * NCH * 128 + NCH * 128],
                        wpq[qh, :, :], inc="wq2",
                        wait=[("mm", p.tok("mm"))] if (hf or qi) else [("vf1", p.tok("vf1"))])
            mmq = None
            for fc in range(NCH):
                lhs = WQS[:, (qi % 2) * NCH * 128 + fc * 128:(qi % 2) * NCH * 128 + (fc + 1) * 128]
                for nb in range(2):
                    rhs = xn1[:, fc * T + nb * 512: fc * T + nb * 512 + 512]
                    w = [("wq2", p.tok("wq2")), ("vq2", p.tok("vq2"))] if (fc == 0 and nb == 0) else ()
                    mmq = p.emit("t", lambda e, o=pA[:, nb * 512:nb * 512 + 512], l=lhs, r=rhs,
                                 st=(fc == 0), sp=(fc == NCH - 1):
                                 e.matmul(o, l, r, start=st, stop=sp),
                                 wait=w, inc="mm" if fc == NCH - 1 else None)
            qev = p.emit("v", lambda e, o=QCUR[:, qsl:qsl + T]: e.tensor_copy(o, pA[:, :]),
                         wait=[("mm", mmq)], inc="vq2")
            rt = None
            for nb in range(2):
                rt = p.emit("t", lambda e, o=pB[:, nb * 512:nb * 512 + 512],
                            r=QCUR[:, qsl + nb * 512:qsl + nb * 512 + 512]:
                            e.matmul(o, protv, r, start=True, stop=True),
                            wait=[("vq2", qev), ("vq2", p.tok("vq2"))] if nb == 0 else (), inc="mm")
            p.emit("v", lambda e, o=ST2[:, :], i=QCUR[:, qsl:qsl + T]:
                   e.tensor_mul(o, i, CS2[:, 0:T]), wait=[("vq2", qev)], inc="vq2")
            p.emit("v", lambda e, o=QCUR[:, qsl:qsl + T]: e.tensor_mul(o, pB[:, :], CS2[:, T:2 * T]),
                   wait=[("mm", rt)], inc="vq2")
            qrope = p.emit("v", lambda e, o=QCUR[:, qsl:qsl + T]: e.tensor_add(o, o, ST2[:, :]),
                           inc="vq2")
            for qthf in range(2):
                exp_toks = []
                for q4 in range(4):
                    qt = qthf * 4 + q4
                    mml = {}
                    for l in range(9):
                        s = (qt + l + 8) % 16
                        lhs = KB4[:, kvl, s * 128:(s + 1) * 128]
                        o = pA[:, l * 128:(l + 1) * 128] if l < 8 else pB[:, 0:128]
                        w = []
                        if l == 0:
                            w = [("vq2", qrope), ("vsc2", p.tok("vsc2")), ("vb2", p.tok("vb2"))]
                        mml[l] = p.emit("t", lambda e, o=o, l_=lhs, r=QCUR[:, qsl + qt * 128:qsl + (qt + 1) * 128]:
                                        e.matmul(o, l_, r, start=True, stop=True),
                                        wait=w, inc="mm" if l in (7, 8) else None)
                    mw = masks[:, qt * 1152:qt * 1152 + 1024]
                    mn = masks[:, qt * 1152 + 1024:qt * 1152 + 1152]
                    vs1 = p.emit("v", lambda e, m=mw: e.tensor_add(SS2[:, :], pA[:, :], m),
                                 wait=[("mm", mml[7]), ("sx2", p.tok("sx2"))], inc="vsc2")
                    ex1 = p.emit("s", lambda e, o=PT1[:, q4 * 1152:q4 * 1152 + 1024]:
                                 e.activation(out=o, in_=SS2[:, :], func=AF.Exp),
                                 wait=[("vsc2", vs1)], inc="sx2")
                    vs2 = p.emit("v", lambda e, m=mn: e.tensor_add(SS2[:, 0:128], pB[:, 0:128], m),
                                 wait=[("mm", mml[8]), ("sx2", ex1)], inc="vsc2")
                    ex2 = p.emit("s", lambda e, o=PT1[:, q4 * 1152 + 1024:q4 * 1152 + 1152]:
                                 e.activation(out=o, in_=SS2[:, 0:128], func=AF.Exp),
                                 wait=[("vsc2", vs2)], inc="sx2")
                    exp_toks.append(ex2)
                for q4 in range(4):
                    qt = qthf * 4 + q4
                    mmt = None
                    for l in range(9):
                        s = (qt + l + 8) % 16
                        lhs = PT1[:, q4 * 1152 + l * 128:q4 * 1152 + (l + 1) * 128]
                        rhs = VB4[:, kvl, s, :]
                        w = []
                        if l == 0:
                            w = [("sx2", exp_toks[q4]), ("vpv2", p.tok("vpv2"))]
                        mmt = p.emit("t", lambda e, o=pD[:, 0:129], l_=lhs, r=rhs, st=(l == 0), sp=(l == 8):
                                     e.matmul(o, l_, r, start=st, stop=sp),
                                     wait=w, inc="mm" if l == 8 else None)
                    qs_ = qt % 2
                    vr = p.emit("v", lambda e, d=dnorm[:, qs_:qs_ + 1]: e.reciprocal(d, pD[:, 128:129]),
                                wait=[("mm", mmt)], inc="vpv2")
                    vn = p.emit("v", lambda e, o=atm[:, qs_ * 128:(qs_ + 1) * 128], d=dnorm[:, qs_:qs_ + 1]:
                                e.tensor_scalar(o, pD[:, 0:128], d, None, ALU.mult),
                                wait=[("vpv2", vr), ("va2", p.tok("va2"))], inc="vpv2")
                    trs = pT[:, 512 + (qt % 2) * 128:512 + ((qt % 2) + 1) * 128]
                    mtt = p.emit("t", lambda e, o=trs, i=atm[:, qs_ * 128:(qs_ + 1) * 128]:
                                 e.transpose(o, i, idn[:, :]),
                                 wait=[("vpv2", vn), ("va2", p.tok("va2"))], inc="mm")
                    att1_fin = p.emit("v", lambda e, o=A1Q[:, (qi % 4) * T + qt * 128:(qi % 4) * T + (qt + 1) * 128],
                                      i=trs: e.tensor_copy(o, i), wait=[("mm", mtt)], inc="va2")
            if qi % 4 == 3:
                qtr = qh // 4
                for oc in range(NCH):
                    osl = (oc % 2) * NCH * 128
                    wot = p.dma("g", WOS[:, osl:osl + 512], wod[oc, :, 4 * qtr * 128:(4 * qtr + 4) * 128],
                                inc="wo2", wait=[("mm", p.tok("mm"))] if oc >= 2 else [("va2", att1_fin)])
                    mmt = None
                    for icl in range(4):
                        lhs = WOS[:, osl + icl * 128:osl + (icl + 1) * 128]
                        for nb in range(2):
                            rhs = A1Q[:, icl * T + nb * 512: icl * T + nb * 512 + 512]
                            w = []
                            if icl == 0 and nb == 0:
                                w = [("wo2", p.tok("wo2")), ("vo2", p.tok("vo2")), ("va2", att1_fin)]
                            mmt = p.emit("t", lambda e, o=pC[:, nb * 512:nb * 512 + 512], l=lhs, r=rhs,
                                         st=(icl == 0), sp=(icl == 3):
                                         e.matmul(o, l, r, start=st, stop=sp),
                                         wait=w, inc="mm" if icl == 3 else None)
                    p.emit("v", lambda e, o=x_acc[:, oc * T:(oc + 1) * T]:
                           e.tensor_add(o, o, pC[:, :]),
                           wait=[("mm", mmt), ("vf1", p.tok("vf1"))], inc="vo2")

    for qtr in range(4):
        sl = x_acc[:, qtr * 4 * T:(qtr + 1) * 4 * T]
        p.dma("y", y[:, qtr * 4 * T:(qtr + 1) * 4 * T], sl, inc="st",
              wait=[("vo2", p.tok("vo2")), ("vdn", p.tok("vdn"))])
    p.emit("y", None, wait=[("st", p.tok("st"))])

    p.replay()
    return nc


def _tile_lhsT(w):
    K, M = w.shape
    kb, mb_ = K // 128, M // 128
    t = w.reshape(kb, 128, mb_, 128).transpose(2, 1, 0, 3)
    return np.ascontiguousarray(t.reshape(mb_, 128, kb * 128))


def prep_inputs(inp):
    x = np.asarray(inp["L0_attn_x"])
    Wqkv = np.asarray(inp["L0_attn_W_qkv"])
    bqkv = np.asarray(inp["L0_attn_b_qkv"])
    Wproj = np.asarray(inp["L0_attn_W_proj"])
    bproj = np.asarray(inp["L0_attn_b_proj"])
    gw = np.asarray(inp["L0_ffn_gate_w"])
    gs = np.asarray(inp["L0_ffn_gate_scale"])
    uw = np.asarray(inp["L0_ffn_up_w"])
    us = np.asarray(inp["L0_ffn_up_scale"])
    dw = np.asarray(inp["L0_ffn_down_w"])
    ds = np.asarray(inp["L0_ffn_down_scale"])

    wq0 = _tile_lhsT(Wqkv[:, 0:2048])
    wk0 = _tile_lhsT(Wqkv[:, 2048:4096])
    wv0 = np.ascontiguousarray(
        Wqkv[:, 4096:6144].reshape(NCH, 128, 4, 512).transpose(2, 0, 1, 3))
    bq0 = np.ascontiguousarray(bqkv[0:2048].reshape(16, 128))
    bk0 = np.ascontiguousarray(bqkv[2048:4096].reshape(16, 128))
    bv0 = np.ascontiguousarray(bqkv[4096:6144].reshape(1, 2048))
    wp0 = _tile_lhsT(Wproj)
    bp0 = np.ascontiguousarray(bproj.reshape(16, 128))
    wg_ = _tile_lhsT(gw.T)
    wu_ = _tile_lhsT(uw.T)
    wdT = dw.T
    wd_ = wdT.reshape(4, 16, 128, 16, 128)
    wd_ = np.ascontiguousarray(wd_.transpose(0, 3, 2, 1, 4).reshape(4, 16, 128, 16 * 128))
    gsc_ = np.ascontiguousarray(gs.reshape(64, 128))
    usc_ = np.ascontiguousarray(us.reshape(64, 128))
    dsc_ = np.ascontiguousarray(ds.reshape(16, 128))
    ident = np.eye(128, dtype=np.float32).astype(ml_dtypes.bfloat16)

    Wpack = np.asarray(inp["L1_attn_W_pack"])
    Wo = np.asarray(inp["L1_attn_W_o"])
    convk = np.asarray(inp["L1_attn_conv_k"]).astype(np.float32)
    convv = np.asarray(inp["L1_attn_conv_v"]).astype(np.float32)
    w1 = np.asarray(inp["L1_ffn_weight"])
    wscale = np.asarray(inp["L1_ffn_weight_scale"]).astype(np.float32)
    iscale = np.asarray(inp["L1_ffn_input_scale"])

    wpq_ = _tile_lhsT(Wpack[:, 0:2048])
    wpk_ = _tile_lhsT(Wpack[:, 2048:3072])
    wpv_ = _tile_lhsT(Wpack[:, 3072:4096])
    wod_ = _tile_lhsT(Wo)
    w1T = np.ascontiguousarray(w1.T)
    w1d_ = _tile_lhsT(w1T.view(np.uint8)).view(w1.dtype)
    protm = np.zeros((128, 128), np.float32)
    for i in range(64):
        protm[i, i + 64] = 1.0
        protm[i + 64, i] = -1.0
    protd_ = protm.astype(ml_dtypes.bfloat16)
    convd_ = np.concatenate([convk.reshape(-1), convv.reshape(-1)]).reshape(1, 32).astype(np.float32)
    iscl_ = np.asarray(iscale).reshape(1, 1)
    wscl_ = wscale.reshape(1, 1)
    SCHf = np.float32(SCH)
    inv_freq = (1.0 / 100000.0 ** (np.arange(0, 128, 2, dtype=np.float32) / 128.0)).astype(np.float32)

    shared = dict(wq0=wq0, wk0=wk0, wv0=wv0, bq0=bq0, bk0=bk0, bv0=bv0,
                  wp0=wp0, bp0=bp0, wg=wg_, wu=wu_, wd=wd_,
                  gsc=gsc_, usc=usc_, dsc=dsc_, ident=ident,
                  wpq=wpq_, wpk=wpk_, wpv=wpv_, wod=wod_, w1d=w1d_,
                  protd=protd_, convd=convd_, iscl=iscl_, wscl=wscl_)
    in_maps = []
    for c in range(8):
        b, pp = c // 2, c % 2
        xo = np.ascontiguousarray(x[b, 1024 * pp:1024 * (pp + 1), :].T)
        xp = np.ascontiguousarray(x[b, 1024 * (1 - pp):1024 * (2 - pp), :].T)
        pos = (1024 * pp + np.arange(1024, dtype=np.float32))
        fr = np.outer(pos, inv_freq)
        cosf = np.cos(fr).astype(np.float32) * SCHf
        sinf = np.sin(fr).astype(np.float32) * SCHf
        cosd_ = np.ascontiguousarray(np.concatenate([cosf.T, cosf.T], axis=0))
        sind_ = np.ascontiguousarray(np.concatenate([sinf.T, sinf.T], axis=0))
        mask = np.full((8, 9, 128, 128), -1e30, np.float32)
        for t in range(8):
            for l in range(9):
                s = (t + l + 8) % 16
                j0 = 1024 * pp + 128 * s if s < 8 else 1024 * (1 - pp) + 128 * (s - 8)
                i_ = 1024 * pp + 128 * t + np.arange(128)[None, :]
                j_ = j0 + np.arange(128)[:, None]
                valid = (j_ <= i_) & (i_ - j_ < 1024)
                mask[t, l][valid] = 0.0
        maskd_ = np.ascontiguousarray(mask.transpose(2, 0, 1, 3).reshape(128, 9216)).astype(ml_dtypes.bfloat16)
        flagd_ = np.full((128, 2), 0.0, np.float32)
        flagd_[:, 0] = float(pp)
        flagd_[:, 1] = 1.0 - float(pp)
        m = dict(shared)
        m["x_own"] = xo
        m["x_part"] = xp
        m["cosd"] = cosd_
        m["sind"] = sind_
        m["maskd"] = maskd_
        m["flagd"] = flagd_
        in_maps.append(m)
    return in_maps


def assemble_output(results):
    out = np.empty((4, 2048, 2048), dtype=np.float32)
    for c in range(8):
        b, pp = c // 2, c % 2
        sh = results[c]["y"].reshape(128, 16, 1024)
        out[b, 1024 * pp:1024 * (pp + 1), :] = sh.transpose(2, 1, 0).reshape(1024, 2048)
    return out


def kernel(**inputs):
    in_maps = prep_inputs(inputs)
    nc = build_nc(dbg_mode=None)
    res = run_bass_kernel_spmd(nc, in_maps, core_ids=list(range(8)))
    return assemble_output(res.results)


# version 02
# speedup vs baseline: 1.0839x; 1.0839x over previous
import numpy as np
import ml_dtypes
import concourse.bass as bass
import concourse.mybir as mybir
from concourse.bass_utils import run_bass_kernel_spmd

F32 = mybir.dt.float32
BF16 = mybir.dt.bfloat16
I32 = mybir.dt.int32
FP8 = mybir.dt.float8e4
AF = mybir.ActivationFunctionType
ALU = mybir.AluOpType
AX = mybir.AxisListType

D = 2048
T = 1024
NCH = 16
NH0 = 16
NIC = 64
SC = float(1.0 / np.sqrt(np.float32(128.0)))
SCH = float(np.sqrt(SC))
EPS = 1e-5

ENGS = ("t", "v", "s", "g", "y")
ARENA_BYTES = 210944


class Prog:
    def __init__(self, nc):
        self.nc = nc
        self.q = {e: [] for e in ENGS}
        self.sems = {}
        self.cnt = {}

    def sem(self, name):
        if name not in self.sems:
            self.sems[name] = self.nc.alloc_semaphore(name)
            self.cnt[name] = 0
        return self.sems[name]

    def emit(self, eng, fn=None, wait=(), inc=None, inc_by=1):
        tok = None
        if inc is not None:
            self.sem(inc)
            self.cnt[inc] += inc_by
            tok = self.cnt[inc]
        for (sn, _v) in wait:
            self.sem(sn)
        waits = tuple((sn, v) for (sn, v) in wait if v > 0)
        sems = self.sems

        def job(e, waits=waits, fn=fn, inc=inc, inc_by=inc_by):
            for (sn, val) in waits:
                e.wait_ge(sems[sn], val)
            if fn is not None:
                inst = fn(e)
                if inc is not None:
                    inst.then_inc(sems[inc], inc_by)
            elif inc is not None:
                e.engine_nop().then_inc(sems[inc], inc_by)
        self.q[eng].append(job)
        return tok

    def dma(self, eng, out, in_, inc, wait=(), slow=False):
        if slow:
            return self.emit(eng, lambda e, o=out, i=in_: e.dma_start(
                out=o, in_=i, allow_slow_non_contiguous=True), wait=wait, inc=inc, inc_by=16)
        return self.emit(eng, lambda e, o=out, i=in_: e.dma_start(out=o, in_=i),
                         wait=wait, inc=inc, inc_by=16)

    def tok(self, name):
        self.sem(name)
        return self.cnt[name]

    def replay(self):
        nc = self.nc
        with nc.Block() as block:
            @block.tensor
            def _(t):
                for job in self.q["t"]:
                    job(t)

            @block.vector
            def _(v):
                for job in self.q["v"]:
                    job(v)

            @block.scalar
            def _(s):
                for job in self.q["s"]:
                    job(s)

            @block.gpsimd
            def _(g):
                for job in self.q["g"]:
                    job(g)

            @block.sync
            def _(y):
                for job in self.q["y"]:
                    job(y)


def build_nc(dbg_mode=None):
    nc = bass.Bass("TRN2", target_bir_lowering=False, detect_race_conditions=False)
    p = Prog(nc)
    DT = nc.dram_tensor
    A = nc.alloc_sbuf_tensor
    P_ = nc.alloc_psum_tensor

    x_own = DT("x_own", [D, T], F32, kind="ExternalInput")
    x_part = DT("x_part", [D, T], F32, kind="ExternalInput")
    wq0 = DT("wq0", [NH0, 128, NCH * 128], F32, kind="ExternalInput")
    wk0 = DT("wk0", [NH0, 128, NCH * 128], F32, kind="ExternalInput")
    wv0 = DT("wv0", [4, NCH, 128, 512], F32, kind="ExternalInput")
    bq0 = DT("bq0", [NH0, 128], F32, kind="ExternalInput")
    bk0 = DT("bk0", [NH0, 128], F32, kind="ExternalInput")
    bv0 = DT("bv0", [1, D], F32, kind="ExternalInput")
    wp0 = DT("wp0", [NCH, 128, NCH * 128], F32, kind="ExternalInput")
    bp0 = DT("bp0", [NCH, 128], F32, kind="ExternalInput")
    wgd = DT("wg", [NIC, 128, NCH * 128], I32, kind="ExternalInput")
    wud = DT("wu", [NIC, 128, NCH * 128], I32, kind="ExternalInput")
    wdd = DT("wd", [4, NCH, 128, 16 * 128], I32, kind="ExternalInput")
    gsc = DT("gsc", [NIC, 128], F32, kind="ExternalInput")
    usc = DT("usc", [NIC, 128], F32, kind="ExternalInput")
    dsc = DT("dsc", [NCH, 128], F32, kind="ExternalInput")
    ident = DT("ident", [128, 128], BF16, kind="ExternalInput")
    wpq = DT("wpq", [16, 128, NCH * 128], F32, kind="ExternalInput")
    wpk = DT("wpk", [8, 128, NCH * 128], F32, kind="ExternalInput")
    wpv = DT("wpv", [8, 128, NCH * 128], F32, kind="ExternalInput")
    wod = DT("wod", [NCH, 128, NCH * 128], F32, kind="ExternalInput")
    w1d = DT("w1d", [NCH, 128, NCH * 128], FP8, kind="ExternalInput")
    cosd = DT("cosd", [128, T], F32, kind="ExternalInput")
    sind = DT("sind", [128, T], F32, kind="ExternalInput")
    protd = DT("protd", [128, 128], BF16, kind="ExternalInput")
    maskd = DT("maskd", [128, 8 * 9 * 128], BF16, kind="ExternalInput")
    convd = DT("convd", [1, 32], F32, kind="ExternalInput")
    flagd = DT("flagd", [128, 2], F32, kind="ExternalInput")
    iscl = DT("iscl", [1, 1], BF16, kind="ExternalInput")
    wscl = DT("wscl", [1, 1], F32, kind="ExternalInput")
    kv_out = DT("kv_out", [2, 8, 128, T], BF16)
    kv_gth = DT("kv_gth", [2, 2, 8, 128, T], BF16)

    y = DT("y", [128, NCH * T], F32, kind="ExternalOutput")
    dbg = DT("dbg", [128, 2 * NCH * T], F32, kind="ExternalOutput") if dbg_mode else None

    AR = A("AR", [128, ARENA_BYTES // 2], BF16)

    def BV(off, cols):
        assert off % 2 == 0 and off + 2 * cols <= ARENA_BYTES, (off, cols)
        return AR[:, off // 2: off // 2 + cols]

    def FV(off, cols):
        assert off % 4 == 0 and off + 4 * cols <= ARENA_BYTES, (off, cols)
        return AR[:, off // 2: off // 2 + 2 * cols].bitcast(F32)

    ones_c = A("ones_c", [128, 1], F32)
    ones_cb = A("ones_cb", [128, 1], BF16)
    ones_r = A("ones_r", [1, 128], F32)
    idn = A("idn", [128, 128], BF16)
    biasqk = A("biasqk", [128, 4], F32)
    atm = A("atm", [128, 3 * 128], BF16)
    dnorm = A("dnorm", [128, 2], F32)
    bpt = A("bpt", [128, NCH], F32)
    fsc = A("fsc", [128, 40], F32)

    xn = BV(0, 32768)
    x_acc = FV(65536, 16384)
    XF = FV(65536, 16384)
    probT = BV(65536, 8192)
    kh_t = BV(81920, 4096)
    qh_t = BV(90112, 2048)
    sT = FV(94208, 1024)
    SRQ = FV(98304, 1024)
    SRM = FV(102400, 1024)
    SRK = FV(106496, 1024)
    SRS = FV(110592, 8)
    mub = FV(110624, 1024)
    ksq = BV(114720, 1024)
    qsq = BV(116768, 1024)
    bvb = FV(118816, 2048)
    xw = FV(131072, 2048)
    sq2 = FV(139264, 2048)
    mb = FV(147456, 2048)
    ST_mo = FV(155648, 1024)
    ST_ro = FV(159744, 1024)
    ST_mp = FV(163840, 1024)
    ST_rp = FV(167936, 1024)
    ST_v = FV(172032, 1024)
    att0 = BV(131072, 16384)
    vg_t = BV(163840, 16 * 4 * 129)
    wvg = BV(180352, 8192)
    whead = BV(196736, 4096)
    wpt = BV(163840, 4096)
    hbuf = BV(131072, 16384)
    wgt = BV(163840, 4096)
    wut = BV(172032, 4096)
    wdt = BV(180224, 4096)
    rbc = FV(188416, 1024)
    ffa = FV(192512, 2048)
    ffm = FV(200704, 1024)
    SBC = FV(204800, 1024)
    SBCn = FV(208896, 512)

    pA = P_("pA", [128, 1024], F32)
    pB = P_("pB", [128, 1024], F32)
    pC = P_("pC", [128, 1024], F32)
    pD = P_("pD", [128, 512], F32)
    pT = P_("pT", [128, 1024], BF16)

    p.dma("y", idn[:, :], ident[:, :], inc="xc")
    p.emit("v", lambda e: e.memset(ones_c[:, :], 1.0), inc="vcst")
    p.emit("v", lambda e: e.memset(ones_cb[:, :], 1.0), inc="vcst")
    cst_tok = p.emit("v", lambda e: e.memset(ones_r[:, :], 1.0), inc="vcst")

    def bcast_row(row_ap, out_ap, n, psum, wait, incname):
        mts = None
        for s0 in range(0, n, 512):
            s1 = min(s0 + 512, n)
            mts = p.emit("t", lambda e, o=psum[:, s0:s1], r=row_ap[:, s0:s1]:
                         e.matmul(o, ones_r[0:1, :], r, start=True, stop=True),
                         wait=wait if s0 == 0 else (), inc="mm")
        return p.emit("v", lambda e, o=out_ap[:, 0:n], i=psum[:, 0:n]: e.tensor_copy(o, i),
                      wait=[("mm", mts)], inc=incname)

    def ln0(src_dram, which, start_wait, mrow, rrow):
        dts = []
        chunk_mm = {}
        for ci in range(NCH):
            sem = "xl%d" % (ci % 2)
            w = list(start_wait)
            if ci >= 2:
                w.append(("mm", 0))
            if which == 0 and ci == 0:
                w.append(("vcst", cst_tok))
            w = [x for x in w if x[1] != 0]
            if ci >= 2:
                w.append(("mm", chunk_mm[ci - 2]))
            dts.append(p.dma("y", XF[:, ci * T:(ci + 1) * T],
                             src_dram[128 * ci:128 * (ci + 1), :], inc=sem, wait=w))
            sl = ci % 2
            sv = sq2[:, sl * T:(sl + 1) * T]
            xv = XF[:, ci * T:(ci + 1) * T]
            w = [(sem, p.tok(sem))]
            vt = p.emit("v", lambda e, a=sv, b=xv: e.tensor_mul(a, b, b), wait=w, inc="vln")
            for nh in range(2):
                s0, s1 = nh * 512, (nh + 1) * 512
                w0 = []
                if ci == 0 and nh == 0 and which == 1:
                    w0 = [("vln", ln0.stats_rd)]
                p.emit("t", lambda e, o=pA[0:1, s0:s1], r=xv[:, s0:s1], st=(ci == 0), sp=(ci == NCH - 1):
                       e.matmul(o, ones_c[:, 0:1], r, start=st, stop=sp),
                       wait=w0 + [(sem, p.tok(sem))])
            mt = None
            for nh in range(2):
                s0, s1 = nh * 512, (nh + 1) * 512
                mt = p.emit("t", lambda e, o=pB[0:1, s0:s1], r=sv[:, s0:s1], st=(ci == 0), sp=(ci == NCH - 1):
                            e.matmul(o, ones_c[:, 0:1], r, start=st, stop=sp),
                            wait=[("vln", vt)] if nh == 0 else (), inc="mm")
            chunk_mm[ci] = mt
        mmtok = chunk_mm[NCH - 1]
        vrow = ST_v[0:1, :]
        p.emit("v", lambda e, o=mrow[0:1, :]: e.tensor_scalar(o, pA[0:1, :], 1.0 / D, None, ALU.mult),
               wait=[("mm", mmtok)], inc="vln")
        p.emit("v", lambda e: e.tensor_scalar(vrow, pB[0:1, :], 1.0 / D, EPS, ALU.mult, ALU.add), inc="vln")
        p.emit("v", lambda e, o=rrow[0:1, :], m=mrow[0:1, :]: e.tensor_mul(o, m, m), inc="vln")
        v1 = p.emit("v", lambda e, r=rrow[0:1, :]: e.tensor_sub(vrow, vrow, r), inc="vln")
        s1_ = p.emit("s", lambda e, o=rrow[0:1, :]: e.activation(out=o, in_=vrow, func=AF.Sqrt),
                     wait=[("vln", v1)], inc="sln")
        v2 = p.emit("v", lambda e, r=rrow[0:1, :]: e.reciprocal(r, r), wait=[("sln", s1_)], inc="vln")
        ln0.stats_rd = p.tok("vln")
        b1 = bcast_row(mrow[0:1, :], mb[:, 0:T], T, pC, [("vln", v2)], "vln")
        b2 = bcast_row(rrow[0:1, :], mb[:, T:2 * T], T, pC, [("vln", b1)], "vln")
        last = None
        for ci in range(NCH):
            xv = XF[:, ci * T:(ci + 1) * T]
            xc = xn[:, ci * 2 * T + which * T: ci * 2 * T + (which + 1) * T]
            p.emit("v", lambda e, a=xv: e.tensor_sub(a, a, mb[:, 0:T]),
                   wait=[("vln", b2)] if ci == 0 else (), inc="vln")
            last = p.emit("v", lambda e, o=xc, a=xv: e.tensor_mul(o, a, mb[:, T:2 * T]), inc="vln")
        return last, ln0.stats_rd

    ln_own, stats_rd0 = ln0(x_own, 0, [], ST_mo, ST_ro)
    ln_part, _ = ln0(x_part, 1, [("vln", ln_own)], ST_mp, ST_rp)
    xn_ready = ln_part

    def dump_dbg(get_src, n_tiles, wait_sem, wait_tok, bounce=None):
        bounce = bounce if bounce is not None else sT
        for i in range(n_tiles):
            src = get_src(i)
            tk2 = p.emit("v", lambda e, o=bounce[:, 0:512], i_=src[:, 0:512]: e.tensor_copy(o, i_),
                         wait=[(wait_sem, wait_tok), ("st", p.tok("st"))], inc="vdbg")
            p.dma("y", dbg[:, i * T:i * T + 512], bounce[:, 0:512], inc="st", wait=[("vdbg", tk2)])
            tk3 = p.emit("v", lambda e, o=bounce[:, 0:512], i_=src[:, 512:1024]: e.tensor_copy(o, i_),
                         wait=[("st", p.tok("st"))], inc="vdbg")
            p.dma("y", dbg[:, i * T + 512:(i + 1) * T], bounce[:, 0:512], inc="st", wait=[("vdbg", tk3)])
        p.emit("y", None, wait=[("st", p.tok("st"))])

    if dbg_mode == "xn0":
        dump_dbg(lambda i: xn[:, (i // 2) * 2 * T + (i % 2) * T:(i // 2) * 2 * T + (i % 2 + 1) * T],
                 32, "vln", xn_ready)
        p.replay()
        return nc

    tkbv = p.dma("y", SRQ[0:1, :], bv0[0:1, 0:1024], inc="xc", wait=[("vln", xn_ready)])
    tkbv2 = p.dma("y", SRK[0:1, :], bv0[0:1, 1024:2048], inc="xc")
    b1 = bcast_row(SRQ[0:1, :], bvb[:, 0:1024], 1024, pC, [("xc", p.tok("xc")), ("vln", xn_ready)], "vln")
    vb_ready = bcast_row(SRK[0:1, :], bvb[:, 1024:2048], 1024, pC, [("vln", b1)], "vln")
    vg4 = vg_t.rearrange("p (t h d) -> p t h d", t=16, h=4)
    cst2 = p.emit("v", lambda e: e.memset(vg4[:, :, :, 128:129], 1.0), inc="vat")

    vgrp_evac = {}
    vglast = {}

    def emit_vgroup(g):
        wtoks = []
        for fc in range(NCH):
            dst = wvg[:, fc * 512:(fc + 1) * 512]
            w = [("mm", vglast[g - 1])] if fc == 0 and g >= 1 else ()
            wtoks.append(p.dma("g", dst, wv0[g, fc, :, :], inc="wv", wait=w))
        ev_last = None
        for tc in range(16):
            which, tloc = tc // 8, tc % 8
            for fc in range(NCH):
                lhs = xn[:, fc * 2 * T + which * T + tloc * 128: fc * 2 * T + which * T + (tloc + 1) * 128]
                rhs = wvg[:, fc * 512:(fc + 1) * 512]
                w = []
                if fc == 0:
                    w = [("wv", p.tok("wv")), ("vat", p.tok("vat"))]
                    if g == 0 and tc == 0:
                        w.append(("vln", vb_ready))
                p.emit("t", lambda e, o=pC[:, 0:512], l=lhs, r=rhs, st=(fc == 0), sp=(fc == NCH - 1):
                       e.matmul(o, l, r, start=st, stop=sp),
                       wait=w, inc="mm" if fc == NCH - 1 else None)
            mmt = p.tok("mm")
            vg_view = vg4[:, tc, :, 0:128]
            ps_view = pC.ap().rearrange("p (h d) -> p h d", h=8)[:, 0:4, :]
            bb_view = bvb.rearrange("p (h d) -> p h d", h=NCH)[:, 4 * g:4 * g + 4, :]
            ev_last = p.emit("v", lambda e, o=vg_view, i=ps_view, b=bb_view: e.tensor_add(o, i, b),
                             wait=[("mm", mmt)], inc="vat")
            vglast[g] = mmt
        vgrp_evac[g] = ev_last

    att_fin = None
    qlast = {}
    klast = {}
    import os
    NHRUN = int(os.environ.get('NHRUN', str(NH0)))
    for h in range(1 if dbg_mode == 'qk0' else NHRUN):
        if h % 4 == 0:
            emit_vgroup(h // 4)
        hs = h % 2
        wsb = hs * NCH * 128

        wk_tok = p.dma("g", whead[:, wsb:wsb + NCH * 128], wk0[h, :, :], inc="wk",
                       wait=[("mm", qlast[h - 2])] if h >= 2 else ())
        bk_tok = p.dma("g", biasqk[:, hs * 2 + 1:hs * 2 + 2],
                       bk0[h:h + 1, :].rearrange("a b -> b a"), inc="wbk",
                       wait=[("vqk", p.tok("vqk"))] if h >= 1 else ())
        kevs = []
        for half2 in range(2):
            mmt = None
            for fc in range(NCH):
                lhs = whead[:, wsb + fc * 128:wsb + (fc + 1) * 128]
                for nb in range(2):
                    rhs = xn[:, fc * 2 * T + half2 * T + nb * 512: fc * 2 * T + half2 * T + nb * 512 + 512]
                    w = []
                    if fc == 0 and nb == 0:
                        w = [("wk", p.tok("wk")), ("vqk", p.tok("vqk")), ("vsc", p.tok("vsc"))]
                        if h == 0 and half2 == 0:
                            w.append(("vln", xn_ready))
                    mmt = p.emit("t", lambda e, o=pA[:, nb * 512:nb * 512 + 512], l=lhs, r=rhs,
                                 st=(fc == 0), sp=(fc == NCH - 1):
                                 e.matmul(o, l, r, start=st, stop=sp),
                                 wait=w, inc="mm" if fc == NCH - 1 else None)
            kdst = kh_t[:, hs * 2 * T + half2 * T: hs * 2 * T + (half2 + 1) * T]
            kev = p.emit("v", lambda e, o=kdst, b=biasqk[:, hs * 2 + 1:hs * 2 + 2]:
                         e.tensor_scalar(o, pA[:, :], b, SCH, ALU.add, ALU.mult),
                         wait=[("mm", mmt), ("wbk", p.tok("wbk"))], inc="vqk")
            kevs.append(kev)
            klast[h] = mmt

        if h >= 1 and os.environ.get('STOPAT') == 'k':
            continue
        wq_tok = p.dma("g", whead[:, wsb:wsb + NCH * 128], wq0[h, :, :], inc="wq",
                       wait=[("mm", klast[h])])
        bq_tok = p.dma("g", biasqk[:, hs * 2:hs * 2 + 1],
                       bq0[h:h + 1, :].rearrange("a b -> b a"), inc="wbq",
                       wait=[("vqk", p.tok("vqk"))] if h >= 1 else ())
        mmq = None
        for fc in range(NCH):
            lhs = whead[:, wsb + fc * 128:wsb + (fc + 1) * 128]
            for nb in range(2):
                rhs = xn[:, fc * 2 * T + nb * 512: fc * 2 * T + nb * 512 + 512]
                w = []
                if fc == 0 and nb == 0:
                    w = [("wq", p.tok("wq")), ("vqk", p.tok("vqk")), ("vsc", p.tok("vsc")),
                         ("vst2", p.tok("vst2"))]
                mmq = p.emit("t", lambda e, o=pB[:, nb * 512:nb * 512 + 512], l=lhs, r=rhs,
                             st=(fc == 0), sp=(fc == NCH - 1):
                             e.matmul(o, l, r, start=st, stop=sp),
                             wait=w, inc="mm" if fc == NCH - 1 else None)
        qdst = qh_t[:, hs * T:(hs + 1) * T]
        qlast[h] = mmq
        vq = p.emit("v", lambda e, o=qdst, b=biasqk[:, hs * 2:hs * 2 + 1]:
                    e.tensor_scalar(o, pB[:, :], b, SCH, ALU.add, ALU.mult),
                    wait=[("mm", mmq), ("wbq", p.tok("wbq"))], inc="vqk")

        vq2 = p.emit("v", lambda e, q=qdst: e.tensor_mul(qsq[:, :], q, q),
                     wait=[("vqk", vq)], inc="vqk")
        for nb in range(2):
            mt = p.emit("t", lambda e, r=qsq[:, nb * 512:nb * 512 + 512]:
                        e.matmul(pD[0:1, 0:512], ones_cb[:, 0:1], r, start=True, stop=True),
                        wait=[("vqk", vq2), ("vst2", p.tok("vst2")), ("vpv", p.tok("vpv"))], inc="mm")
            p.emit("v", lambda e, o=SRQ[0:1, nb * 512:nb * 512 + 512]:
                   e.tensor_copy(o, pD[0:1, 0:512]), wait=[("mm", mt)], inc="vst2")
        ksrc = kh_t[:, hs * 2 * T:(hs + 1) * 2 * T]
        for kk in range(2):
            vk2 = p.emit("v", lambda e, i=ksrc[:, kk * T:(kk + 1) * T]: e.tensor_mul(ksq[:, :], i, i),
                         wait=[("vqk", kevs[kk]), ("vst2", p.tok("vst2"))], inc="vqk")
            for nb in range(2):
                mt = p.emit("t", lambda e, r=ksq[:, nb * 512:nb * 512 + 512]:
                            e.matmul(pD[0:1, 0:512], ones_cb[:, 0:1], r, start=True, stop=True),
                            wait=[("vqk", vk2), ("vst2", p.tok("vst2"))], inc="mm")
                p.emit("v", lambda e, o=SRK[0:1, nb * 512:nb * 512 + 512]:
                       e.tensor_copy(o, pD[0:1, 0:512]), wait=[("mm", mt)], inc="vst2")
            p.emit("v", lambda e, o=SRS[0:1, kk:kk + 1]: e.reduce_max(o, SRK[0:1, :], axis=AX.X),
                   inc="vst2")
        vkm = p.emit("v", lambda e: e.tensor_max(SRS[0:1, 2:3], SRS[0:1, 0:1], SRS[0:1, 1:2]),
                     inc="vst2")
        vm = p.emit("v", lambda e: e.tensor_scalar(SRM[0:1, :], SRQ[0:1, :],
                                                   SRS[0:1, 2:3], 0.5, ALU.add, ALU.mult),
                    wait=[("vst2", vkm)], inc="vst2")
        bmu = bcast_row(SRM[0:1, :], mub, T, pB,
                        [("vst2", vm), ("vqk", vq), ("vsc", p.tok("vsc"))], "vst2")

        hh = h % 4
        for qhf in range(2):
            qsl = qdst[:, qhf * 512:(qhf + 1) * 512]
            exp_last = None
            for kc in range(16):
                ps_ = [pA[:, 0:512], pA[:, 512:1024], pB[:, 0:512], pB[:, 512:1024]][kc % 4]
                lhs = ksrc[:, kc * 128:(kc + 1) * 128]
                w = [("vsc", p.tok("vsc"))]
                if kc == 0 and qhf == 0:
                    w.append(("vst2", bmu))
                if kc <= 3 and qhf == 0:
                    w.append(("vst2", bmu))
                mm_last = p.emit("t", lambda e, o=ps_, l=lhs, r=qsl:
                                 e.matmul(o, l, r, start=True, stop=True),
                                 wait=w, inc="mm")
                stv = sT[:, (kc % 2) * 512:((kc % 2) + 1) * 512]
                vsub = p.emit("v", lambda e, o=stv, i=ps_, m=mub[:, qhf * 512:(qhf + 1) * 512]:
                              e.tensor_sub(o, i, m),
                              wait=[("mm", mm_last), ("sex", p.tok("sex"))], inc="vsc")
                exp_last = p.emit("s", lambda e, o=probT[:, kc * 512:(kc + 1) * 512], i=stv:
                                  e.activation(out=o, in_=i, func=AF.Exp),
                                  wait=[("vsc", vsub)], inc="sex")
            for qc4 in range(4):
                qc = qhf * 4 + qc4
                mmt = None
                for kc in range(16):
                    lhs = probT[:, kc * 512 + qc4 * 128: kc * 512 + (qc4 + 1) * 128]
                    rhs = vg4[:, kc, hh, :]
                    w = []
                    if kc == 0:
                        w = [("sex", exp_last), ("vat", vgrp_evac[h // 4]), ("vpv", p.tok("vpv"))]
                    mmt = p.emit("t", lambda e, o=pD[:, 0:129], l=lhs, r=rhs, st=(kc == 0), sp=(kc == 15):
                                 e.matmul(o, l, r, start=st, stop=sp),
                                 wait=w, inc="mm" if kc == 15 else None)
                qs_ = qc % 2
                asl = atm[:, qs_ * 128:(qs_ + 1) * 128]
                vr = p.emit("v", lambda e, d=dnorm[:, qs_:qs_ + 1]: e.reciprocal(d, pD[:, 128:129]),
                            wait=[("mm", mmt)], inc="vpv")
                vn = p.emit("v", lambda e, o=asl, d=dnorm[:, qs_:qs_ + 1]:
                            e.tensor_scalar(o, pD[:, 0:128], d, None, ALU.mult),
                            wait=[("vpv", vr), ("sat", p.tok("sat"))], inc="vpv")
                trs = pT[:, (qc % 4) * 128:((qc % 4) + 1) * 128]
                w_tr = [("vpv", vn), ("sat", p.tok("sat"))]
                if h == 0 and qc == 0:
                    w_tr.append(("xc", 16))
                mtt = p.emit("t", lambda e, o=trs, i=asl: e.transpose(o, i, idn[:, :]),
                             wait=w_tr, inc="mm")
                att_fin = p.emit("v", lambda e, o=att0[:, h * T + qc * 128: h * T + (qc + 1) * 128],
                                 i=trs: e.tensor_copy(o, i), wait=[("mm", mtt)], inc="sat")

    if dbg_mode == "qk0":
        tiles = []
        tiles.append((qh_t[:, 0:T], "vqk"))
        tiles.append((kh_t[:, 0:T], "vqk"))
        tiles.append((kh_t[:, T:2 * T], "vqk"))
        tiles.append((mub[:, 0:T], "vst2"))
        vgh0 = vg4[:, :, 0, 0:128]
        fin_tok = {"vqk": p.tok("vqk"), "vst2": p.tok("vst2"), "sex": p.tok("sex"), "vat": p.tok("vat"),
                   "vpv": p.tok("vpv"), "sat": p.tok("sat")}
        idx = 0
        def put(src_ap, sem):
            nonlocal idx
            tk2 = p.emit("v", lambda e, o=sT[:, 0:512], i_=src_ap: e.tensor_copy(o, i_),
                         wait=[(sem, fin_tok[sem]), ("st", p.tok("st"))], inc="vdbg")
            p.dma("y", dbg[:, idx * 512:(idx + 1) * 512], sT[:, 0:512], inc="st", wait=[("vdbg", tk2)])
            idx += 1
        for (ap, sem) in tiles:
            put(ap[:, 0:512], sem)
            put(ap[:, 512:1024], sem)
        for tc2 in range(16):
            src_ap = vgh0[:, tc2, :]
            tk2 = p.emit("v", lambda e, o=sT[:, 0:128], i_=src_ap: e.tensor_copy(o, i_),
                         wait=[("vat", fin_tok["vat"]), ("st", p.tok("st"))], inc="vdbg")
            p.dma("y", dbg[:, idx * 512 + (tc2 % 4) * 128: idx * 512 + (tc2 % 4) * 128 + 128],
                  sT[:, 0:128], inc="st", wait=[("vdbg", tk2)])
            if tc2 % 4 == 3:
                idx += 1
        for kc in range(16):
            put(probT[:, kc * 512:(kc + 1) * 512], "sex")
        for i2 in range(2):
            put(att0[:, i2 * 512:(i2 + 1) * 512], "sat")
        mmt2 = None
        for fc in range(NCH):
            lhs = xn[:, fc * 2 * T + 0 * T + 0 * 128: fc * 2 * T + 0 * T + 128]
            mmt2 = p.emit("t", lambda e, l=lhs, r=wvg[:, fc * 512:(fc + 1) * 512],
                          st_=(fc == 0), sp=(fc == NCH - 1):
                          e.matmul(pD[:, 0:512], l, r, start=st_, stop=sp),
                          wait=[("st", p.tok("st")), ("vpv", p.tok("vpv"))] if fc == 0 else (),
                          inc="mm" if fc == NCH - 1 else None)
        tkr = p.emit("v", lambda e: e.tensor_copy(sT[:, 0:512], pD[:, 0:512]),
                     wait=[("mm", mmt2)], inc="vdbg")
        p.dma("y", dbg[:, idx * 512:(idx + 1) * 512], sT[:, 0:512], inc="st", wait=[("vdbg", tkr)])
        idx += 1
        p.emit("y", None, wait=[("st", p.tok("st"))])
        p.replay()
        return nc

    if dbg_mode == "attnraw":
        dump_dbg(lambda i: att0[:, i * T:(i + 1) * T], NCH, "sat", att_fin, bounce=sT)
        p.replay()
        return nc

    xacc_tks = []
    for ci in range(NCH):
        xacc_tks.append(p.dma("y", x_acc[:, ci * T:(ci + 1) * T],
                              x_own[128 * ci:128 * (ci + 1), :], inc="xc",
                              wait=[("sat", att_fin)]))
    bptk = p.dma("g", bpt[:, :], bp0[:, :].rearrange("c b -> b c"), inc="wpb",
                 wait=[("sat", att_fin)], slow=True)

    proj_last = None
    plast = {}
    for oc in range(NCH):
        osl = (oc % 2) * NCH * 128
        wtk = p.dma("g", wpt[:, osl:osl + NCH * 128], wp0[oc, :, :], inc="wp",
                    wait=[("mm", plast[oc - 2])] if oc >= 2 else [("sat", att_fin)])
        mmt = None
        for ic in range(NCH):
            lhs = wpt[:, osl + ic * 128:osl + (ic + 1) * 128]
            for nb in range(2):
                rhs = att0[:, ic * T + nb * 512: ic * T + nb * 512 + 512]
                w = []
                if ic == 0 and nb == 0:
                    w = [("wp", p.tok("wp")), ("vac", p.tok("vac")), ("vsc", p.tok("vsc"))]
                    if oc == 0:
                        w.append(("sat", att_fin))
                mmt = p.emit("t", lambda e, o=pA[:, nb * 512:nb * 512 + 512], l=lhs, r=rhs,
                             st=(ic == 0), sp=(ic == NCH - 1):
                             e.matmul(o, l, r, start=st, stop=sp),
                             wait=w, inc="mm" if ic == NCH - 1 else None)
        plast[oc] = mmt
        xa = x_acc[:, oc * T:(oc + 1) * T]
        w = [("mm", mmt), ("xc", p.tok("xc"))]
        if oc == 0:
            w.append(("wpb", bptk))
        proj_last = p.emit("v", lambda e, o=xa, b=bpt[:, oc:oc + 1]:
                           e.scalar_tensor_tensor(o, pA[:, :], b, o, ALU.add, ALU.add),
                           wait=w, inc="vac")

    if dbg_mode == "attn0":
        dump_dbg(lambda i: x_acc[:, i * T:(i + 1) * T], NCH, "vac", proj_last, bounce=ffa)
        p.replay()
        return nc

    rrowf = rbc[0:1, 0:1024]
    for nb in range(2):
        mmt = None
        for ci in range(NCH):
            rhs = xn[:, ci * 2 * T + nb * 512: ci * 2 * T + nb * 512 + 512]
            w = []
            if ci == 0:
                w = [("vr", p.tok("vr")), ("vpv", p.tok("vpv")), ("vst2", p.tok("vst2"))]
            mmt = p.emit("t", lambda e, r=rhs, st=(ci == 0), sp=(ci == NCH - 1):
                         e.matmul(pD[0:1, 0:512], ones_cb[:, 0:1], r, start=st, stop=sp),
                         wait=w, inc="mm" if ci == NCH - 1 else None)
        p.emit("v", lambda e, o=rbc[0:1, nb * 512:nb * 512 + 512]:
               e.tensor_copy(o, pD[0:1, 0:512]), wait=[("mm", mmt)], inc="vr")
    v128 = p.emit("v", lambda e: e.tensor_scalar(rrowf, rrowf, 128.0, None, ALU.mult),
                  wait=[("vr", p.tok("vr"))], inc="vr")
    brt = bcast_row(rrowf, rbc, T, pB, [("vr", v128), ("vsc", p.tok("vsc"))], "vr")
    dst_tk = p.dma("g", fsc[:, 4:4 + NCH], dsc[:, :].rearrange("c b -> b c"), inc="wds",
                   wait=[("sat", att_fin)], slow=True)
    nds_tok = p.emit("v", lambda e: e.tensor_scalar(fsc[:, 4 + NCH:4 + 2 * NCH],
                                                    fsc[:, 4:4 + NCH], -128.0, None, ALU.mult),
                     wait=[("wds", dst_tk)], inc="vr")

    dv = None
    glastd = {}
    ulastd = {}
    dlastd = {}
    for grp in range(4):
        hlast = None
        for il in range(16):
            ic = grp * 16 + il
            wslot = ic % 2
            wgt_d = wgt[:, wslot * NCH * 128 // 2:wslot * NCH * 128 // 2 + NCH * 128]
            wut_d = wut[:, wslot * NCH * 128 // 2:wslot * NCH * 128 // 2 + NCH * 128]
            wgt_d = wgt[:, wslot * 2048:wslot * 2048 + 2048]
            wut_d = wut[:, wslot * 2048:wslot * 2048 + 2048]
            wgk = p.dma("g", wgt_d, wgd[ic, :, :], inc="wg",
                        wait=[("mm", glastd[ic - 2])] if ic >= 2 else [("vac", p.tok("vac"))])
            wuk = p.dma("g", wut_d, wud[ic, :, :], inc="wu",
                        wait=[("mm", ulastd[ic - 2])] if ic >= 2 else [("vac", p.tok("vac"))])
            sctk = p.dma("g", fsc[:, wslot * 2:wslot * 2 + 1],
                         gsc[ic:ic + 1, :].rearrange("a b -> b a"), inc="wsg",
                         wait=[("vff", p.tok("vff"))])
            sctk2 = p.dma("g", fsc[:, wslot * 2 + 1:wslot * 2 + 2],
                          usc[ic:ic + 1, :].rearrange("a b -> b a"), inc="wsu",
                          wait=[("vff", p.tok("vff"))])
            mg = mu = None
            for fc in range(NCH):
                lg = wgt_d[:, fc * 128:(fc + 1) * 128]
                for nb in range(2):
                    rhs = xn[:, fc * 2 * T + nb * 512: fc * 2 * T + nb * 512 + 512]
                    w = []
                    if fc == 0 and nb == 0:
                        w = [("wg", p.tok("wg")), ("vff", p.tok("vff")), ("vdn", p.tok("vdn")),
                             ("vr", p.tok("vr"))]
                        if ic == 0:
                            w.append(("vr", brt))
                            w.append(("vac", p.tok("vac")))
                    mg = p.emit("t", lambda e, o=pA[:, nb * 512:nb * 512 + 512], l=lg, r=rhs,
                                st=(fc == 0), sp=(fc == NCH - 1):
                                e.matmul(o, l, r, start=st, stop=sp),
                                wait=w, inc="mm" if fc == NCH - 1 else None)
            for fc in range(NCH):
                lu = wut_d[:, fc * 128:(fc + 1) * 128]
                for nb in range(2):
                    rhs = xn[:, fc * 2 * T + nb * 512: fc * 2 * T + nb * 512 + 512]
                    w = [("wu", p.tok("wu"))] if (fc == 0 and nb == 0) else ()
                    mu = p.emit("t", lambda e, o=pB[:, nb * 512:nb * 512 + 512], l=lu, r=rhs,
                                st=(fc == 0), sp=(fc == NCH - 1):
                                e.matmul(o, l, r, start=st, stop=sp),
                                wait=w, inc="mm" if fc == NCH - 1 else None)
            glastd[ic] = mg
            ulastd[ic] = mu
            ag = ffa[:, 0:T]
            au = ffa[:, T:2 * T]
            v1 = p.emit("v", lambda e, o=ag: e.tensor_sub(o, pA[:, :], rbc[:, :]),
                        wait=[("mm", mg)], inc="vff")
            v2 = p.emit("v", lambda e, o=au: e.tensor_sub(o, pB[:, :], rbc[:, :]),
                        wait=[("mm", mu)], inc="vff")
            s1_ = p.emit("s", lambda e, o=ffm[:, :], i=ag, sc=fsc[:, wslot * 2:wslot * 2 + 1]:
                         e.activation(out=o, in_=i, func=AF.Sigmoid, scale=sc),
                         wait=[("vff", v1), ("wsg", p.tok("wsg"))], inc="sff")
            p.emit("v", lambda e: e.tensor_mul(ffa[:, 0:T], ffa[:, 0:T], ffm[:, :]),
                   wait=[("sff", s1_)], inc="vff")
            v4 = p.emit("v", lambda e: e.tensor_mul(ffa[:, 0:T], ffa[:, 0:T], ffa[:, T:2 * T]),
                        inc="vff")
            v5 = p.emit("v", lambda e, a=wslot: e.tensor_mul(fsc[:, a * 2:a * 2 + 1],
                                                             fsc[:, a * 2:a * 2 + 1],
                                                             fsc[:, a * 2 + 1:a * 2 + 2]),
                        wait=[("wsu", p.tok("wsu")), ("sff", s1_)], inc="vff")
            hlast = p.emit("v", lambda e, o=hbuf[:, il * T:(il + 1) * T], a=wslot:
                           e.tensor_scalar(o, ffa[:, 0:T], fsc[:, a * 2:a * 2 + 1], None, ALU.mult),
                           wait=[("vff", v4), ("vff", v5)], inc="vff")
        for nb in range(2):
            mmt = None
            for il in range(16):
                rhs = hbuf[:, il * T + nb * 512: il * T + nb * 512 + 512]
                w = []
                if il == 0:
                    w = [("vff", hlast), ("vr", p.tok("vr"))]
                mmt = p.emit("t", lambda e, o=pB[0:1, nb * 512:nb * 512 + 512], r=rhs,
                             st=(il == 0), sp=(il == 15):
                             e.matmul(o, ones_cb[:, 0:1], r, start=st, stop=sp),
                             wait=w, inc="mm" if il == 15 else None)
            p.emit("v", lambda e, nb=nb: e.tensor_copy(SBC[0:1, nb * 512:nb * 512 + 512],
                                                       pB[0:1, nb * 512:nb * 512 + 512]),
                   wait=[("mm", mmt)], inc="vr")
        sbc_tok = bcast_row(SBC[0:1, :], SBC, T, pB, [("vr", p.tok("vr"))], "vr")
        for hc in range(NCH):
            wslot = hc % 2
            wdt_d = wdt[:, wslot * 2048:wslot * 2048 + 2048]
            wdk = p.dma("g", wdt_d, wdd[grp, hc, :, :], inc="wd",
                        wait=[("mm", dlastd[(grp, hc - 2)])] if hc >= 2 else (
                            [("mm", dlastd[(grp - 1, NCH - 1)])] if grp >= 1 else [("vac", p.tok("vac"))]))
            ps = pC if hc % 2 == 0 else pA
            mmt = None
            for il in range(16):
                lhs = wdt_d[:, il * 128:(il + 1) * 128]
                for nb in range(2):
                    rhs = hbuf[:, il * T + nb * 512: il * T + nb * 512 + 512]
                    w = []
                    if il == 0 and nb == 0:
                        w = [("wd", p.tok("wd")), ("vff", hlast), ("vdn", p.tok("vdn"))]
                        if grp == 0 and hc <= 1:
                            w.append(("vat", p.tok("vat")))
                    mmt = p.emit("t", lambda e, o=ps[:, nb * 512:nb * 512 + 512], l=lhs, r=rhs,
                                 st=(il == 0), sp=(il == 15):
                                 e.matmul(o, l, r, start=st, stop=sp),
                                 wait=w, inc="mm" if il == 15 else None)
            dlastd[(grp, hc)] = mmt
            xa = x_acc[:, hc * T:(hc + 1) * T]
            w = [("mm", mmt), ("vac", proj_last if grp == 0 else 0)]
            if grp == 0 and hc == 0:
                w.append(("wds", dst_tk))
            dv = p.emit("v", lambda e, o=xa, s=fsc[:, 4 + hc:4 + hc + 1], ps_=ps:
                        e.scalar_tensor_tensor(o, ps_[:, :], s, o, ALU.mult, ALU.add),
                        wait=w, inc="vdn")
            dv = p.emit("v", lambda e, o=xa, s=fsc[:, 4 + NCH + hc:4 + NCH + hc + 1]:
                        e.scalar_tensor_tensor(o, SBC[:, :], s, o, ALU.mult, ALU.add),
                        wait=[("vr", sbc_tok), ("vr", nds_tok)], inc="vdn")


    xn1 = BV(0, 16384)
    qx = BV(32768, 16384)
    QS0 = FV(131072, 1024)
    QS1 = FV(135168, 1024)
    QS2 = FV(139264, 1024)
    QS3 = FV(143360, 1024)
    KVT = BV(147456, 2048)
    CS1 = FV(155648, 2048)
    WPS = BV(163840, 4096)
    W1S = BV(172032, 4096)
    LR_m = FV(180224, 1024)
    LR_r = FV(184320, 1024)
    LR_v = FV(188416, 1024)
    MB1 = FV(192512, 2048)
    SCL = FV(200704, 16)
    RSB = FV(200768, 1024)
    protv = atm[:, 256:384]
    isb = atm[:, 384:385] if False else None

    tokP3 = [("vdn", p.tok("vdn")), ("vr", p.tok("vr")), ("vac", p.tok("vac"))]
    p.dma("y", protv, protd[:, :], inc="xc", wait=tokP3)
    p.dma("y", CS1[:, 0:T], cosd[:, :], inc="xc")
    p.dma("y", CS1[:, T:2 * T], sind[:, :], inc="xc")
    p.dma("y", fsc[:, 2:3], flagd[:, 0:1], inc="xc", slow=True)
    p.dma("y", fsc[:, 3:4], flagd[:, 1:2], inc="xc", slow=True)
    isld = p.dma("y", mb[0:1, 0:1].bitcast(BF16)[0:1, 0:1] if False else qh_t[0:1, 0:1], iscl[0:1, :], inc="xc")
    wsld = p.dma("y", SCL[0:1, 1:2], wscl[0:1, :], inc="xc")

    prev_mm = None
    for ci in range(NCH):
        xv = x_acc[:, ci * T:(ci + 1) * T]
        w0 = list(tokP3) if ci == 0 else ([("mm", prev_mm)] if prev_mm else [])
        vt = p.emit("v", lambda e, a=QS0[:, :], b=xv: e.tensor_mul(a, b, b), wait=w0, inc="vl1")
        for nh in range(2):
            s0, s1 = nh * 512, (nh + 1) * 512
            p.emit("t", lambda e, o=pA[0:1, s0:s1], r=xv[:, s0:s1], st=(ci == 0), sp=(ci == NCH - 1):
                   e.matmul(o, ones_c[:, 0:1], r, start=st, stop=sp),
                   wait=[("vl1", vt)] if nh == 0 else ())
        mt = None
        for nh in range(2):
            s0, s1 = nh * 512, (nh + 1) * 512
            mt = p.emit("t", lambda e, o=pB[0:1, s0:s1], r=QS0[:, s0:s1], st=(ci == 0), sp=(ci == NCH - 1):
                        e.matmul(o, ones_c[:, 0:1], r, start=st, stop=sp),
                        wait=[("vl1", vt)] if nh == 0 else (), inc="mm")
        prev_mm = mt
    p.emit("v", lambda e: e.tensor_scalar(LR_m[0:1, :], pA[0:1, :], 1.0 / D, None, ALU.mult),
           wait=[("mm", prev_mm)], inc="vl1")
    p.emit("v", lambda e: e.tensor_scalar(LR_v[0:1, :], pB[0:1, :], 1.0 / D, EPS, ALU.mult, ALU.add), inc="vl1")
    p.emit("v", lambda e: e.tensor_mul(LR_r[0:1, :], LR_m[0:1, :], LR_m[0:1, :]), inc="vl1")
    v1 = p.emit("v", lambda e: e.tensor_sub(LR_v[0:1, :], LR_v[0:1, :], LR_r[0:1, :]), inc="vl1")
    s1_ = p.emit("s", lambda e: e.activation(out=LR_r[0:1, :], in_=LR_v[0:1, :], func=AF.Sqrt),
                 wait=[("vl1", v1)], inc="sl1")
    v2 = p.emit("v", lambda e: e.reciprocal(LR_r[0:1, :], LR_r[0:1, :]), wait=[("sl1", s1_)], inc="vl1")
    vcv = p.emit("v", lambda e: e.tensor_copy(SCL[0:1, 0:1], qh_t[0:1, 0:1]),
                 wait=[("xc", p.tok("xc"))], inc="vl1")
    vinv = p.emit("v", lambda e: e.reciprocal(SCL[0:1, 2:3], SCL[0:1, 0:1]), wait=[("vl1", vcv)], inc="vl1")
    vsw = p.emit("v", lambda e: e.tensor_mul(SCL[0:1, 3:4], SCL[0:1, 0:1], SCL[0:1, 1:2]),
                 wait=[("vl1", vcv)], inc="vl1")
    vrsi = p.emit("v", lambda e: e.tensor_scalar(LR_v[0:1, 0:T], LR_r[0:1, :], SCL[0:1, 2:3], None, ALU.mult),
                  wait=[("vl1", v2), ("vl1", vinv)], inc="vl1")
    b1 = bcast_row(LR_m[0:1, :], MB1[:, 0:T], T, pC, [("vl1", v2)], "vl1")
    b2 = bcast_row(LR_r[0:1, :], MB1[:, T:2 * T], T, pC, [("vl1", b1)], "vl1")
    rsb_tok = bcast_row(LR_v[0:1, 0:T], RSB, T, pC, [("vl1", vrsi), ("vl1", b2)], "vl1")
    cv2 = p.dma("y", LR_m[0:1, 0:32], convd[0:1, :], inc="xc", wait=[("vl1", b2)])
    bcv = bcast_row(LR_m[0:1, 0:32], fsc[:, 4:36], 32, pD, [("xc", p.tok("xc"))], "vl1")
    msw = p.emit("t", lambda e: e.matmul(pD[:, 0:1], ones_r[0:1, :], SCL[0:1, 3:4], start=True, stop=True),
                 wait=[("vl1", vsw), ("vl1", bcv)], inc="mm")
    bsw = p.emit("v", lambda e: e.tensor_copy(fsc[:, 0:1], pD[:, 0:1]), wait=[("mm", msw)], inc="vl1")

    QC = float(2 ** 20 + 1)
    QG = float(3 * 2 ** 13)
    QTH = float((1.5 * 2.0 ** -6) ** 2)
    qx_done = xn1_done = None
    for ci in range(NCH):
        xv = x_acc[:, ci * T:(ci + 1) * T]
        p.emit("v", lambda e, a=xv: e.tensor_sub(QS0[:, :], a, MB1[:, 0:T]),
               wait=[("vl1", rsb_tok), ("sl1", p.tok("sl1"))] if ci == 0 else (), inc="vl1")
        xn1_done = p.emit("v", lambda e, o=xn1[:, ci * T:(ci + 1) * T]:
                          e.tensor_mul(o, QS0[:, :], MB1[:, T:2 * T]), inc="vl1")
        p.emit("v", lambda e: e.tensor_mul(QS0[:, :], QS0[:, :], RSB[:, :]), inc="vl1")
        p.emit("v", lambda e: e.tensor_scalar(QS0[:, :], QS0[:, :], -448.0, 448.0, ALU.max, ALU.min), inc="vl1")
        p.emit("v", lambda e: e.tensor_scalar(QS1[:, :], QS0[:, :], QG, QG, ALU.add, ALU.subtract), inc="vl1")
        st_ = p.emit("s", lambda e: e.mul(QS2[:, :], QS0[:, :], QC), wait=[("vl1", p.tok("vl1"))], inc="sl1")
        p.emit("v", lambda e: e.tensor_sub(QS3[:, :], QS2[:, :], QS0[:, :]), wait=[("sl1", st_)], inc="vl1")
        p.emit("v", lambda e: e.tensor_sub(QS2[:, :], QS2[:, :], QS3[:, :]), inc="vl1")
        p.emit("v", lambda e: e.tensor_mul(QS3[:, :], QS0[:, :], QS0[:, :]), inc="vl1")
        p.emit("v", lambda e: e.tensor_scalar(QS3[:, :], QS3[:, :], QTH, 1e30, ALU.subtract, ALU.mult), inc="vl1")
        p.emit("v", lambda e: e.tensor_scalar(QS3[:, :], QS3[:, :], 0.0, 1.0, ALU.max, ALU.min), inc="vl1")
        p.emit("v", lambda e: e.tensor_sub(QS2[:, :], QS2[:, :], QS1[:, :]), inc="vl1")
        p.emit("v", lambda e: e.tensor_mul(QS2[:, :], QS2[:, :], QS3[:, :]), inc="vl1")
        qx_done = p.emit("v", lambda e, o=qx[:, ci * T:(ci + 1) * T]:
                         e.tensor_add(o, QS2[:, :], QS1[:, :]), inc="vl1")

    pub_last = None
    for kv in range(8):
        wsl = (kv % 2) * NCH * 128
        wkt = p.dma("g", WPS[:, wsl:wsl + NCH * 128], wpk[kv, :, :], inc="wk1",
                    wait=[("mm", p.tok("mm"))] if kv >= 1 else [("vl1", qx_done)])
        mmt = None
        for fc in range(NCH):
            lhs = WPS[:, wsl + fc * 128:wsl + (fc + 1) * 128]
            for nb in range(2):
                rhs = xn1[:, fc * T + nb * 512: fc * T + nb * 512 + 512]
                w = []
                if fc == 0 and nb == 0:
                    w = [("wk1", p.tok("wk1")), ("vkv", p.tok("vkv"))]
                    if kv == 0:
                        w.append(("vl1", xn1_done))
                mmt = p.emit("t", lambda e, o=pA[:, nb * 512:nb * 512 + 512], l=lhs, r=rhs,
                             st=(fc == 0), sp=(fc == NCH - 1):
                             e.matmul(o, l, r, start=st, stop=sp),
                             wait=w, inc="mm" if fc == NCH - 1 else None)
        kev = p.emit("v", lambda e: e.tensor_copy(KVT[:, 0:T], pA[:, :]),
                     wait=[("mm", mmt), ("pub", p.tok("pub"))], inc="vkv")
        rt = None
        for nb in range(2):
            rt = p.emit("t", lambda e, o=pB[:, nb * 512:nb * 512 + 512],
                        r=KVT[:, nb * 512:nb * 512 + 512]:
                        e.matmul(o, protv, r, start=True, stop=True),
                        wait=[("vkv", kev), ("vrp", p.tok("vrp"))] if nb == 0 else (), inc="mm")
        r1 = p.emit("v", lambda e: e.tensor_mul(QS0[:, :], KVT[:, 0:T], CS1[:, 0:T]),
                    wait=[("vkv", kev)], inc="vrp")
        r2 = p.emit("v", lambda e: e.tensor_mul(QS1[:, :], pB[:, :], CS1[:, T:2 * T]),
                    wait=[("mm", rt)], inc="vrp")
        r3 = p.emit("v", lambda e: e.tensor_add(KVT[:, T:2 * T], QS0[:, :], QS1[:, :]),
                    wait=[("vrp", r2)], inc="vrp")
        pub_last = p.dma("y", kv_out[0, kv, :, :], KVT[:, T:2 * T], inc="pub", wait=[("vrp", r3)])
        wvt_ = p.dma("g", WPS[:, wsl:wsl + NCH * 128], wpv[kv, :, :], inc="wk1",
                     wait=[("mm", p.tok("mm"))])
        mmt = None
        for fc in range(NCH):
            lhs = WPS[:, wsl + fc * 128:wsl + (fc + 1) * 128]
            for nb in range(2):
                rhs = xn1[:, fc * T + nb * 512: fc * T + nb * 512 + 512]
                w = [("wk1", p.tok("wk1")), ("vkv", p.tok("vkv"))] if (fc == 0 and nb == 0) else ()
                mmt = p.emit("t", lambda e, o=pA[:, nb * 512:nb * 512 + 512], l=lhs, r=rhs,
                             st=(fc == 0), sp=(fc == NCH - 1):
                             e.matmul(o, l, r, start=st, stop=sp),
                             wait=w, inc="mm" if fc == NCH - 1 else None)
        vev = p.emit("v", lambda e: e.tensor_copy(KVT[:, 0:T], pA[:, :]),
                     wait=[("mm", mmt), ("pub", p.tok("pub"))], inc="vkv")
        pub_last = p.dma("y", kv_out[1, kv, :, :], KVT[:, 0:T], inc="pub", wait=[("vkv", vev)])

    cc_tok = p.emit("g", lambda e: e.collective_compute(
        "AllGather", ALU.bypass, replica_groups=[[0, 1], [2, 3], [4, 5], [6, 7]],
        ins=[kv_out.ap().opt()], outs=[kv_gth.ap().opt()]),
        wait=[("pub", p.tok("pub"))], inc="cc")

    ffn1_last = None
    for oc in range(NCH):
        wsl = (oc % 2) * NCH * 128
        wft = p.dma("g", W1S[:, wsl:wsl + NCH * 128], w1d[oc, :, :], inc="wf1",
                    wait=[("mm", p.tok("mm"))] if oc >= 2 else [("vl1", qx_done)])
        ps = pA if oc % 2 == 0 else pB
        mmt = None
        for ic in range(NCH):
            lhs = W1S[:, wsl + ic * 128:wsl + (ic + 1) * 128]
            for nb in range(2):
                rhs = qx[:, ic * T + nb * 512: ic * T + nb * 512 + 512]
                w = []
                if ic == 0 and nb == 0:
                    w = [("wf1", p.tok("wf1")), ("vf1", p.tok("vf1")), ("vkv", p.tok("vkv")),
                         ("vrp", p.tok("vrp"))]
                mmt = p.emit("t", lambda e, o=ps[:, nb * 512:nb * 512 + 512], l=lhs, r=rhs,
                             st=(ic == 0), sp=(ic == NCH - 1):
                             e.matmul(o, l, r, start=st, stop=sp),
                             wait=w, inc="mm" if ic == NCH - 1 else None)
        ffn1_last = p.emit("v", lambda e, o=x_acc[:, oc * T:(oc + 1) * T], ps_=ps:
                           e.scalar_tensor_tensor(o, ps_[:, :], fsc[:, 0:1], o, ALU.mult, ALU.add),
                           wait=[("mm", mmt), ("vl1", bsw)], inc="vf1")

    masks = BV(32768, 9216)
    CS2 = FV(51200, 2048)
    ST2 = BV(59392, 1024)
    SS2 = FV(61440, 1024)
    KB = BV(131072, 8192)
    VB = BV(147456, 8256)
    PT1 = BV(164352, 4608)
    WQS = BV(173568, 4096)
    QCUR = BV(181760, 2048)
    WOS = BV(185856, 4096)
    A1Q = BV(194048, 4096)
    KB4 = KB.rearrange("p (v t) -> p v t", v=4)
    VB4 = VB.rearrange("p (v b c) -> p v b c", v=4, b=16)

    mld = p.dma("y", masks[:, :], maskd[:, :], inc="xc",
                wait=[("vf1", p.tok("vf1"))])
    p.dma("y", CS2[:, 0:T], cosd[:, :], inc="xc")
    csl2 = p.dma("y", CS2[:, T:2 * T], sind[:, :], inc="xc")

    att1_fin = None
    for hf in range(2):
        rb0w = [("cc", cc_tok), ("xc", p.tok("xc")), ("vf1", p.tok("vf1"))]
        if hf == 1:
            rb0w.append(("mm", p.tok("mm")))
            rb0w.append(("vb2", p.tok("vb2")))
        for (reg, base_src) in ((0, 0), (1, 1)):
            dst = KB4[:, :, reg * T:(reg + 1) * T]
            d1 = p.dma("y", dst, kv_gth[base_src, 0, 4 * hf:4 * hf + 4, :, :].rearrange("v p t -> p v t"),
                       inc="xr", wait=rb0w if reg == 0 else ())
            b1_ = p.emit("v", lambda e, o=dst: e.tensor_scalar(o, o, fsc[:, 3:4], None, ALU.mult),
                         wait=[("xr", p.tok("xr"))], inc="vb2")
            for kvl in range(4):
                d2 = p.dma("y", ST2[:, :], kv_gth[1 - base_src, 0, 4 * hf + kvl, :, :], inc="xr",
                           wait=[("vb2", p.tok("vb2"))])
                b1_ = p.emit("v", lambda e, o=KB4[:, kvl, reg * T:(reg + 1) * T]:
                             e.scalar_tensor_tensor(o, ST2[:, :], fsc[:, 2:3], o, ALU.mult, ALU.add),
                             wait=[("xr", p.tok("xr"))], inc="vb2")
        for kvl in range(4):
            kvg = 4 * hf + kvl
            w0c = fsc[:, 4 + 2 * kvg:5 + 2 * kvg]
            w1c = fsc[:, 5 + 2 * kvg:6 + 2 * kvg]
            kown = KB4[:, kvl, 0:T]
            kpar = KB4[:, kvl, T:2 * T]
            bb = p.emit("v", lambda e, o=ST2[:, 0:1], i=KB4[:, kvl, 2 * T - 1:2 * T]:
                        e.tensor_scalar(o, i, fsc[:, 2:3], None, ALU.mult),
                        wait=[("vb2", p.tok("vb2"))], inc="vb2")
            p.emit("v", lambda e, o=ST2[:, 1:T], i=kown[:, 0:T - 1], w=w0c:
                   e.tensor_scalar(o, i, w, None, ALU.mult), inc="vb2")
            p.emit("v", lambda e, o=ST2[:, 0:1], w=w0c:
                   e.tensor_scalar(o, ST2[:, 0:1], w, None, ALU.mult), inc="vb2")
            p.emit("v", lambda e, o=kown, w=w1c:
                   e.scalar_tensor_tensor(o, kown, w, ST2[:, 0:T], ALU.mult, ALU.add), inc="vb2")
            p.emit("v", lambda e, o=ST2[:, 1:T], i=kpar[:, 0:T - 1], w=w0c:
                   e.tensor_scalar(o, i, w, None, ALU.mult), inc="vb2")
            p.emit("v", lambda e, o=ST2[:, 0:1]: e.memset(o, 0.0), inc="vb2")
            p.emit("v", lambda e, o=kpar, w=w1c:
                   e.scalar_tensor_tensor(o, kpar, w, ST2[:, 0:T], ALU.mult, ALU.add), inc="vb2")
        VFM = PT1[:, 0:2048]
        on1 = p.emit("v", lambda e: e.memset(VB4[:, :, :, 128:129], 1.0),
                     wait=[("vb2", p.tok("vb2"))] + ([("mm", p.tok("mm"))] if hf == 1 else []), inc="vb2")
        for kvl in range(4):
            kvg = 4 * hf + kvl
            w0c = fsc[:, 4 + 32 + 2 * kvg - 28:5 + 32 + 2 * kvg - 28] if False else fsc[:, 20 + 2 * kvg:21 + 2 * kvg]
            w1c = fsc[:, 21 + 2 * kvg:22 + 2 * kvg]
            dv1 = p.dma("y", VFM[:, 0:T], kv_gth[0, 1, kvg, :, :], inc="xr",
                        wait=[("mm", p.tok("mm")), ("vb2", p.tok("vb2"))] if kvl == 0 else [("vb2", p.tok("vb2"))])
            p.emit("v", lambda e, o=VFM[:, 0:T]: e.tensor_scalar(o, o, fsc[:, 3:4], None, ALU.mult),
                   wait=[("xr", p.tok("xr"))], inc="vb2")
            dv2 = p.dma("y", ST2[:, :], kv_gth[1, 1, kvg, :, :], inc="xr", wait=[("vb2", p.tok("vb2"))])
            p.emit("v", lambda e, o=VFM[:, 0:T]:
                   e.scalar_tensor_tensor(o, ST2[:, :], fsc[:, 2:3], o, ALU.mult, ALU.add),
                   wait=[("xr", p.tok("xr"))], inc="vb2")
            dv3 = p.dma("y", VFM[:, T:2 * T], kv_gth[1, 1, kvg, :, :], inc="xr", wait=[("vb2", p.tok("vb2"))])
            p.emit("v", lambda e, o=VFM[:, T:2 * T]: e.tensor_scalar(o, o, fsc[:, 3:4], None, ALU.mult),
                   wait=[("xr", p.tok("xr"))], inc="vb2")
            dv4 = p.dma("y", ST2[:, :], kv_gth[0, 1, kvg, :, :], inc="xr", wait=[("vb2", p.tok("vb2"))])
            p.emit("v", lambda e, o=VFM[:, T:2 * T]:
                   e.scalar_tensor_tensor(o, ST2[:, :], fsc[:, 2:3], o, ALU.mult, ALU.add),
                   wait=[("xr", p.tok("xr"))], inc="vb2")
            bb = p.emit("v", lambda e, o=ST2[:, 0:1], i=VFM[:, 2 * T - 1:2 * T]:
                        e.tensor_scalar(o, i, fsc[:, 2:3], None, ALU.mult), inc="vb2")
            p.emit("v", lambda e, o=ST2[:, 1:T], i=VFM[:, 0:T - 1], w=w0c:
                   e.tensor_scalar(o, i, w, None, ALU.mult), inc="vb2")
            p.emit("v", lambda e, o=ST2[:, 0:1], w=w0c:
                   e.tensor_scalar(o, ST2[:, 0:1], w, None, ALU.mult), inc="vb2")
            p.emit("v", lambda e, o=VFM[:, 0:T], w=w1c:
                   e.scalar_tensor_tensor(o, VFM[:, 0:T], w, ST2[:, 0:T], ALU.mult, ALU.add), inc="vb2")
            p.emit("v", lambda e, o=ST2[:, 1:T], i=VFM[:, T:2 * T - 1], w=w0c:
                   e.tensor_scalar(o, i, w, None, ALU.mult), inc="vb2")
            p.emit("v", lambda e, o=ST2[:, 0:1]: e.memset(o, 0.0), inc="vb2")
            vcv_ = p.emit("v", lambda e, o=VFM[:, T:2 * T], w=w1c:
                          e.scalar_tensor_tensor(o, VFM[:, T:2 * T], w, ST2[:, 0:T], ALU.mult, ALU.add),
                          inc="vb2")
            for blk in range(16):
                trs = pT[:, (blk % 4) * 128:((blk % 4) + 1) * 128]
                mtt = p.emit("t", lambda e, o=trs, i=VFM[:, blk * 128:(blk + 1) * 128]:
                             e.transpose(o, i, idn[:, :]),
                             wait=[("vb2", vcv_ if blk == 0 else 0), ("vb2", p.tok("vb2"))], inc="mm")
                p.emit("v", lambda e, o=VB4[:, kvl, blk, 0:128], i=trs: e.tensor_copy(o, i),
                       wait=[("mm", mtt)], inc="vb2")
        for qi in range(8):
            qh = 8 * hf + qi
            kvl = qi // 2
            qsl = (qi % 2) * T
            wqt = p.dma("g", WQS[:, (qi % 2) * NCH * 128:(qi % 2) * NCH * 128 + NCH * 128],
                        wpq[qh, :, :], inc="wq2",
                        wait=[("mm", p.tok("mm"))] if (hf or qi) else [("vf1", p.tok("vf1"))])
            mmq = None
            for fc in range(NCH):
                lhs = WQS[:, (qi % 2) * NCH * 128 + fc * 128:(qi % 2) * NCH * 128 + (fc + 1) * 128]
                for nb in range(2):
                    rhs = xn1[:, fc * T + nb * 512: fc * T + nb * 512 + 512]
                    w = [("wq2", p.tok("wq2")), ("vq2", p.tok("vq2"))] if (fc == 0 and nb == 0) else ()
                    mmq = p.emit("t", lambda e, o=pA[:, nb * 512:nb * 512 + 512], l=lhs, r=rhs,
                                 st=(fc == 0), sp=(fc == NCH - 1):
                                 e.matmul(o, l, r, start=st, stop=sp),
                                 wait=w, inc="mm" if fc == NCH - 1 else None)
            qev = p.emit("v", lambda e, o=QCUR[:, qsl:qsl + T]: e.tensor_copy(o, pA[:, :]),
                         wait=[("mm", mmq)], inc="vq2")
            rt = None
            for nb in range(2):
                rt = p.emit("t", lambda e, o=pB[:, nb * 512:nb * 512 + 512],
                            r=QCUR[:, qsl + nb * 512:qsl + nb * 512 + 512]:
                            e.matmul(o, protv, r, start=True, stop=True),
                            wait=[("vq2", qev), ("vq2", p.tok("vq2"))] if nb == 0 else (), inc="mm")
            p.emit("v", lambda e, o=ST2[:, :], i=QCUR[:, qsl:qsl + T]:
                   e.tensor_mul(o, i, CS2[:, 0:T]), wait=[("vq2", qev)], inc="vq2")
            p.emit("v", lambda e, o=QCUR[:, qsl:qsl + T]: e.tensor_mul(o, pB[:, :], CS2[:, T:2 * T]),
                   wait=[("mm", rt)], inc="vq2")
            qrope = p.emit("v", lambda e, o=QCUR[:, qsl:qsl + T]: e.tensor_add(o, o, ST2[:, :]),
                           inc="vq2")
            for qthf in range(2):
                exp_toks = []
                for q4 in range(4):
                    qt = qthf * 4 + q4
                    mml = {}
                    for l in range(9):
                        s = (qt + l + 8) % 16
                        lhs = KB4[:, kvl, s * 128:(s + 1) * 128]
                        o = pA[:, l * 128:(l + 1) * 128] if l < 8 else pB[:, 0:128]
                        w = []
                        if l == 0:
                            w = [("vq2", qrope), ("vsc2", p.tok("vsc2")), ("vb2", p.tok("vb2"))]
                        mml[l] = p.emit("t", lambda e, o=o, l_=lhs, r=QCUR[:, qsl + qt * 128:qsl + (qt + 1) * 128]:
                                        e.matmul(o, l_, r, start=True, stop=True),
                                        wait=w, inc="mm" if l in (7, 8) else None)
                    mw = masks[:, qt * 1152:qt * 1152 + 1024]
                    mn = masks[:, qt * 1152 + 1024:qt * 1152 + 1152]
                    vs1 = p.emit("v", lambda e, m=mw: e.tensor_add(SS2[:, :], pA[:, :], m),
                                 wait=[("mm", mml[7]), ("sx2", p.tok("sx2"))], inc="vsc2")
                    ex1 = p.emit("s", lambda e, o=PT1[:, q4 * 1152:q4 * 1152 + 1024]:
                                 e.activation(out=o, in_=SS2[:, :], func=AF.Exp),
                                 wait=[("vsc2", vs1)], inc="sx2")
                    vs2 = p.emit("v", lambda e, m=mn: e.tensor_add(SS2[:, 0:128], pB[:, 0:128], m),
                                 wait=[("mm", mml[8]), ("sx2", ex1)], inc="vsc2")
                    ex2 = p.emit("s", lambda e, o=PT1[:, q4 * 1152 + 1024:q4 * 1152 + 1152]:
                                 e.activation(out=o, in_=SS2[:, 0:128], func=AF.Exp),
                                 wait=[("vsc2", vs2)], inc="sx2")
                    exp_toks.append(ex2)
                for q4 in range(4):
                    qt = qthf * 4 + q4
                    mmt = None
                    for l in range(9):
                        s = (qt + l + 8) % 16
                        lhs = PT1[:, q4 * 1152 + l * 128:q4 * 1152 + (l + 1) * 128]
                        rhs = VB4[:, kvl, s, :]
                        w = []
                        if l == 0:
                            w = [("sx2", exp_toks[q4]), ("vpv2", p.tok("vpv2"))]
                        mmt = p.emit("t", lambda e, o=pD[:, 0:129], l_=lhs, r=rhs, st=(l == 0), sp=(l == 8):
                                     e.matmul(o, l_, r, start=st, stop=sp),
                                     wait=w, inc="mm" if l == 8 else None)
                    qs_ = qt % 2
                    vr = p.emit("v", lambda e, d=dnorm[:, qs_:qs_ + 1]: e.reciprocal(d, pD[:, 128:129]),
                                wait=[("mm", mmt)], inc="vpv2")
                    vn = p.emit("v", lambda e, o=atm[:, qs_ * 128:(qs_ + 1) * 128], d=dnorm[:, qs_:qs_ + 1]:
                                e.tensor_scalar(o, pD[:, 0:128], d, None, ALU.mult),
                                wait=[("vpv2", vr), ("va2", p.tok("va2"))], inc="vpv2")
                    trs = pT[:, 512 + (qt % 2) * 128:512 + ((qt % 2) + 1) * 128]
                    mtt = p.emit("t", lambda e, o=trs, i=atm[:, qs_ * 128:(qs_ + 1) * 128]:
                                 e.transpose(o, i, idn[:, :]),
                                 wait=[("vpv2", vn), ("va2", p.tok("va2"))], inc="mm")
                    att1_fin = p.emit("v", lambda e, o=A1Q[:, (qi % 4) * T + qt * 128:(qi % 4) * T + (qt + 1) * 128],
                                      i=trs: e.tensor_copy(o, i), wait=[("mm", mtt)], inc="va2")
            if qi % 4 == 3:
                qtr = qh // 4
                for oc in range(NCH):
                    osl = (oc % 2) * NCH * 128
                    wot = p.dma("g", WOS[:, osl:osl + 512], wod[oc, :, 4 * qtr * 128:(4 * qtr + 4) * 128],
                                inc="wo2", wait=[("mm", p.tok("mm"))] if oc >= 2 else [("va2", att1_fin)])
                    mmt = None
                    for icl in range(4):
                        lhs = WOS[:, osl + icl * 128:osl + (icl + 1) * 128]
                        for nb in range(2):
                            rhs = A1Q[:, icl * T + nb * 512: icl * T + nb * 512 + 512]
                            w = []
                            if icl == 0 and nb == 0:
                                w = [("wo2", p.tok("wo2")), ("vo2", p.tok("vo2")), ("va2", att1_fin)]
                            mmt = p.emit("t", lambda e, o=pC[:, nb * 512:nb * 512 + 512], l=lhs, r=rhs,
                                         st=(icl == 0), sp=(icl == 3):
                                         e.matmul(o, l, r, start=st, stop=sp),
                                         wait=w, inc="mm" if icl == 3 else None)
                    p.emit("v", lambda e, o=x_acc[:, oc * T:(oc + 1) * T]:
                           e.tensor_add(o, o, pC[:, :]),
                           wait=[("mm", mmt), ("vf1", p.tok("vf1"))], inc="vo2")

    for qtr in range(4):
        sl = x_acc[:, qtr * 4 * T:(qtr + 1) * 4 * T]
        p.dma("y", y[:, qtr * 4 * T:(qtr + 1) * 4 * T], sl, inc="st",
              wait=[("vo2", p.tok("vo2")), ("vdn", p.tok("vdn"))])
    p.emit("y", None, wait=[("st", p.tok("st"))])

    p.replay()
    return nc


def _tile_lhsT(w):
    K, M = w.shape
    kb, mb_ = K // 128, M // 128
    t = w.reshape(kb, 128, mb_, 128).transpose(2, 1, 0, 3)
    return np.ascontiguousarray(t.reshape(mb_, 128, kb * 128))


def prep_inputs(inp):
    x = np.asarray(inp["L0_attn_x"])
    Wqkv = np.asarray(inp["L0_attn_W_qkv"])
    bqkv = np.asarray(inp["L0_attn_b_qkv"])
    Wproj = np.asarray(inp["L0_attn_W_proj"])
    bproj = np.asarray(inp["L0_attn_b_proj"])
    gw = np.asarray(inp["L0_ffn_gate_w"])
    gs = np.asarray(inp["L0_ffn_gate_scale"])
    uw = np.asarray(inp["L0_ffn_up_w"])
    us = np.asarray(inp["L0_ffn_up_scale"])
    dw = np.asarray(inp["L0_ffn_down_w"])
    ds = np.asarray(inp["L0_ffn_down_scale"])

    wq0 = _tile_lhsT(Wqkv[:, 0:2048])
    wk0 = _tile_lhsT(Wqkv[:, 2048:4096])
    wv0 = np.ascontiguousarray(
        Wqkv[:, 4096:6144].reshape(NCH, 128, 4, 512).transpose(2, 0, 1, 3))
    bq0 = np.ascontiguousarray(bqkv[0:2048].reshape(16, 128))
    bk0 = np.ascontiguousarray(bqkv[2048:4096].reshape(16, 128))
    bv0 = np.ascontiguousarray(bqkv[4096:6144].reshape(1, 2048))
    wp0 = _tile_lhsT(Wproj)
    bp0 = np.ascontiguousarray(bproj.reshape(16, 128))
    wg_ = _tile_lhsT(gw.T)
    wu_ = _tile_lhsT(uw.T)
    wdT = dw.T
    wd_ = wdT.reshape(4, 16, 128, 16, 128)
    wd_ = np.ascontiguousarray(wd_.transpose(0, 3, 2, 1, 4).reshape(4, 16, 128, 16 * 128))
    gsc_ = np.ascontiguousarray(gs.reshape(64, 128))
    usc_ = np.ascontiguousarray(us.reshape(64, 128))
    dsc_ = np.ascontiguousarray(ds.reshape(16, 128))
    ident = np.eye(128, dtype=np.float32).astype(ml_dtypes.bfloat16)

    Wpack = np.asarray(inp["L1_attn_W_pack"])
    Wo = np.asarray(inp["L1_attn_W_o"])
    convk = np.asarray(inp["L1_attn_conv_k"]).astype(np.float32)
    convv = np.asarray(inp["L1_attn_conv_v"]).astype(np.float32)
    w1 = np.asarray(inp["L1_ffn_weight"])
    wscale = np.asarray(inp["L1_ffn_weight_scale"]).astype(np.float32)
    iscale = np.asarray(inp["L1_ffn_input_scale"])

    wpq_ = _tile_lhsT(Wpack[:, 0:2048])
    wpk_ = _tile_lhsT(Wpack[:, 2048:3072])
    wpv_ = _tile_lhsT(Wpack[:, 3072:4096])
    wod_ = _tile_lhsT(Wo)
    w1T = np.ascontiguousarray(w1.T)
    w1d_ = _tile_lhsT(w1T.view(np.uint8)).view(w1.dtype)
    protm = np.zeros((128, 128), np.float32)
    for i in range(64):
        protm[i, i + 64] = 1.0
        protm[i + 64, i] = -1.0
    protd_ = protm.astype(ml_dtypes.bfloat16)
    convd_ = np.concatenate([convk.reshape(-1), convv.reshape(-1)]).reshape(1, 32).astype(np.float32)
    iscl_ = np.asarray(iscale).reshape(1, 1)
    wscl_ = wscale.reshape(1, 1)
    SCHf = np.float32(SCH)
    inv_freq = (1.0 / 100000.0 ** (np.arange(0, 128, 2, dtype=np.float32) / 128.0)).astype(np.float32)

    shared = dict(wq0=wq0, wk0=wk0, wv0=wv0, bq0=bq0, bk0=bk0, bv0=bv0,
                  wp0=wp0, bp0=bp0, wg=wg_, wu=wu_, wd=wd_,
                  gsc=gsc_, usc=usc_, dsc=dsc_, ident=ident,
                  wpq=wpq_, wpk=wpk_, wpv=wpv_, wod=wod_, w1d=w1d_,
                  protd=protd_, convd=convd_, iscl=iscl_, wscl=wscl_)
    in_maps = []
    for c in range(8):
        b, pp = c // 2, c % 2
        xo = np.ascontiguousarray(x[b, 1024 * pp:1024 * (pp + 1), :].T)
        xp = np.ascontiguousarray(x[b, 1024 * (1 - pp):1024 * (2 - pp), :].T)
        pos = (1024 * pp + np.arange(1024, dtype=np.float32))
        fr = np.outer(pos, inv_freq)
        cosf = np.cos(fr).astype(np.float32) * SCHf
        sinf = np.sin(fr).astype(np.float32) * SCHf
        cosd_ = np.ascontiguousarray(np.concatenate([cosf.T, cosf.T], axis=0))
        sind_ = np.ascontiguousarray(np.concatenate([sinf.T, sinf.T], axis=0))
        mask = np.full((8, 9, 128, 128), -1e30, np.float32)
        for t in range(8):
            for l in range(9):
                s = (t + l + 8) % 16
                j0 = 1024 * pp + 128 * s if s < 8 else 1024 * (1 - pp) + 128 * (s - 8)
                i_ = 1024 * pp + 128 * t + np.arange(128)[None, :]
                j_ = j0 + np.arange(128)[:, None]
                valid = (j_ <= i_) & (i_ - j_ < 1024)
                mask[t, l][valid] = 0.0
        maskd_ = np.ascontiguousarray(mask.transpose(2, 0, 1, 3).reshape(128, 9216)).astype(ml_dtypes.bfloat16)
        flagd_ = np.full((128, 2), 0.0, np.float32)
        flagd_[:, 0] = float(pp)
        flagd_[:, 1] = 1.0 - float(pp)
        m = dict(shared)
        m["x_own"] = xo
        m["x_part"] = xp
        m["cosd"] = cosd_
        m["sind"] = sind_
        m["maskd"] = maskd_
        m["flagd"] = flagd_
        in_maps.append(m)
    return in_maps


def assemble_output(results):
    out = np.empty((4, 2048, 2048), dtype=np.float32)
    for c in range(8):
        b, pp = c // 2, c % 2
        sh = results[c]["y"].reshape(128, 16, 1024)
        out[b, 1024 * pp:1024 * (pp + 1), :] = sh.transpose(2, 1, 0).reshape(1024, 2048)
    return out


def kernel(**inputs):
    in_maps = prep_inputs(inputs)
    nc = build_nc(dbg_mode=None)
    res = run_bass_kernel_spmd(nc, in_maps, core_ids=list(range(8)))
    return assemble_output(res.results)


# version 04
# speedup vs baseline: 1.1468x; 1.0581x over previous
import numpy as np
import ml_dtypes
import concourse.bass as bass
import concourse.mybir as mybir
from concourse.bass_utils import run_bass_kernel_spmd

F32 = mybir.dt.float32
BF16 = mybir.dt.bfloat16
I32 = mybir.dt.int32
FP8 = mybir.dt.float8e4
AF = mybir.ActivationFunctionType
ALU = mybir.AluOpType
AX = mybir.AxisListType

D = 2048
T = 1024
NCH = 16
NH0 = 16
NIC = 64
SC = float(1.0 / np.sqrt(np.float32(128.0)))
SCH = float(np.sqrt(SC))
EPS = 1e-5

ENGS = ("t", "v", "s", "g", "y")
ARENA_BYTES = 210944


class Prog:
    def __init__(self, nc):
        self.nc = nc
        self.q = {e: [] for e in ENGS}
        self.sems = {}
        self.cnt = {}

    def sem(self, name):
        if name not in self.sems:
            self.sems[name] = self.nc.alloc_semaphore(name)
            self.cnt[name] = 0
        return self.sems[name]

    def emit(self, eng, fn=None, wait=(), inc=None, inc_by=1):
        tok = None
        if inc is not None:
            self.sem(inc)
            self.cnt[inc] += inc_by
            tok = self.cnt[inc]
        for (sn, _v) in wait:
            self.sem(sn)
        waits = tuple((sn, v) for (sn, v) in wait if v > 0)
        sems = self.sems

        def job(e, waits=waits, fn=fn, inc=inc, inc_by=inc_by):
            for (sn, val) in waits:
                e.wait_ge(sems[sn], val)
            if fn is not None:
                inst = fn(e)
                if inc is not None:
                    inst.then_inc(sems[inc], inc_by)
            elif inc is not None:
                e.engine_nop().then_inc(sems[inc], inc_by)
        self.q[eng].append(job)
        return tok

    def dma(self, eng, out, in_, inc, wait=(), slow=False):
        if slow:
            return self.emit(eng, lambda e, o=out, i=in_: e.dma_start(
                out=o, in_=i, allow_slow_non_contiguous=True), wait=wait, inc=inc, inc_by=16)
        return self.emit(eng, lambda e, o=out, i=in_: e.dma_start(out=o, in_=i),
                         wait=wait, inc=inc, inc_by=16)

    def tok(self, name):
        self.sem(name)
        return self.cnt[name]

    def replay(self):
        nc = self.nc
        with nc.Block() as block:
            @block.tensor
            def _(t):
                for job in self.q["t"]:
                    job(t)

            @block.vector
            def _(v):
                for job in self.q["v"]:
                    job(v)

            @block.scalar
            def _(s):
                for job in self.q["s"]:
                    job(s)

            @block.gpsimd
            def _(g):
                for job in self.q["g"]:
                    job(g)

            @block.sync
            def _(y):
                for job in self.q["y"]:
                    job(y)


def build_nc(dbg_mode=None):
    nc = bass.Bass("TRN2", target_bir_lowering=False, detect_race_conditions=False)
    p = Prog(nc)
    DT = nc.dram_tensor
    A = nc.alloc_sbuf_tensor
    P_ = nc.alloc_psum_tensor

    x_own = DT("x_own", [D, T], F32, kind="ExternalInput")
    x_part = DT("x_part", [D, T], F32, kind="ExternalInput")
    wq0 = DT("wq0", [NH0, 128, NCH * 128], F32, kind="ExternalInput")
    wk0 = DT("wk0", [NH0, 128, NCH * 128], F32, kind="ExternalInput")
    wv0 = DT("wv0", [4, NCH, 128, 512], F32, kind="ExternalInput")
    bq0 = DT("bq0", [NH0, 128], F32, kind="ExternalInput")
    bk0 = DT("bk0", [NH0, 128], F32, kind="ExternalInput")
    bv0 = DT("bv0", [1, D], F32, kind="ExternalInput")
    wp0 = DT("wp0", [NCH, 128, NCH * 128], F32, kind="ExternalInput")
    bp0 = DT("bp0", [NCH, 128], F32, kind="ExternalInput")
    wgd = DT("wg", [NIC, 128, NCH * 128], I32, kind="ExternalInput")
    wud = DT("wu", [NIC, 128, NCH * 128], I32, kind="ExternalInput")
    wdd = DT("wd", [4, NCH, 128, 16 * 128], I32, kind="ExternalInput")
    gsc = DT("gsc", [NIC, 128], F32, kind="ExternalInput")
    usc = DT("usc", [NIC, 128], F32, kind="ExternalInput")
    dsc = DT("dsc", [NCH, 128], F32, kind="ExternalInput")
    ident = DT("ident", [128, 128], BF16, kind="ExternalInput")
    wpq = DT("wpq", [16, 128, NCH * 128], F32, kind="ExternalInput")
    wpk = DT("wpk", [8, 128, NCH * 128], F32, kind="ExternalInput")
    wpv = DT("wpv", [8, 128, NCH * 128], F32, kind="ExternalInput")
    wod = DT("wod", [NCH, 128, NCH * 128], F32, kind="ExternalInput")
    w1d = DT("w1d", [NCH, 128, NCH * 128], FP8, kind="ExternalInput")
    cosd = DT("cosd", [128, T], F32, kind="ExternalInput")
    sind = DT("sind", [128, T], F32, kind="ExternalInput")
    protd = DT("protd", [128, 128], BF16, kind="ExternalInput")
    maskd = DT("maskd", [128, 8 * 9 * 128], BF16, kind="ExternalInput")
    convd = DT("convd", [1, 32], F32, kind="ExternalInput")
    flagd = DT("flagd", [128, 2], F32, kind="ExternalInput")
    iscl = DT("iscl", [1, 1], BF16, kind="ExternalInput")
    wscl = DT("wscl", [1, 1], F32, kind="ExternalInput")
    kv_out = DT("kv_out", [2, 8, 128, T], BF16)
    kv_gth = DT("kv_gth", [2, 2, 8, 128, T], BF16)

    y = DT("y", [128, NCH * T], F32, kind="ExternalOutput")
    dbg = DT("dbg", [128, 2 * NCH * T], F32, kind="ExternalOutput") if dbg_mode else None

    AR = A("AR", [128, ARENA_BYTES // 2], BF16)

    def BV(off, cols):
        assert off % 2 == 0 and off + 2 * cols <= ARENA_BYTES, (off, cols)
        return AR[:, off // 2: off // 2 + cols]

    def FV(off, cols):
        assert off % 4 == 0 and off + 4 * cols <= ARENA_BYTES, (off, cols)
        return AR[:, off // 2: off // 2 + 2 * cols].bitcast(F32)

    ones_c = A("ones_c", [128, 1], F32)
    ones_cb = A("ones_cb", [128, 1], BF16)
    ones_r = A("ones_r", [1, 128], F32)
    idn = A("idn", [128, 128], BF16)
    biasqk = A("biasqk", [128, 4], F32)
    atm = A("atm", [128, 3 * 128], BF16)
    dnorm = A("dnorm", [128, 2], F32)
    bpt = A("bpt", [128, NCH], F32)
    fsc = A("fsc", [128, 40], F32)

    xn = BV(0, 32768)
    x_acc = FV(65536, 16384)
    XF = FV(65536, 16384)
    probT = BV(65536, 8192)
    kh_t = BV(81920, 4096)
    qh_t = BV(90112, 2048)
    sT = FV(94208, 1024)
    SRQ = FV(98304, 1024)
    SRM = FV(102400, 1024)
    SRK = FV(106496, 1024)
    SRS = FV(110592, 8)
    mub = FV(110624, 1024)
    ksq = BV(114720, 1024)
    qsq = BV(116768, 1024)
    bvb = FV(118816, 2048)
    xw = FV(131072, 2048)
    sq2 = FV(139264, 2048)
    mb = FV(147456, 2048)
    ST_mo = FV(155648, 1024)
    ST_ro = FV(159744, 1024)
    ST_mp = FV(163840, 1024)
    ST_rp = FV(167936, 1024)
    ST_v = FV(172032, 1024)
    att0 = BV(131072, 16384)
    vg_t = BV(163840, 16 * 4 * 129)
    wvg = BV(180352, 8192)
    whead = BV(196736, 4096)
    wpt = BV(163840, 4096)
    hbuf = BV(131072, 16384)
    wgt = BV(163840, 4096)
    wut = BV(172032, 4096)
    wdt = BV(180224, 4096)
    rbc = FV(188416, 1024)
    ffa = FV(192512, 2048)
    ffm = FV(200704, 1024)
    SBC = FV(204800, 1024)
    SBCn = FV(208896, 512)

    pA = P_("pA", [128, 1024], F32)
    pB = P_("pB", [128, 1024], F32)
    pC = P_("pC", [128, 1024], F32)
    pD = P_("pD", [128, 512], F32)
    pT = P_("pT", [128, 1024], BF16)

    p.dma("y", idn[:, :], ident[:, :], inc="xc")
    p.emit("v", lambda e: e.memset(ones_c[:, :], 1.0), inc="vcst")
    p.emit("v", lambda e: e.memset(ones_cb[:, :], 1.0), inc="vcst")
    cst_tok = p.emit("v", lambda e: e.memset(ones_r[:, :], 1.0), inc="vcst")

    def bcast_row(row_ap, out_ap, n, psum, wait, incname):
        mts = None
        for s0 in range(0, n, 512):
            s1 = min(s0 + 512, n)
            mts = p.emit("t", lambda e, o=psum[:, s0:s1], r=row_ap[:, s0:s1]:
                         e.matmul(o, ones_r[0:1, :], r, start=True, stop=True),
                         wait=wait if s0 == 0 else (), inc="mm")
        return p.emit("v", lambda e, o=out_ap[:, 0:n], i=psum[:, 0:n]: e.tensor_copy(o, i),
                      wait=[("mm", mts)], inc=incname)

    def ln0(src_dram, which, start_wait, mrow, rrow):
        dts = []
        chunk_mm = {}
        for ci in range(NCH):
            sem = "xl%d" % (ci % 2)
            w = list(start_wait)
            if ci >= 2:
                w.append(("mm", 0))
            if which == 0 and ci == 0:
                w.append(("vcst", cst_tok))
            w = [x for x in w if x[1] != 0]
            if ci >= 2:
                w.append(("mm", chunk_mm[ci - 2]))
            dts.append(p.dma("y", XF[:, ci * T:(ci + 1) * T],
                             src_dram[128 * ci:128 * (ci + 1), :], inc=sem, wait=w))
            sl = ci % 2
            sv = sq2[:, sl * T:(sl + 1) * T]
            xv = XF[:, ci * T:(ci + 1) * T]
            w = [(sem, p.tok(sem))]
            vt = p.emit("v", lambda e, a=sv, b=xv: e.tensor_mul(a, b, b), wait=w, inc="vln")
            for nh in range(2):
                s0, s1 = nh * 512, (nh + 1) * 512
                w0 = []
                if ci == 0 and nh == 0 and which == 1:
                    w0 = [("vln", ln0.stats_rd)]
                p.emit("t", lambda e, o=pA[0:1, s0:s1], r=xv[:, s0:s1], st=(ci == 0), sp=(ci == NCH - 1):
                       e.matmul(o, ones_c[:, 0:1], r, start=st, stop=sp),
                       wait=w0 + [(sem, p.tok(sem))])
            mt = None
            for nh in range(2):
                s0, s1 = nh * 512, (nh + 1) * 512
                mt = p.emit("t", lambda e, o=pB[0:1, s0:s1], r=sv[:, s0:s1], st=(ci == 0), sp=(ci == NCH - 1):
                            e.matmul(o, ones_c[:, 0:1], r, start=st, stop=sp),
                            wait=[("vln", vt)] if nh == 0 else (), inc="mm")
            chunk_mm[ci] = mt
        mmtok = chunk_mm[NCH - 1]
        vrow = ST_v[0:1, :]
        p.emit("v", lambda e, o=mrow[0:1, :]: e.tensor_scalar(o, pA[0:1, :], 1.0 / D, None, ALU.mult),
               wait=[("mm", mmtok)], inc="vln")
        p.emit("v", lambda e: e.tensor_scalar(vrow, pB[0:1, :], 1.0 / D, EPS, ALU.mult, ALU.add), inc="vln")
        p.emit("v", lambda e, o=rrow[0:1, :], m=mrow[0:1, :]: e.tensor_mul(o, m, m), inc="vln")
        v1 = p.emit("v", lambda e, r=rrow[0:1, :]: e.tensor_sub(vrow, vrow, r), inc="vln")
        s1_ = p.emit("s", lambda e, o=rrow[0:1, :]: e.activation(out=o, in_=vrow, func=AF.Sqrt),
                     wait=[("vln", v1)], inc="sln")
        v2 = p.emit("v", lambda e, r=rrow[0:1, :]: e.reciprocal(r, r), wait=[("sln", s1_)], inc="vln")
        ln0.stats_rd = p.tok("vln")
        b1 = bcast_row(mrow[0:1, :], mb[:, 0:T], T, pC, [("vln", v2)], "vln")
        b2 = bcast_row(rrow[0:1, :], mb[:, T:2 * T], T, pC, [("vln", b1)], "vln")
        last = None
        for ci in range(NCH):
            xv = XF[:, ci * T:(ci + 1) * T]
            xc = xn[:, ci * 2 * T + which * T: ci * 2 * T + (which + 1) * T]
            p.emit("v", lambda e, a=xv: e.tensor_sub(a, a, mb[:, 0:T]),
                   wait=[("vln", b2)] if ci == 0 else (), inc="vln")
            last = p.emit("v", lambda e, o=xc, a=xv: e.tensor_mul(o, a, mb[:, T:2 * T]), inc="vln")
        return last, ln0.stats_rd

    ln_own, stats_rd0 = ln0(x_own, 0, [], ST_mo, ST_ro)
    ln_part, _ = ln0(x_part, 1, [("vln", ln_own)], ST_mp, ST_rp)
    xn_ready = ln_part

    def dump_dbg(get_src, n_tiles, wait_sem, wait_tok, bounce=None):
        bounce = bounce if bounce is not None else sT
        for i in range(n_tiles):
            src = get_src(i)
            tk2 = p.emit("v", lambda e, o=bounce[:, 0:512], i_=src[:, 0:512]: e.tensor_copy(o, i_),
                         wait=[(wait_sem, wait_tok), ("st", p.tok("st"))], inc="vdbg")
            p.dma("y", dbg[:, i * T:i * T + 512], bounce[:, 0:512], inc="st", wait=[("vdbg", tk2)])
            tk3 = p.emit("v", lambda e, o=bounce[:, 0:512], i_=src[:, 512:1024]: e.tensor_copy(o, i_),
                         wait=[("st", p.tok("st"))], inc="vdbg")
            p.dma("y", dbg[:, i * T + 512:(i + 1) * T], bounce[:, 0:512], inc="st", wait=[("vdbg", tk3)])
        p.emit("y", None, wait=[("st", p.tok("st"))])

    if dbg_mode == "xn0":
        dump_dbg(lambda i: xn[:, (i // 2) * 2 * T + (i % 2) * T:(i // 2) * 2 * T + (i % 2 + 1) * T],
                 32, "vln", xn_ready)
        p.replay()
        return nc

    tkbv = p.dma("y", SRQ[0:1, :], bv0[0:1, 0:1024], inc="xc", wait=[("vln", xn_ready)])
    tkbv2 = p.dma("y", SRK[0:1, :], bv0[0:1, 1024:2048], inc="xc")
    b1 = bcast_row(SRQ[0:1, :], bvb[:, 0:1024], 1024, pC, [("xc", p.tok("xc")), ("vln", xn_ready)], "vln")
    vb_ready = bcast_row(SRK[0:1, :], bvb[:, 1024:2048], 1024, pC, [("vln", b1)], "vln")
    vg4 = vg_t.rearrange("p (t h d) -> p t h d", t=16, h=4)
    cst2 = p.emit("v", lambda e: e.memset(vg4[:, :, :, 128:129], 1.0), inc="vat")

    vgrp_evac = {}
    vglast = {}

    def emit_vgroup(g):
        wtoks = []
        for fc in range(NCH):
            dst = wvg[:, fc * 512:(fc + 1) * 512]
            w = [("mm", vglast[g - 1])] if fc == 0 and g >= 1 else ()
            wtoks.append(p.dma("g", dst, wv0[g, fc, :, :], inc="wv", wait=w))
        ev_last = None
        for tc in range(16):
            which, tloc = tc // 8, tc % 8
            for fc in range(NCH):
                lhs = xn[:, fc * 2 * T + which * T + tloc * 128: fc * 2 * T + which * T + (tloc + 1) * 128]
                rhs = wvg[:, fc * 512:(fc + 1) * 512]
                w = []
                if fc == 0:
                    w = [("wv", p.tok("wv")), ("vat", p.tok("vat"))]
                    if g == 0 and tc == 0:
                        w.append(("vln", vb_ready))
                p.emit("t", lambda e, o=pC[:, 0:512], l=lhs, r=rhs, st=(fc == 0), sp=(fc == NCH - 1):
                       e.matmul(o, l, r, start=st, stop=sp),
                       wait=w, inc="mm" if fc == NCH - 1 else None)
            mmt = p.tok("mm")
            vg_view = vg4[:, tc, :, 0:128]
            ps_view = pC.ap().rearrange("p (h d) -> p h d", h=8)[:, 0:4, :]
            bb_view = bvb.rearrange("p (h d) -> p h d", h=NCH)[:, 4 * g:4 * g + 4, :]
            ev_last = p.emit("v", lambda e, o=vg_view, i=ps_view, b=bb_view: e.tensor_add(o, i, b),
                             wait=[("mm", mmt)], inc="vat")
            vglast[g] = mmt
        vgrp_evac[g] = ev_last

    att_fin = None
    qlast = {}
    klast = {}
    import os
    NHRUN = int(os.environ.get('NHRUN', str(NH0)))
    for h in range(1 if dbg_mode == 'qk0' else NHRUN):
        if h % 4 == 0:
            emit_vgroup(h // 4)
        hs = h % 2
        wsb = hs * NCH * 128

        wk_tok = p.dma("g", whead[:, wsb:wsb + NCH * 128], wk0[h, :, :], inc="wk",
                       wait=[("mm", qlast[h - 2])] if h >= 2 else ())
        bk_tok = p.dma("g", biasqk[:, hs * 2 + 1:hs * 2 + 2],
                       bk0[h:h + 1, :].rearrange("a b -> b a"), inc="wbk",
                       wait=[("vqk", p.tok("vqk"))] if h >= 1 else ())
        kevs = []
        for half2 in range(2):
            mmt = None
            for fc in range(NCH):
                lhs = whead[:, wsb + fc * 128:wsb + (fc + 1) * 128]
                for nb in range(2):
                    rhs = xn[:, fc * 2 * T + half2 * T + nb * 512: fc * 2 * T + half2 * T + nb * 512 + 512]
                    w = []
                    if fc == 0 and nb == 0:
                        w = [("wk", p.tok("wk")), ("vqk", p.tok("vqk")), ("vsc", p.tok("vsc"))]
                        if h == 0 and half2 == 0:
                            w.append(("vln", xn_ready))
                    mmt = p.emit("t", lambda e, o=pA[:, nb * 512:nb * 512 + 512], l=lhs, r=rhs,
                                 st=(fc == 0), sp=(fc == NCH - 1):
                                 e.matmul(o, l, r, start=st, stop=sp),
                                 wait=w, inc="mm" if fc == NCH - 1 else None)
            kdst = kh_t[:, hs * 2 * T + half2 * T: hs * 2 * T + (half2 + 1) * T]
            kev = p.emit("v", lambda e, o=kdst, b=biasqk[:, hs * 2 + 1:hs * 2 + 2]:
                         e.tensor_scalar(o, pA[:, :], b, SCH, ALU.add, ALU.mult),
                         wait=[("mm", mmt), ("wbk", p.tok("wbk"))], inc="vqk")
            kevs.append(kev)
            klast[h] = mmt

        if h >= 1 and os.environ.get('STOPAT') == 'k':
            continue
        wq_tok = p.dma("g", whead[:, wsb:wsb + NCH * 128], wq0[h, :, :], inc="wq",
                       wait=[("mm", klast[h])])
        bq_tok = p.dma("g", biasqk[:, hs * 2:hs * 2 + 1],
                       bq0[h:h + 1, :].rearrange("a b -> b a"), inc="wbq",
                       wait=[("vqk", p.tok("vqk"))] if h >= 1 else ())
        mmq = None
        for fc in range(NCH):
            lhs = whead[:, wsb + fc * 128:wsb + (fc + 1) * 128]
            for nb in range(2):
                rhs = xn[:, fc * 2 * T + nb * 512: fc * 2 * T + nb * 512 + 512]
                w = []
                if fc == 0 and nb == 0:
                    w = [("wq", p.tok("wq")), ("vqk", p.tok("vqk")), ("vsc", p.tok("vsc")),
                         ("vst2", p.tok("vst2"))]
                mmq = p.emit("t", lambda e, o=pB[:, nb * 512:nb * 512 + 512], l=lhs, r=rhs,
                             st=(fc == 0), sp=(fc == NCH - 1):
                             e.matmul(o, l, r, start=st, stop=sp),
                             wait=w, inc="mm" if fc == NCH - 1 else None)
        qdst = qh_t[:, hs * T:(hs + 1) * T]
        qlast[h] = mmq
        vq = p.emit("v", lambda e, o=qdst, b=biasqk[:, hs * 2:hs * 2 + 1]:
                    e.tensor_scalar(o, pB[:, :], b, SCH, ALU.add, ALU.mult),
                    wait=[("mm", mmq), ("wbq", p.tok("wbq"))], inc="vqk")

        hh = h % 4
        ksrc = kh_t[:, hs * 2 * T:(hs + 1) * 2 * T]
        for qhf in range(2):
            qsl = qdst[:, qhf * 512:(qhf + 1) * 512]
            exp_last = None
            for kc in range(16):
                ps_ = [pA[:, 0:512], pA[:, 512:1024], pB[:, 0:512], pB[:, 512:1024]][kc % 4]
                lhs = ksrc[:, kc * 128:(kc + 1) * 128]
                w = [("sex", p.tok("sex"))]
                if kc == 0:
                    w.append(("vqk", vq))
                mm_last = p.emit("t", lambda e, o=ps_, l=lhs, r=qsl:
                                 e.matmul(o, l, r, start=True, stop=True),
                                 wait=w, inc="mm")
                exp_last = p.emit("s", lambda e, o=probT[:, kc * 512:(kc + 1) * 512], i=ps_:
                                  e.activation(out=o, in_=i, func=AF.Exp),
                                  wait=[("mm", mm_last)], inc="sex")
            for qc4 in range(4):
                qc = qhf * 4 + qc4
                mmt = None
                for kc in range(16):
                    lhs = probT[:, kc * 512 + qc4 * 128: kc * 512 + (qc4 + 1) * 128]
                    rhs = vg4[:, kc, hh, :]
                    w = []
                    if kc == 0:
                        w = [("sex", exp_last), ("vat", vgrp_evac[h // 4]), ("vpv", p.tok("vpv"))]
                    mmt = p.emit("t", lambda e, o=pD[:, 0:129], l=lhs, r=rhs, st=(kc == 0), sp=(kc == 15):
                                 e.matmul(o, l, r, start=st, stop=sp),
                                 wait=w, inc="mm" if kc == 15 else None)
                qs_ = qc % 2
                asl = atm[:, qs_ * 128:(qs_ + 1) * 128]
                vr = p.emit("v", lambda e, d=dnorm[:, qs_:qs_ + 1]: e.reciprocal(d, pD[:, 128:129]),
                            wait=[("mm", mmt)], inc="vpv")
                vn = p.emit("v", lambda e, o=asl, d=dnorm[:, qs_:qs_ + 1]:
                            e.tensor_scalar(o, pD[:, 0:128], d, None, ALU.mult),
                            wait=[("vpv", vr), ("sat", p.tok("sat"))], inc="vpv")
                trs = pT[:, (qc % 4) * 128:((qc % 4) + 1) * 128]
                w_tr = [("vpv", vn), ("sat", p.tok("sat"))]
                if h == 0 and qc == 0:
                    w_tr.append(("xc", 16))
                mtt = p.emit("t", lambda e, o=trs, i=asl: e.transpose(o, i, idn[:, :]),
                             wait=w_tr, inc="mm")
                att_fin = p.emit("v", lambda e, o=att0[:, h * T + qc * 128: h * T + (qc + 1) * 128],
                                 i=trs: e.tensor_copy(o, i), wait=[("mm", mtt)], inc="sat")

    if dbg_mode == "qk0":
        tiles = []
        tiles.append((qh_t[:, 0:T], "vqk"))
        tiles.append((kh_t[:, 0:T], "vqk"))
        tiles.append((kh_t[:, T:2 * T], "vqk"))
        tiles.append((mub[:, 0:T], "vst2"))
        vgh0 = vg4[:, :, 0, 0:128]
        fin_tok = {"vqk": p.tok("vqk"), "vst2": p.tok("vst2"), "sex": p.tok("sex"), "vat": p.tok("vat"),
                   "vpv": p.tok("vpv"), "sat": p.tok("sat")}
        idx = 0
        def put(src_ap, sem):
            nonlocal idx
            tk2 = p.emit("v", lambda e, o=sT[:, 0:512], i_=src_ap: e.tensor_copy(o, i_),
                         wait=[(sem, fin_tok[sem]), ("st", p.tok("st"))], inc="vdbg")
            p.dma("y", dbg[:, idx * 512:(idx + 1) * 512], sT[:, 0:512], inc="st", wait=[("vdbg", tk2)])
            idx += 1
        for (ap, sem) in tiles:
            put(ap[:, 0:512], sem)
            put(ap[:, 512:1024], sem)
        for tc2 in range(16):
            src_ap = vgh0[:, tc2, :]
            tk2 = p.emit("v", lambda e, o=sT[:, 0:128], i_=src_ap: e.tensor_copy(o, i_),
                         wait=[("vat", fin_tok["vat"]), ("st", p.tok("st"))], inc="vdbg")
            p.dma("y", dbg[:, idx * 512 + (tc2 % 4) * 128: idx * 512 + (tc2 % 4) * 128 + 128],
                  sT[:, 0:128], inc="st", wait=[("vdbg", tk2)])
            if tc2 % 4 == 3:
                idx += 1
        for kc in range(16):
            put(probT[:, kc * 512:(kc + 1) * 512], "sex")
        for i2 in range(2):
            put(att0[:, i2 * 512:(i2 + 1) * 512], "sat")
        mmt2 = None
        for fc in range(NCH):
            lhs = xn[:, fc * 2 * T + 0 * T + 0 * 128: fc * 2 * T + 0 * T + 128]
            mmt2 = p.emit("t", lambda e, l=lhs, r=wvg[:, fc * 512:(fc + 1) * 512],
                          st_=(fc == 0), sp=(fc == NCH - 1):
                          e.matmul(pD[:, 0:512], l, r, start=st_, stop=sp),
                          wait=[("st", p.tok("st")), ("vpv", p.tok("vpv"))] if fc == 0 else (),
                          inc="mm" if fc == NCH - 1 else None)
        tkr = p.emit("v", lambda e: e.tensor_copy(sT[:, 0:512], pD[:, 0:512]),
                     wait=[("mm", mmt2)], inc="vdbg")
        p.dma("y", dbg[:, idx * 512:(idx + 1) * 512], sT[:, 0:512], inc="st", wait=[("vdbg", tkr)])
        idx += 1
        p.emit("y", None, wait=[("st", p.tok("st"))])
        p.replay()
        return nc

    if dbg_mode == "attnraw":
        dump_dbg(lambda i: att0[:, i * T:(i + 1) * T], NCH, "sat", att_fin, bounce=sT)
        p.replay()
        return nc

    xacc_tks = []
    for ci in range(NCH):
        xacc_tks.append(p.dma("y", x_acc[:, ci * T:(ci + 1) * T],
                              x_own[128 * ci:128 * (ci + 1), :], inc="xc",
                              wait=[("sat", att_fin)]))
    bptk = p.dma("g", bpt[:, :], bp0[:, :].rearrange("c b -> b c"), inc="wpb",
                 wait=[("sat", att_fin)], slow=True)

    proj_last = None
    plast = {}
    for oc in range(NCH):
        osl = (oc % 2) * NCH * 128
        wtk = p.dma("g", wpt[:, osl:osl + NCH * 128], wp0[oc, :, :], inc="wp",
                    wait=[("mm", plast[oc - 2])] if oc >= 2 else [("sat", att_fin)])
        mmt = None
        for ic in range(NCH):
            lhs = wpt[:, osl + ic * 128:osl + (ic + 1) * 128]
            for nb in range(2):
                rhs = att0[:, ic * T + nb * 512: ic * T + nb * 512 + 512]
                w = []
                if ic == 0 and nb == 0:
                    w = [("wp", p.tok("wp")), ("vac", p.tok("vac")), ("vsc", p.tok("vsc"))]
                    if oc == 0:
                        w.append(("sat", att_fin))
                mmt = p.emit("t", lambda e, o=pA[:, nb * 512:nb * 512 + 512], l=lhs, r=rhs,
                             st=(ic == 0), sp=(ic == NCH - 1):
                             e.matmul(o, l, r, start=st, stop=sp),
                             wait=w, inc="mm" if ic == NCH - 1 else None)
        plast[oc] = mmt
        xa = x_acc[:, oc * T:(oc + 1) * T]
        w = [("mm", mmt), ("xc", p.tok("xc"))]
        if oc == 0:
            w.append(("wpb", bptk))
        proj_last = p.emit("v", lambda e, o=xa, b=bpt[:, oc:oc + 1]:
                           e.scalar_tensor_tensor(o, pA[:, :], b, o, ALU.add, ALU.add),
                           wait=w, inc="vac")

    if dbg_mode == "attn0":
        dump_dbg(lambda i: x_acc[:, i * T:(i + 1) * T], NCH, "vac", proj_last, bounce=ffa)
        p.replay()
        return nc

    rrowf = rbc[0:1, 0:1024]
    for nb in range(2):
        mmt = None
        for ci in range(NCH):
            rhs = xn[:, ci * 2 * T + nb * 512: ci * 2 * T + nb * 512 + 512]
            w = []
            if ci == 0:
                w = [("vr", p.tok("vr")), ("vpv", p.tok("vpv")), ("vst2", p.tok("vst2"))]
            mmt = p.emit("t", lambda e, r=rhs, st=(ci == 0), sp=(ci == NCH - 1):
                         e.matmul(pD[0:1, 0:512], ones_cb[:, 0:1], r, start=st, stop=sp),
                         wait=w, inc="mm" if ci == NCH - 1 else None)
        p.emit("v", lambda e, o=rbc[0:1, nb * 512:nb * 512 + 512]:
               e.tensor_copy(o, pD[0:1, 0:512]), wait=[("mm", mmt)], inc="vr")
    v128 = p.emit("v", lambda e: e.tensor_scalar(rrowf, rrowf, 128.0, None, ALU.mult),
                  wait=[("vr", p.tok("vr"))], inc="vr")
    brt = bcast_row(rrowf, rbc, T, pB, [("vr", v128), ("vsc", p.tok("vsc"))], "vr")
    dst_tk = p.dma("g", fsc[:, 4:4 + NCH], dsc[:, :].rearrange("c b -> b c"), inc="wds",
                   wait=[("sat", att_fin)], slow=True)
    nds_tok = p.emit("v", lambda e: e.tensor_scalar(fsc[:, 4 + NCH:4 + 2 * NCH],
                                                    fsc[:, 4:4 + NCH], -128.0, None, ALU.mult),
                     wait=[("wds", dst_tk)], inc="vr")

    dv = None
    glastd = {}
    ulastd = {}
    dlastd = {}
    for grp in range(4):
        hlast = None
        for il in range(16):
            ic = grp * 16 + il
            wslot = ic % 2
            wgt_d = wgt[:, wslot * NCH * 128 // 2:wslot * NCH * 128 // 2 + NCH * 128]
            wut_d = wut[:, wslot * NCH * 128 // 2:wslot * NCH * 128 // 2 + NCH * 128]
            wgt_d = wgt[:, wslot * 2048:wslot * 2048 + 2048]
            wut_d = wut[:, wslot * 2048:wslot * 2048 + 2048]
            wgk = p.dma("g", wgt_d, wgd[ic, :, :], inc="wg",
                        wait=[("mm", glastd[ic - 2])] if ic >= 2 else [("vac", p.tok("vac"))])
            wuk = p.dma("g", wut_d, wud[ic, :, :], inc="wu",
                        wait=[("mm", ulastd[ic - 2])] if ic >= 2 else [("vac", p.tok("vac"))])
            sctk = p.dma("g", fsc[:, wslot * 2:wslot * 2 + 1],
                         gsc[ic:ic + 1, :].rearrange("a b -> b a"), inc="wsg",
                         wait=[("vff", p.tok("vff"))])
            sctk2 = p.dma("g", fsc[:, wslot * 2 + 1:wslot * 2 + 2],
                          usc[ic:ic + 1, :].rearrange("a b -> b a"), inc="wsu",
                          wait=[("vff", p.tok("vff"))])
            mg = mu = None
            for fc in range(NCH):
                lg = wgt_d[:, fc * 128:(fc + 1) * 128]
                for nb in range(2):
                    rhs = xn[:, fc * 2 * T + nb * 512: fc * 2 * T + nb * 512 + 512]
                    w = []
                    if fc == 0 and nb == 0:
                        w = [("wg", p.tok("wg")), ("vff", p.tok("vff")), ("vdn", p.tok("vdn")),
                             ("vr", p.tok("vr"))]
                        if ic == 0:
                            w.append(("vr", brt))
                            w.append(("vac", p.tok("vac")))
                    mg = p.emit("t", lambda e, o=pA[:, nb * 512:nb * 512 + 512], l=lg, r=rhs,
                                st=(fc == 0), sp=(fc == NCH - 1):
                                e.matmul(o, l, r, start=st, stop=sp),
                                wait=w, inc="mm" if fc == NCH - 1 else None)
            for fc in range(NCH):
                lu = wut_d[:, fc * 128:(fc + 1) * 128]
                for nb in range(2):
                    rhs = xn[:, fc * 2 * T + nb * 512: fc * 2 * T + nb * 512 + 512]
                    w = [("wu", p.tok("wu"))] if (fc == 0 and nb == 0) else ()
                    mu = p.emit("t", lambda e, o=pB[:, nb * 512:nb * 512 + 512], l=lu, r=rhs,
                                st=(fc == 0), sp=(fc == NCH - 1):
                                e.matmul(o, l, r, start=st, stop=sp),
                                wait=w, inc="mm" if fc == NCH - 1 else None)
            glastd[ic] = mg
            ulastd[ic] = mu
            ag = ffa[:, 0:T]
            au = ffa[:, T:2 * T]
            v1 = p.emit("v", lambda e, o=ag: e.tensor_sub(o, pA[:, :], rbc[:, :]),
                        wait=[("mm", mg)], inc="vff")
            v2 = p.emit("v", lambda e, o=au: e.tensor_sub(o, pB[:, :], rbc[:, :]),
                        wait=[("mm", mu)], inc="vff")
            s1_ = p.emit("s", lambda e, o=ffm[:, :], i=ag, sc=fsc[:, wslot * 2:wslot * 2 + 1]:
                         e.activation(out=o, in_=i, func=AF.Sigmoid, scale=sc),
                         wait=[("vff", v1), ("wsg", p.tok("wsg"))], inc="sff")
            p.emit("v", lambda e: e.tensor_mul(ffa[:, 0:T], ffa[:, 0:T], ffm[:, :]),
                   wait=[("sff", s1_)], inc="vff")
            v4 = p.emit("v", lambda e: e.tensor_mul(ffa[:, 0:T], ffa[:, 0:T], ffa[:, T:2 * T]),
                        inc="vff")
            v5 = p.emit("v", lambda e, a=wslot: e.tensor_mul(fsc[:, a * 2:a * 2 + 1],
                                                             fsc[:, a * 2:a * 2 + 1],
                                                             fsc[:, a * 2 + 1:a * 2 + 2]),
                        wait=[("wsu", p.tok("wsu")), ("sff", s1_)], inc="vff")
            hlast = p.emit("v", lambda e, o=hbuf[:, il * T:(il + 1) * T], a=wslot:
                           e.tensor_scalar(o, ffa[:, 0:T], fsc[:, a * 2:a * 2 + 1], None, ALU.mult),
                           wait=[("vff", v4), ("vff", v5)], inc="vff")
        for nb in range(2):
            mmt = None
            for il in range(16):
                rhs = hbuf[:, il * T + nb * 512: il * T + nb * 512 + 512]
                w = []
                if il == 0:
                    w = [("vff", hlast), ("vr", p.tok("vr"))]
                mmt = p.emit("t", lambda e, o=pB[0:1, nb * 512:nb * 512 + 512], r=rhs,
                             st=(il == 0), sp=(il == 15):
                             e.matmul(o, ones_cb[:, 0:1], r, start=st, stop=sp),
                             wait=w, inc="mm" if il == 15 else None)
            p.emit("v", lambda e, nb=nb: e.tensor_copy(SBC[0:1, nb * 512:nb * 512 + 512],
                                                       pB[0:1, nb * 512:nb * 512 + 512]),
                   wait=[("mm", mmt)], inc="vr")
        sbc_tok = bcast_row(SBC[0:1, :], SBC, T, pB, [("vr", p.tok("vr"))], "vr")
        for hc in range(NCH):
            wslot = hc % 2
            wdt_d = wdt[:, wslot * 2048:wslot * 2048 + 2048]
            wdk = p.dma("g", wdt_d, wdd[grp, hc, :, :], inc="wd",
                        wait=[("mm", dlastd[(grp, hc - 2)])] if hc >= 2 else (
                            [("mm", dlastd[(grp - 1, NCH - 1)])] if grp >= 1 else [("vac", p.tok("vac"))]))
            ps = pC if hc % 2 == 0 else pA
            mmt = None
            for il in range(16):
                lhs = wdt_d[:, il * 128:(il + 1) * 128]
                for nb in range(2):
                    rhs = hbuf[:, il * T + nb * 512: il * T + nb * 512 + 512]
                    w = []
                    if il == 0 and nb == 0:
                        w = [("wd", p.tok("wd")), ("vff", hlast), ("vdn", p.tok("vdn"))]
                        if grp == 0 and hc <= 1:
                            w.append(("vat", p.tok("vat")))
                    mmt = p.emit("t", lambda e, o=ps[:, nb * 512:nb * 512 + 512], l=lhs, r=rhs,
                                 st=(il == 0), sp=(il == 15):
                                 e.matmul(o, l, r, start=st, stop=sp),
                                 wait=w, inc="mm" if il == 15 else None)
            dlastd[(grp, hc)] = mmt
            xa = x_acc[:, hc * T:(hc + 1) * T]
            w = [("mm", mmt), ("vac", proj_last if grp == 0 else 0)]
            if grp == 0 and hc == 0:
                w.append(("wds", dst_tk))
            dv = p.emit("v", lambda e, o=xa, s=fsc[:, 4 + hc:4 + hc + 1], ps_=ps:
                        e.scalar_tensor_tensor(o, ps_[:, :], s, o, ALU.mult, ALU.add),
                        wait=w, inc="vdn")
            dv = p.emit("v", lambda e, o=xa, s=fsc[:, 4 + NCH + hc:4 + NCH + hc + 1]:
                        e.scalar_tensor_tensor(o, SBC[:, :], s, o, ALU.mult, ALU.add),
                        wait=[("vr", sbc_tok), ("vr", nds_tok)], inc="vdn")


    xn1 = BV(0, 16384)
    qx = BV(32768, 16384)
    QS0 = FV(131072, 1024)
    QS1 = FV(135168, 1024)
    QS2 = FV(139264, 1024)
    QS3 = FV(143360, 1024)
    KVT = BV(147456, 2048)
    CS1 = FV(155648, 2048)
    WPS = BV(163840, 4096)
    W1S = BV(172032, 4096)
    LR_m = FV(180224, 1024)
    LR_r = FV(184320, 1024)
    LR_v = FV(188416, 1024)
    MB1 = FV(192512, 2048)
    SCL = FV(200704, 16)
    RSB = FV(200768, 1024)
    protv = atm[:, 256:384]
    isb = atm[:, 384:385] if False else None

    tokP3 = [("vdn", p.tok("vdn")), ("vr", p.tok("vr")), ("vac", p.tok("vac"))]
    p.dma("y", protv, protd[:, :], inc="xc", wait=tokP3)
    p.dma("y", CS1[:, 0:T], cosd[:, :], inc="xc")
    p.dma("y", CS1[:, T:2 * T], sind[:, :], inc="xc")
    p.dma("y", fsc[:, 2:3], flagd[:, 0:1], inc="xc", slow=True)
    p.dma("y", fsc[:, 3:4], flagd[:, 1:2], inc="xc", slow=True)
    isld = p.dma("y", mb[0:1, 0:1].bitcast(BF16)[0:1, 0:1] if False else qh_t[0:1, 0:1], iscl[0:1, :], inc="xc")
    wsld = p.dma("y", SCL[0:1, 1:2], wscl[0:1, :], inc="xc")

    prev_mm = None
    for ci in range(NCH):
        xv = x_acc[:, ci * T:(ci + 1) * T]
        w0 = list(tokP3) if ci == 0 else ([("mm", prev_mm)] if prev_mm else [])
        vt = p.emit("v", lambda e, a=QS0[:, :], b=xv: e.tensor_mul(a, b, b), wait=w0, inc="vl1")
        for nh in range(2):
            s0, s1 = nh * 512, (nh + 1) * 512
            p.emit("t", lambda e, o=pA[0:1, s0:s1], r=xv[:, s0:s1], st=(ci == 0), sp=(ci == NCH - 1):
                   e.matmul(o, ones_c[:, 0:1], r, start=st, stop=sp),
                   wait=[("vl1", vt)] if nh == 0 else ())
        mt = None
        for nh in range(2):
            s0, s1 = nh * 512, (nh + 1) * 512
            mt = p.emit("t", lambda e, o=pB[0:1, s0:s1], r=QS0[:, s0:s1], st=(ci == 0), sp=(ci == NCH - 1):
                        e.matmul(o, ones_c[:, 0:1], r, start=st, stop=sp),
                        wait=[("vl1", vt)] if nh == 0 else (), inc="mm")
        prev_mm = mt
    p.emit("v", lambda e: e.tensor_scalar(LR_m[0:1, :], pA[0:1, :], 1.0 / D, None, ALU.mult),
           wait=[("mm", prev_mm)], inc="vl1")
    p.emit("v", lambda e: e.tensor_scalar(LR_v[0:1, :], pB[0:1, :], 1.0 / D, EPS, ALU.mult, ALU.add), inc="vl1")
    p.emit("v", lambda e: e.tensor_mul(LR_r[0:1, :], LR_m[0:1, :], LR_m[0:1, :]), inc="vl1")
    v1 = p.emit("v", lambda e: e.tensor_sub(LR_v[0:1, :], LR_v[0:1, :], LR_r[0:1, :]), inc="vl1")
    s1_ = p.emit("s", lambda e: e.activation(out=LR_r[0:1, :], in_=LR_v[0:1, :], func=AF.Sqrt),
                 wait=[("vl1", v1)], inc="sl1")
    v2 = p.emit("v", lambda e: e.reciprocal(LR_r[0:1, :], LR_r[0:1, :]), wait=[("sl1", s1_)], inc="vl1")
    vcv = p.emit("v", lambda e: e.tensor_copy(SCL[0:1, 0:1], qh_t[0:1, 0:1]),
                 wait=[("xc", p.tok("xc"))], inc="vl1")
    vinv = p.emit("v", lambda e: e.reciprocal(SCL[0:1, 2:3], SCL[0:1, 0:1]), wait=[("vl1", vcv)], inc="vl1")
    vsw = p.emit("v", lambda e: e.tensor_mul(SCL[0:1, 3:4], SCL[0:1, 0:1], SCL[0:1, 1:2]),
                 wait=[("vl1", vcv)], inc="vl1")
    vrsi = p.emit("v", lambda e: e.tensor_scalar(LR_v[0:1, 0:T], LR_r[0:1, :], SCL[0:1, 2:3], None, ALU.mult),
                  wait=[("vl1", v2), ("vl1", vinv)], inc="vl1")
    b1 = bcast_row(LR_m[0:1, :], MB1[:, 0:T], T, pC, [("vl1", v2)], "vl1")
    b2 = bcast_row(LR_r[0:1, :], MB1[:, T:2 * T], T, pC, [("vl1", b1)], "vl1")
    rsb_tok = bcast_row(LR_v[0:1, 0:T], RSB, T, pC, [("vl1", vrsi), ("vl1", b2)], "vl1")
    cv2 = p.dma("y", LR_m[0:1, 0:32], convd[0:1, :], inc="xc", wait=[("vl1", b2)])
    bcv = bcast_row(LR_m[0:1, 0:32], fsc[:, 4:36], 32, pD, [("xc", p.tok("xc"))], "vl1")
    msw = p.emit("t", lambda e: e.matmul(pD[:, 0:1], ones_r[0:1, :], SCL[0:1, 3:4], start=True, stop=True),
                 wait=[("vl1", vsw), ("vl1", bcv)], inc="mm")
    bsw = p.emit("v", lambda e: e.tensor_copy(fsc[:, 0:1], pD[:, 0:1]), wait=[("mm", msw)], inc="vl1")

    QC = float(2 ** 20 + 1)
    QG = float(3 * 2 ** 13)
    QTH = float((1.5 * 2.0 ** -6) ** 2)
    qx_done = xn1_done = None
    for ci in range(NCH):
        xv = x_acc[:, ci * T:(ci + 1) * T]
        p.emit("v", lambda e, a=xv: e.tensor_sub(QS0[:, :], a, MB1[:, 0:T]),
               wait=[("vl1", rsb_tok), ("sl1", p.tok("sl1"))] if ci == 0 else (), inc="vl1")
        xn1_done = p.emit("v", lambda e, o=xn1[:, ci * T:(ci + 1) * T]:
                          e.tensor_mul(o, QS0[:, :], MB1[:, T:2 * T]), inc="vl1")
        p.emit("v", lambda e: e.tensor_mul(QS0[:, :], QS0[:, :], RSB[:, :]), inc="vl1")
        p.emit("v", lambda e: e.tensor_scalar(QS0[:, :], QS0[:, :], -448.0, 448.0, ALU.max, ALU.min), inc="vl1")
        p.emit("v", lambda e: e.tensor_scalar(QS1[:, :], QS0[:, :], QG, QG, ALU.add, ALU.subtract), inc="vl1")
        st_ = p.emit("s", lambda e: e.mul(QS2[:, :], QS0[:, :], QC), wait=[("vl1", p.tok("vl1"))], inc="sl1")
        p.emit("v", lambda e: e.tensor_sub(QS3[:, :], QS2[:, :], QS0[:, :]), wait=[("sl1", st_)], inc="vl1")
        p.emit("v", lambda e: e.tensor_sub(QS2[:, :], QS2[:, :], QS3[:, :]), inc="vl1")
        p.emit("v", lambda e: e.tensor_mul(QS3[:, :], QS0[:, :], QS0[:, :]), inc="vl1")
        p.emit("v", lambda e: e.tensor_scalar(QS3[:, :], QS3[:, :], QTH, 1e30, ALU.subtract, ALU.mult), inc="vl1")
        p.emit("v", lambda e: e.tensor_scalar(QS3[:, :], QS3[:, :], 0.0, 1.0, ALU.max, ALU.min), inc="vl1")
        p.emit("v", lambda e: e.tensor_sub(QS2[:, :], QS2[:, :], QS1[:, :]), inc="vl1")
        p.emit("v", lambda e: e.tensor_mul(QS2[:, :], QS2[:, :], QS3[:, :]), inc="vl1")
        qx_done = p.emit("v", lambda e, o=qx[:, ci * T:(ci + 1) * T]:
                         e.tensor_add(o, QS2[:, :], QS1[:, :]), inc="vl1")

    pub_last = None
    for kv in range(8):
        wsl = (kv % 2) * NCH * 128
        wkt = p.dma("g", WPS[:, wsl:wsl + NCH * 128], wpk[kv, :, :], inc="wk1",
                    wait=[("mm", p.tok("mm"))] if kv >= 1 else [("vl1", qx_done)])
        mmt = None
        for fc in range(NCH):
            lhs = WPS[:, wsl + fc * 128:wsl + (fc + 1) * 128]
            for nb in range(2):
                rhs = xn1[:, fc * T + nb * 512: fc * T + nb * 512 + 512]
                w = []
                if fc == 0 and nb == 0:
                    w = [("wk1", p.tok("wk1")), ("vkv", p.tok("vkv"))]
                    if kv == 0:
                        w.append(("vl1", xn1_done))
                mmt = p.emit("t", lambda e, o=pA[:, nb * 512:nb * 512 + 512], l=lhs, r=rhs,
                             st=(fc == 0), sp=(fc == NCH - 1):
                             e.matmul(o, l, r, start=st, stop=sp),
                             wait=w, inc="mm" if fc == NCH - 1 else None)
        kev = p.emit("v", lambda e: e.tensor_copy(KVT[:, 0:T], pA[:, :]),
                     wait=[("mm", mmt), ("pub", p.tok("pub"))], inc="vkv")
        rt = None
        for nb in range(2):
            rt = p.emit("t", lambda e, o=pB[:, nb * 512:nb * 512 + 512],
                        r=KVT[:, nb * 512:nb * 512 + 512]:
                        e.matmul(o, protv, r, start=True, stop=True),
                        wait=[("vkv", kev), ("vrp", p.tok("vrp"))] if nb == 0 else (), inc="mm")
        r1 = p.emit("v", lambda e: e.tensor_mul(QS0[:, :], KVT[:, 0:T], CS1[:, 0:T]),
                    wait=[("vkv", kev)], inc="vrp")
        r2 = p.emit("v", lambda e: e.tensor_mul(QS1[:, :], pB[:, :], CS1[:, T:2 * T]),
                    wait=[("mm", rt)], inc="vrp")
        r3 = p.emit("v", lambda e: e.tensor_add(KVT[:, T:2 * T], QS0[:, :], QS1[:, :]),
                    wait=[("vrp", r2)], inc="vrp")
        pub_last = p.dma("y", kv_out[0, kv, :, :], KVT[:, T:2 * T], inc="pub", wait=[("vrp", r3)])
        wvt_ = p.dma("g", WPS[:, wsl:wsl + NCH * 128], wpv[kv, :, :], inc="wk1",
                     wait=[("mm", p.tok("mm"))])
        mmt = None
        for fc in range(NCH):
            lhs = WPS[:, wsl + fc * 128:wsl + (fc + 1) * 128]
            for nb in range(2):
                rhs = xn1[:, fc * T + nb * 512: fc * T + nb * 512 + 512]
                w = [("wk1", p.tok("wk1")), ("vkv", p.tok("vkv"))] if (fc == 0 and nb == 0) else ()
                mmt = p.emit("t", lambda e, o=pA[:, nb * 512:nb * 512 + 512], l=lhs, r=rhs,
                             st=(fc == 0), sp=(fc == NCH - 1):
                             e.matmul(o, l, r, start=st, stop=sp),
                             wait=w, inc="mm" if fc == NCH - 1 else None)
        vev = p.emit("v", lambda e: e.tensor_copy(KVT[:, 0:T], pA[:, :]),
                     wait=[("mm", mmt), ("pub", p.tok("pub"))], inc="vkv")
        pub_last = p.dma("y", kv_out[1, kv, :, :], KVT[:, 0:T], inc="pub", wait=[("vkv", vev)])

    cc_tok = p.emit("g", lambda e: e.collective_compute(
        "AllGather", ALU.bypass, replica_groups=[[0, 1], [2, 3], [4, 5], [6, 7]],
        ins=[kv_out.ap().opt()], outs=[kv_gth.ap().opt()]),
        wait=[("pub", p.tok("pub"))], inc="cc")

    ffn1_last = None
    for oc in range(NCH):
        wsl = (oc % 2) * NCH * 128
        wft = p.dma("g", W1S[:, wsl:wsl + NCH * 128], w1d[oc, :, :], inc="wf1",
                    wait=[("mm", p.tok("mm"))] if oc >= 2 else [("vl1", qx_done)])
        ps = pA if oc % 2 == 0 else pB
        mmt = None
        for ic in range(NCH):
            lhs = W1S[:, wsl + ic * 128:wsl + (ic + 1) * 128]
            for nb in range(2):
                rhs = qx[:, ic * T + nb * 512: ic * T + nb * 512 + 512]
                w = []
                if ic == 0 and nb == 0:
                    w = [("wf1", p.tok("wf1")), ("vf1", p.tok("vf1")), ("vkv", p.tok("vkv")),
                         ("vrp", p.tok("vrp"))]
                mmt = p.emit("t", lambda e, o=ps[:, nb * 512:nb * 512 + 512], l=lhs, r=rhs,
                             st=(ic == 0), sp=(ic == NCH - 1):
                             e.matmul(o, l, r, start=st, stop=sp),
                             wait=w, inc="mm" if ic == NCH - 1 else None)
        ffn1_last = p.emit("v", lambda e, o=x_acc[:, oc * T:(oc + 1) * T], ps_=ps:
                           e.scalar_tensor_tensor(o, ps_[:, :], fsc[:, 0:1], o, ALU.mult, ALU.add),
                           wait=[("mm", mmt), ("vl1", bsw)], inc="vf1")

    masks = BV(32768, 9216)
    CS2 = FV(51200, 2048)
    ST2 = BV(59392, 1024)
    SS2 = FV(61440, 1024)
    KB = BV(131072, 8192)
    VB = BV(147456, 8256)
    PT1 = BV(164352, 4608)
    WQS = BV(173568, 4096)
    QCUR = BV(181760, 2048)
    WOS = BV(185856, 4096)
    A1Q = BV(194048, 4096)
    KB4 = KB.rearrange("p (v t) -> p v t", v=4)
    VB4 = VB.rearrange("p (v b c) -> p v b c", v=4, b=16)

    mld = p.dma("y", masks[:, :], maskd[:, :], inc="xc",
                wait=[("vf1", p.tok("vf1"))])
    p.dma("y", CS2[:, 0:T], cosd[:, :], inc="xc")
    csl2 = p.dma("y", CS2[:, T:2 * T], sind[:, :], inc="xc")

    att1_fin = None
    for hf in range(2):
        rb0w = [("cc", cc_tok), ("xc", p.tok("xc")), ("vf1", p.tok("vf1"))]
        if hf == 1:
            rb0w.append(("mm", p.tok("mm")))
            rb0w.append(("vb2", p.tok("vb2")))
        for (reg, base_src) in ((0, 0), (1, 1)):
            dst = KB4[:, :, reg * T:(reg + 1) * T]
            d1 = p.dma("y", dst, kv_gth[base_src, 0, 4 * hf:4 * hf + 4, :, :].rearrange("v p t -> p v t"),
                       inc="xr", wait=rb0w if reg == 0 else ())
            b1_ = p.emit("v", lambda e, o=dst: e.tensor_scalar(o, o, fsc[:, 3:4], None, ALU.mult),
                         wait=[("xr", p.tok("xr"))], inc="vb2")
            for kvl in range(4):
                d2 = p.dma("y", ST2[:, :], kv_gth[1 - base_src, 0, 4 * hf + kvl, :, :], inc="xr",
                           wait=[("vb2", p.tok("vb2"))])
                b1_ = p.emit("v", lambda e, o=KB4[:, kvl, reg * T:(reg + 1) * T]:
                             e.scalar_tensor_tensor(o, ST2[:, :], fsc[:, 2:3], o, ALU.mult, ALU.add),
                             wait=[("xr", p.tok("xr"))], inc="vb2")
        for kvl in range(4):
            kvg = 4 * hf + kvl
            w0c = fsc[:, 4 + 2 * kvg:5 + 2 * kvg]
            w1c = fsc[:, 5 + 2 * kvg:6 + 2 * kvg]
            kown = KB4[:, kvl, 0:T]
            kpar = KB4[:, kvl, T:2 * T]
            bb = p.emit("v", lambda e, o=ST2[:, 0:1], i=KB4[:, kvl, 2 * T - 1:2 * T]:
                        e.tensor_scalar(o, i, fsc[:, 2:3], None, ALU.mult),
                        wait=[("vb2", p.tok("vb2"))], inc="vb2")
            p.emit("v", lambda e, o=ST2[:, 1:T], i=kown[:, 0:T - 1], w=w0c:
                   e.tensor_scalar(o, i, w, None, ALU.mult), inc="vb2")
            p.emit("v", lambda e, o=ST2[:, 0:1], w=w0c:
                   e.tensor_scalar(o, ST2[:, 0:1], w, None, ALU.mult), inc="vb2")
            p.emit("v", lambda e, o=kown, w=w1c:
                   e.scalar_tensor_tensor(o, kown, w, ST2[:, 0:T], ALU.mult, ALU.add), inc="vb2")
            p.emit("v", lambda e, o=ST2[:, 1:T], i=kpar[:, 0:T - 1], w=w0c:
                   e.tensor_scalar(o, i, w, None, ALU.mult), inc="vb2")
            p.emit("v", lambda e, o=ST2[:, 0:1]: e.memset(o, 0.0), inc="vb2")
            p.emit("v", lambda e, o=kpar, w=w1c:
                   e.scalar_tensor_tensor(o, kpar, w, ST2[:, 0:T], ALU.mult, ALU.add), inc="vb2")
        VFM = PT1[:, 0:2048]
        on1 = p.emit("v", lambda e: e.memset(VB4[:, :, :, 128:129], 1.0),
                     wait=[("vb2", p.tok("vb2"))] + ([("mm", p.tok("mm"))] if hf == 1 else []), inc="vb2")
        for kvl in range(4):
            kvg = 4 * hf + kvl
            w0c = fsc[:, 4 + 32 + 2 * kvg - 28:5 + 32 + 2 * kvg - 28] if False else fsc[:, 20 + 2 * kvg:21 + 2 * kvg]
            w1c = fsc[:, 21 + 2 * kvg:22 + 2 * kvg]
            dv1 = p.dma("y", VFM[:, 0:T], kv_gth[0, 1, kvg, :, :], inc="xr",
                        wait=[("mm", p.tok("mm")), ("vb2", p.tok("vb2"))] if kvl == 0 else [("vb2", p.tok("vb2"))])
            p.emit("v", lambda e, o=VFM[:, 0:T]: e.tensor_scalar(o, o, fsc[:, 3:4], None, ALU.mult),
                   wait=[("xr", p.tok("xr"))], inc="vb2")
            dv2 = p.dma("y", ST2[:, :], kv_gth[1, 1, kvg, :, :], inc="xr", wait=[("vb2", p.tok("vb2"))])
            p.emit("v", lambda e, o=VFM[:, 0:T]:
                   e.scalar_tensor_tensor(o, ST2[:, :], fsc[:, 2:3], o, ALU.mult, ALU.add),
                   wait=[("xr", p.tok("xr"))], inc="vb2")
            dv3 = p.dma("y", VFM[:, T:2 * T], kv_gth[1, 1, kvg, :, :], inc="xr", wait=[("vb2", p.tok("vb2"))])
            p.emit("v", lambda e, o=VFM[:, T:2 * T]: e.tensor_scalar(o, o, fsc[:, 3:4], None, ALU.mult),
                   wait=[("xr", p.tok("xr"))], inc="vb2")
            dv4 = p.dma("y", ST2[:, :], kv_gth[0, 1, kvg, :, :], inc="xr", wait=[("vb2", p.tok("vb2"))])
            p.emit("v", lambda e, o=VFM[:, T:2 * T]:
                   e.scalar_tensor_tensor(o, ST2[:, :], fsc[:, 2:3], o, ALU.mult, ALU.add),
                   wait=[("xr", p.tok("xr"))], inc="vb2")
            bb = p.emit("v", lambda e, o=ST2[:, 0:1], i=VFM[:, 2 * T - 1:2 * T]:
                        e.tensor_scalar(o, i, fsc[:, 2:3], None, ALU.mult), inc="vb2")
            p.emit("v", lambda e, o=ST2[:, 1:T], i=VFM[:, 0:T - 1], w=w0c:
                   e.tensor_scalar(o, i, w, None, ALU.mult), inc="vb2")
            p.emit("v", lambda e, o=ST2[:, 0:1], w=w0c:
                   e.tensor_scalar(o, ST2[:, 0:1], w, None, ALU.mult), inc="vb2")
            p.emit("v", lambda e, o=VFM[:, 0:T], w=w1c:
                   e.scalar_tensor_tensor(o, VFM[:, 0:T], w, ST2[:, 0:T], ALU.mult, ALU.add), inc="vb2")
            p.emit("v", lambda e, o=ST2[:, 1:T], i=VFM[:, T:2 * T - 1], w=w0c:
                   e.tensor_scalar(o, i, w, None, ALU.mult), inc="vb2")
            p.emit("v", lambda e, o=ST2[:, 0:1]: e.memset(o, 0.0), inc="vb2")
            vcv_ = p.emit("v", lambda e, o=VFM[:, T:2 * T], w=w1c:
                          e.scalar_tensor_tensor(o, VFM[:, T:2 * T], w, ST2[:, 0:T], ALU.mult, ALU.add),
                          inc="vb2")
            for blk in range(16):
                trs = pT[:, (blk % 4) * 128:((blk % 4) + 1) * 128]
                mtt = p.emit("t", lambda e, o=trs, i=VFM[:, blk * 128:(blk + 1) * 128]:
                             e.transpose(o, i, idn[:, :]),
                             wait=[("vb2", vcv_ if blk == 0 else 0), ("vb2", p.tok("vb2"))], inc="mm")
                p.emit("v", lambda e, o=VB4[:, kvl, blk, 0:128], i=trs: e.tensor_copy(o, i),
                       wait=[("mm", mtt)], inc="vb2")
        for qi in range(8):
            qh = 8 * hf + qi
            kvl = qi // 2
            qsl = (qi % 2) * T
            wqt = p.dma("g", WQS[:, (qi % 2) * NCH * 128:(qi % 2) * NCH * 128 + NCH * 128],
                        wpq[qh, :, :], inc="wq2",
                        wait=[("mm", p.tok("mm"))] if (hf or qi) else [("vf1", p.tok("vf1"))])
            mmq = None
            for fc in range(NCH):
                lhs = WQS[:, (qi % 2) * NCH * 128 + fc * 128:(qi % 2) * NCH * 128 + (fc + 1) * 128]
                for nb in range(2):
                    rhs = xn1[:, fc * T + nb * 512: fc * T + nb * 512 + 512]
                    w = [("wq2", p.tok("wq2")), ("vq2", p.tok("vq2"))] if (fc == 0 and nb == 0) else ()
                    mmq = p.emit("t", lambda e, o=pA[:, nb * 512:nb * 512 + 512], l=lhs, r=rhs,
                                 st=(fc == 0), sp=(fc == NCH - 1):
                                 e.matmul(o, l, r, start=st, stop=sp),
                                 wait=w, inc="mm" if fc == NCH - 1 else None)
            qev = p.emit("v", lambda e, o=QCUR[:, qsl:qsl + T]: e.tensor_copy(o, pA[:, :]),
                         wait=[("mm", mmq)], inc="vq2")
            rt = None
            for nb in range(2):
                rt = p.emit("t", lambda e, o=pB[:, nb * 512:nb * 512 + 512],
                            r=QCUR[:, qsl + nb * 512:qsl + nb * 512 + 512]:
                            e.matmul(o, protv, r, start=True, stop=True),
                            wait=[("vq2", qev), ("vq2", p.tok("vq2"))] if nb == 0 else (), inc="mm")
            p.emit("v", lambda e, o=ST2[:, :], i=QCUR[:, qsl:qsl + T]:
                   e.tensor_mul(o, i, CS2[:, 0:T]), wait=[("vq2", qev)], inc="vq2")
            p.emit("v", lambda e, o=QCUR[:, qsl:qsl + T]: e.tensor_mul(o, pB[:, :], CS2[:, T:2 * T]),
                   wait=[("mm", rt)], inc="vq2")
            qrope = p.emit("v", lambda e, o=QCUR[:, qsl:qsl + T]: e.tensor_add(o, o, ST2[:, :]),
                           inc="vq2")
            for qthf in range(2):
                exp_toks = []
                for q4 in range(4):
                    qt = qthf * 4 + q4
                    mml = {}
                    for l in range(9):
                        s = (qt + l + 8) % 16
                        lhs = KB4[:, kvl, s * 128:(s + 1) * 128]
                        o = pA[:, l * 128:(l + 1) * 128] if l < 8 else pB[:, 0:128]
                        w = []
                        if l == 0:
                            w = [("vq2", qrope), ("vsc2", p.tok("vsc2")), ("vb2", p.tok("vb2"))]
                        mml[l] = p.emit("t", lambda e, o=o, l_=lhs, r=QCUR[:, qsl + qt * 128:qsl + (qt + 1) * 128]:
                                        e.matmul(o, l_, r, start=True, stop=True),
                                        wait=w, inc="mm" if l in (7, 8) else None)
                    mw = masks[:, qt * 1152:qt * 1152 + 1024]
                    mn = masks[:, qt * 1152 + 1024:qt * 1152 + 1152]
                    vs1 = p.emit("v", lambda e, m=mw: e.tensor_add(SS2[:, :], pA[:, :], m),
                                 wait=[("mm", mml[7]), ("sx2", p.tok("sx2"))], inc="vsc2")
                    ex1 = p.emit("s", lambda e, o=PT1[:, q4 * 1152:q4 * 1152 + 1024]:
                                 e.activation(out=o, in_=SS2[:, :], func=AF.Exp),
                                 wait=[("vsc2", vs1)], inc="sx2")
                    vs2 = p.emit("v", lambda e, m=mn: e.tensor_add(SS2[:, 0:128], pB[:, 0:128], m),
                                 wait=[("mm", mml[8]), ("sx2", ex1)], inc="vsc2")
                    ex2 = p.emit("s", lambda e, o=PT1[:, q4 * 1152 + 1024:q4 * 1152 + 1152]:
                                 e.activation(out=o, in_=SS2[:, 0:128], func=AF.Exp),
                                 wait=[("vsc2", vs2)], inc="sx2")
                    exp_toks.append(ex2)
                for q4 in range(4):
                    qt = qthf * 4 + q4
                    mmt = None
                    for l in range(9):
                        s = (qt + l + 8) % 16
                        lhs = PT1[:, q4 * 1152 + l * 128:q4 * 1152 + (l + 1) * 128]
                        rhs = VB4[:, kvl, s, :]
                        w = []
                        if l == 0:
                            w = [("sx2", exp_toks[q4]), ("vpv2", p.tok("vpv2"))]
                        mmt = p.emit("t", lambda e, o=pD[:, 0:129], l_=lhs, r=rhs, st=(l == 0), sp=(l == 8):
                                     e.matmul(o, l_, r, start=st, stop=sp),
                                     wait=w, inc="mm" if l == 8 else None)
                    qs_ = qt % 2
                    vr = p.emit("v", lambda e, d=dnorm[:, qs_:qs_ + 1]: e.reciprocal(d, pD[:, 128:129]),
                                wait=[("mm", mmt)], inc="vpv2")
                    vn = p.emit("v", lambda e, o=atm[:, qs_ * 128:(qs_ + 1) * 128], d=dnorm[:, qs_:qs_ + 1]:
                                e.tensor_scalar(o, pD[:, 0:128], d, None, ALU.mult),
                                wait=[("vpv2", vr), ("va2", p.tok("va2"))], inc="vpv2")
                    trs = pT[:, 512 + (qt % 2) * 128:512 + ((qt % 2) + 1) * 128]
                    mtt = p.emit("t", lambda e, o=trs, i=atm[:, qs_ * 128:(qs_ + 1) * 128]:
                                 e.transpose(o, i, idn[:, :]),
                                 wait=[("vpv2", vn), ("va2", p.tok("va2"))], inc="mm")
                    att1_fin = p.emit("v", lambda e, o=A1Q[:, (qi % 4) * T + qt * 128:(qi % 4) * T + (qt + 1) * 128],
                                      i=trs: e.tensor_copy(o, i), wait=[("mm", mtt)], inc="va2")
            if qi % 4 == 3:
                qtr = qh // 4
                for oc in range(NCH):
                    osl = (oc % 2) * NCH * 128
                    wot = p.dma("g", WOS[:, osl:osl + 512], wod[oc, :, 4 * qtr * 128:(4 * qtr + 4) * 128],
                                inc="wo2", wait=[("mm", p.tok("mm"))] if oc >= 2 else [("va2", att1_fin)])
                    mmt = None
                    for icl in range(4):
                        lhs = WOS[:, osl + icl * 128:osl + (icl + 1) * 128]
                        for nb in range(2):
                            rhs = A1Q[:, icl * T + nb * 512: icl * T + nb * 512 + 512]
                            w = []
                            if icl == 0 and nb == 0:
                                w = [("wo2", p.tok("wo2")), ("vo2", p.tok("vo2")), ("va2", att1_fin)]
                            mmt = p.emit("t", lambda e, o=pC[:, nb * 512:nb * 512 + 512], l=lhs, r=rhs,
                                         st=(icl == 0), sp=(icl == 3):
                                         e.matmul(o, l, r, start=st, stop=sp),
                                         wait=w, inc="mm" if icl == 3 else None)
                    p.emit("v", lambda e, o=x_acc[:, oc * T:(oc + 1) * T]:
                           e.tensor_add(o, o, pC[:, :]),
                           wait=[("mm", mmt), ("vf1", p.tok("vf1"))], inc="vo2")

    for qtr in range(4):
        sl = x_acc[:, qtr * 4 * T:(qtr + 1) * 4 * T]
        p.dma("y", y[:, qtr * 4 * T:(qtr + 1) * 4 * T], sl, inc="st",
              wait=[("vo2", p.tok("vo2")), ("vdn", p.tok("vdn"))])
    p.emit("y", None, wait=[("st", p.tok("st"))])

    p.replay()
    return nc


def _tile_lhsT(w):
    K, M = w.shape
    kb, mb_ = K // 128, M // 128
    t = w.reshape(kb, 128, mb_, 128).transpose(2, 1, 0, 3)
    return np.ascontiguousarray(t.reshape(mb_, 128, kb * 128))


def prep_inputs(inp):
    x = np.asarray(inp["L0_attn_x"])
    Wqkv = np.asarray(inp["L0_attn_W_qkv"])
    bqkv = np.asarray(inp["L0_attn_b_qkv"])
    Wproj = np.asarray(inp["L0_attn_W_proj"])
    bproj = np.asarray(inp["L0_attn_b_proj"])
    gw = np.asarray(inp["L0_ffn_gate_w"])
    gs = np.asarray(inp["L0_ffn_gate_scale"])
    uw = np.asarray(inp["L0_ffn_up_w"])
    us = np.asarray(inp["L0_ffn_up_scale"])
    dw = np.asarray(inp["L0_ffn_down_w"])
    ds = np.asarray(inp["L0_ffn_down_scale"])

    wq0 = _tile_lhsT(Wqkv[:, 0:2048])
    wk0 = _tile_lhsT(Wqkv[:, 2048:4096])
    wv0 = np.ascontiguousarray(
        Wqkv[:, 4096:6144].reshape(NCH, 128, 4, 512).transpose(2, 0, 1, 3))
    bq0 = np.ascontiguousarray(bqkv[0:2048].reshape(16, 128))
    bk0 = np.ascontiguousarray(bqkv[2048:4096].reshape(16, 128))
    bv0 = np.ascontiguousarray(bqkv[4096:6144].reshape(1, 2048))
    wp0 = _tile_lhsT(Wproj)
    bp0 = np.ascontiguousarray(bproj.reshape(16, 128))
    wg_ = _tile_lhsT(gw.T)
    wu_ = _tile_lhsT(uw.T)
    wdT = dw.T
    wd_ = wdT.reshape(4, 16, 128, 16, 128)
    wd_ = np.ascontiguousarray(wd_.transpose(0, 3, 2, 1, 4).reshape(4, 16, 128, 16 * 128))
    gsc_ = np.ascontiguousarray(gs.reshape(64, 128))
    usc_ = np.ascontiguousarray(us.reshape(64, 128))
    dsc_ = np.ascontiguousarray(ds.reshape(16, 128))
    ident = np.eye(128, dtype=np.float32).astype(ml_dtypes.bfloat16)

    Wpack = np.asarray(inp["L1_attn_W_pack"])
    Wo = np.asarray(inp["L1_attn_W_o"])
    convk = np.asarray(inp["L1_attn_conv_k"]).astype(np.float32)
    convv = np.asarray(inp["L1_attn_conv_v"]).astype(np.float32)
    w1 = np.asarray(inp["L1_ffn_weight"])
    wscale = np.asarray(inp["L1_ffn_weight_scale"]).astype(np.float32)
    iscale = np.asarray(inp["L1_ffn_input_scale"])

    wpq_ = _tile_lhsT(Wpack[:, 0:2048])
    wpk_ = _tile_lhsT(Wpack[:, 2048:3072])
    wpv_ = _tile_lhsT(Wpack[:, 3072:4096])
    wod_ = _tile_lhsT(Wo)
    w1T = np.ascontiguousarray(w1.T)
    w1d_ = _tile_lhsT(w1T.view(np.uint8)).view(w1.dtype)
    protm = np.zeros((128, 128), np.float32)
    for i in range(64):
        protm[i, i + 64] = 1.0
        protm[i + 64, i] = -1.0
    protd_ = protm.astype(ml_dtypes.bfloat16)
    convd_ = np.concatenate([convk.reshape(-1), convv.reshape(-1)]).reshape(1, 32).astype(np.float32)
    iscl_ = np.asarray(iscale).reshape(1, 1)
    wscl_ = wscale.reshape(1, 1)
    SCHf = np.float32(SCH)
    inv_freq = (1.0 / 100000.0 ** (np.arange(0, 128, 2, dtype=np.float32) / 128.0)).astype(np.float32)

    shared = dict(wq0=wq0, wk0=wk0, wv0=wv0, bq0=bq0, bk0=bk0, bv0=bv0,
                  wp0=wp0, bp0=bp0, wg=wg_, wu=wu_, wd=wd_,
                  gsc=gsc_, usc=usc_, dsc=dsc_, ident=ident,
                  wpq=wpq_, wpk=wpk_, wpv=wpv_, wod=wod_, w1d=w1d_,
                  protd=protd_, convd=convd_, iscl=iscl_, wscl=wscl_)
    in_maps = []
    for c in range(8):
        b, pp = c // 2, c % 2
        xo = np.ascontiguousarray(x[b, 1024 * pp:1024 * (pp + 1), :].T)
        xp = np.ascontiguousarray(x[b, 1024 * (1 - pp):1024 * (2 - pp), :].T)
        pos = (1024 * pp + np.arange(1024, dtype=np.float32))
        fr = np.outer(pos, inv_freq)
        cosf = np.cos(fr).astype(np.float32) * SCHf
        sinf = np.sin(fr).astype(np.float32) * SCHf
        cosd_ = np.ascontiguousarray(np.concatenate([cosf.T, cosf.T], axis=0))
        sind_ = np.ascontiguousarray(np.concatenate([sinf.T, sinf.T], axis=0))
        mask = np.full((8, 9, 128, 128), -1e30, np.float32)
        for t in range(8):
            for l in range(9):
                s = (t + l + 8) % 16
                j0 = 1024 * pp + 128 * s if s < 8 else 1024 * (1 - pp) + 128 * (s - 8)
                i_ = 1024 * pp + 128 * t + np.arange(128)[None, :]
                j_ = j0 + np.arange(128)[:, None]
                valid = (j_ <= i_) & (i_ - j_ < 1024)
                mask[t, l][valid] = 0.0
        maskd_ = np.ascontiguousarray(mask.transpose(2, 0, 1, 3).reshape(128, 9216)).astype(ml_dtypes.bfloat16)
        flagd_ = np.full((128, 2), 0.0, np.float32)
        flagd_[:, 0] = float(pp)
        flagd_[:, 1] = 1.0 - float(pp)
        m = dict(shared)
        m["x_own"] = xo
        m["x_part"] = xp
        m["cosd"] = cosd_
        m["sind"] = sind_
        m["maskd"] = maskd_
        m["flagd"] = flagd_
        in_maps.append(m)
    return in_maps


def assemble_output(results):
    out = np.empty((4, 2048, 2048), dtype=np.float32)
    for c in range(8):
        b, pp = c // 2, c % 2
        sh = results[c]["y"].reshape(128, 16, 1024)
        out[b, 1024 * pp:1024 * (pp + 1), :] = sh.transpose(2, 1, 0).reshape(1024, 2048)
    return out


def kernel(**inputs):
    in_maps = prep_inputs(inputs)
    nc = build_nc(dbg_mode=None)
    res = run_bass_kernel_spmd(nc, in_maps, core_ids=list(range(8)))
    return assemble_output(res.results)


# version 05 (final)
# speedup vs baseline: 1.2172x; 1.0614x over previous
import numpy as np
import ml_dtypes
import concourse.bass as bass
import concourse.mybir as mybir
from concourse.bass_utils import run_bass_kernel_spmd

F32 = mybir.dt.float32
BF16 = mybir.dt.bfloat16
I32 = mybir.dt.int32
FP8 = mybir.dt.float8e4
AF = mybir.ActivationFunctionType
ALU = mybir.AluOpType
AX = mybir.AxisListType

D = 2048
T = 1024
NCH = 16
NH0 = 16
NIC = 64
SC = float(1.0 / np.sqrt(np.float32(128.0)))
SCH = float(np.sqrt(SC))
EPS = 1e-5

ENGS = ("t", "v", "s", "g", "y")
ARENA_BYTES = 210944


class Prog:
    def __init__(self, nc):
        self.nc = nc
        self.q = {e: [] for e in ENGS}
        self.sems = {}
        self.cnt = {}

    def sem(self, name):
        if name not in self.sems:
            self.sems[name] = self.nc.alloc_semaphore(name)
            self.cnt[name] = 0
        return self.sems[name]

    def emit(self, eng, fn=None, wait=(), inc=None, inc_by=1):
        tok = None
        if inc is not None:
            self.sem(inc)
            self.cnt[inc] += inc_by
            tok = self.cnt[inc]
        for (sn, _v) in wait:
            self.sem(sn)
        waits = tuple((sn, v) for (sn, v) in wait if v > 0)
        sems = self.sems

        def job(e, waits=waits, fn=fn, inc=inc, inc_by=inc_by):
            for (sn, val) in waits:
                e.wait_ge(sems[sn], val)
            if fn is not None:
                inst = fn(e)
                if inc is not None:
                    inst.then_inc(sems[inc], inc_by)
            elif inc is not None:
                e.engine_nop().then_inc(sems[inc], inc_by)
        self.q[eng].append(job)
        return tok

    def dma(self, eng, out, in_, inc, wait=(), slow=False):
        if slow:
            return self.emit(eng, lambda e, o=out, i=in_: e.dma_start(
                out=o, in_=i, allow_slow_non_contiguous=True), wait=wait, inc=inc, inc_by=16)
        return self.emit(eng, lambda e, o=out, i=in_: e.dma_start(out=o, in_=i),
                         wait=wait, inc=inc, inc_by=16)

    def tok(self, name):
        self.sem(name)
        return self.cnt[name]

    def replay(self):
        nc = self.nc
        with nc.Block() as block:
            @block.tensor
            def _(t):
                for job in self.q["t"]:
                    job(t)

            @block.vector
            def _(v):
                for job in self.q["v"]:
                    job(v)

            @block.scalar
            def _(s):
                for job in self.q["s"]:
                    job(s)

            @block.gpsimd
            def _(g):
                for job in self.q["g"]:
                    job(g)

            @block.sync
            def _(y):
                for job in self.q["y"]:
                    job(y)


def build_nc(dbg_mode=None):
    nc = bass.Bass("TRN2", target_bir_lowering=False, detect_race_conditions=False)
    p = Prog(nc)
    DT = nc.dram_tensor
    A = nc.alloc_sbuf_tensor
    P_ = nc.alloc_psum_tensor

    x_own = DT("x_own", [D, T], F32, kind="ExternalInput")
    x_part = DT("x_part", [D, T], F32, kind="ExternalInput")
    wq0 = DT("wq0", [NH0, 128, NCH * 128], F32, kind="ExternalInput")
    wk0 = DT("wk0", [NH0, 128, NCH * 128], F32, kind="ExternalInput")
    wv0 = DT("wv0", [4, NCH, 128, 512], F32, kind="ExternalInput")
    bq0 = DT("bq0", [NH0, 128], F32, kind="ExternalInput")
    bk0 = DT("bk0", [NH0, 128], F32, kind="ExternalInput")
    bv0 = DT("bv0", [1, D], F32, kind="ExternalInput")
    wp0 = DT("wp0", [NCH, 128, NCH * 128], F32, kind="ExternalInput")
    bp0 = DT("bp0", [NCH, 128], F32, kind="ExternalInput")
    wgd = DT("wg", [NIC, 128, NCH * 128], I32, kind="ExternalInput")
    wud = DT("wu", [NIC, 128, NCH * 128], I32, kind="ExternalInput")
    wdd = DT("wd", [4, NCH, 128, 16 * 128], I32, kind="ExternalInput")
    gsc = DT("gsc", [NIC, 128], F32, kind="ExternalInput")
    usc = DT("usc", [NIC, 128], F32, kind="ExternalInput")
    dsc = DT("dsc", [NCH, 128], F32, kind="ExternalInput")
    ident = DT("ident", [128, 128], BF16, kind="ExternalInput")
    wpq = DT("wpq", [16, 128, NCH * 128], F32, kind="ExternalInput")
    wpk = DT("wpk", [8, 128, NCH * 128], F32, kind="ExternalInput")
    wpv = DT("wpv", [8, 128, NCH * 128], F32, kind="ExternalInput")
    wod = DT("wod", [NCH, 128, NCH * 128], F32, kind="ExternalInput")
    w1d = DT("w1d", [NCH, 128, NCH * 128], FP8, kind="ExternalInput")
    cosd = DT("cosd", [128, T], F32, kind="ExternalInput")
    sind = DT("sind", [128, T], F32, kind="ExternalInput")
    protd = DT("protd", [128, 128], BF16, kind="ExternalInput")
    maskd = DT("maskd", [128, 8 * 9 * 128], BF16, kind="ExternalInput")
    convd = DT("convd", [1, 32], F32, kind="ExternalInput")
    flagd = DT("flagd", [128, 2], F32, kind="ExternalInput")
    iscl = DT("iscl", [1, 1], BF16, kind="ExternalInput")
    wscl = DT("wscl", [1, 1], F32, kind="ExternalInput")
    kv_out = DT("kv_out", [2, 8, 128, T], BF16)
    kv_gth = DT("kv_gth", [2, 2, 8, 128, T], BF16)

    y = DT("y", [128, NCH * T], F32, kind="ExternalOutput")
    dbg = DT("dbg", [128, 2 * NCH * T], F32, kind="ExternalOutput") if dbg_mode else None

    AR = A("AR", [128, ARENA_BYTES // 2], BF16)

    def BV(off, cols):
        assert off % 2 == 0 and off + 2 * cols <= ARENA_BYTES, (off, cols)
        return AR[:, off // 2: off // 2 + cols]

    def FV(off, cols):
        assert off % 4 == 0 and off + 4 * cols <= ARENA_BYTES, (off, cols)
        return AR[:, off // 2: off // 2 + 2 * cols].bitcast(F32)

    ones_c = A("ones_c", [128, 1], F32)
    ones_cb = A("ones_cb", [128, 1], BF16)
    ones_r = A("ones_r", [1, 128], F32)
    idn = A("idn", [128, 128], BF16)
    biasqk = A("biasqk", [128, 4], F32)
    atm = A("atm", [128, 3 * 128], BF16)
    dnorm = A("dnorm", [128, 2], F32)
    bpt = A("bpt", [128, NCH], F32)
    fsc = A("fsc", [128, 40], F32)

    xn = BV(0, 32768)
    x_acc = FV(65536, 16384)
    XF = FV(65536, 16384)
    probT = BV(65536, 8192)
    kh_t = BV(81920, 4096)
    qh_t = BV(90112, 2048)
    sT = FV(94208, 1024)
    SRQ = FV(98304, 1024)
    SRM = FV(102400, 1024)
    SRK = FV(106496, 1024)
    SRS = FV(110592, 8)
    mub = FV(110624, 1024)
    ksq = BV(114720, 1024)
    qsq = BV(116768, 1024)
    bvb = FV(118816, 2048)
    xw = FV(131072, 2048)
    sq2 = FV(139264, 2048)
    mb = FV(147456, 2048)
    ST_mo = FV(155648, 1024)
    ST_ro = FV(159744, 1024)
    ST_mp = FV(163840, 1024)
    ST_rp = FV(167936, 1024)
    ST_v = FV(172032, 1024)
    att0 = BV(131072, 16384)
    vg_t = BV(163840, 16 * 4 * 129)
    wvg = BV(180352, 8192)
    whead = BV(196736, 4096)
    wpt = BV(163840, 4096)
    hbuf = BV(131072, 16384)
    wgt = BV(163840, 4096)
    wut = BV(172032, 4096)
    wdt = BV(180224, 4096)
    rbc = FV(188416, 1024)
    ffa = FV(192512, 2048)
    ffm = FV(200704, 1024)
    SBC = FV(204800, 1024)
    SBCn = FV(208896, 512)

    pA = P_("pA", [128, 1024], F32)
    pB = P_("pB", [128, 1024], F32)
    pC = P_("pC", [128, 1024], F32)
    pD = P_("pD", [128, 512], F32)
    pT = P_("pT", [128, 1024], BF16)

    p.dma("y", idn[:, :], ident[:, :], inc="xc")
    p.emit("v", lambda e: e.memset(ones_c[:, :], 1.0), inc="vcst")
    p.emit("v", lambda e: e.memset(ones_cb[:, :], 1.0), inc="vcst")
    cst_tok = p.emit("v", lambda e: e.memset(ones_r[:, :], 1.0), inc="vcst")

    def bcast_row(row_ap, out_ap, n, psum, wait, incname):
        mts = None
        for s0 in range(0, n, 512):
            s1 = min(s0 + 512, n)
            mts = p.emit("t", lambda e, o=psum[:, s0:s1], r=row_ap[:, s0:s1]:
                         e.matmul(o, ones_r[0:1, :], r, start=True, stop=True),
                         wait=wait if s0 == 0 else (), inc="mm")
        return p.emit("v", lambda e, o=out_ap[:, 0:n], i=psum[:, 0:n]: e.tensor_copy(o, i),
                      wait=[("mm", mts)], inc=incname)

    def ln0(src_dram, which, start_wait, mrow, rrow):
        dts = []
        chunk_mm = {}
        for ci in range(NCH):
            sem = "xl%d" % (ci % 2)
            w = list(start_wait)
            if ci >= 2:
                w.append(("mm", 0))
            if which == 0 and ci == 0:
                w.append(("vcst", cst_tok))
            w = [x for x in w if x[1] != 0]
            if ci >= 2:
                w.append(("mm", chunk_mm[ci - 2]))
            dts.append(p.dma("y", XF[:, ci * T:(ci + 1) * T],
                             src_dram[128 * ci:128 * (ci + 1), :], inc=sem, wait=w))
            sl = ci % 2
            sv = sq2[:, sl * T:(sl + 1) * T]
            xv = XF[:, ci * T:(ci + 1) * T]
            w = [(sem, p.tok(sem))]
            vt = p.emit("v", lambda e, a=sv, b=xv: e.tensor_mul(a, b, b), wait=w, inc="vln")
            for nh in range(2):
                s0, s1 = nh * 512, (nh + 1) * 512
                w0 = []
                if ci == 0 and nh == 0 and which == 1:
                    w0 = [("vln", ln0.stats_rd)]
                p.emit("t", lambda e, o=pA[0:1, s0:s1], r=xv[:, s0:s1], st=(ci == 0), sp=(ci == NCH - 1):
                       e.matmul(o, ones_c[:, 0:1], r, start=st, stop=sp),
                       wait=w0 + [(sem, p.tok(sem))])
            mt = None
            for nh in range(2):
                s0, s1 = nh * 512, (nh + 1) * 512
                mt = p.emit("t", lambda e, o=pB[0:1, s0:s1], r=sv[:, s0:s1], st=(ci == 0), sp=(ci == NCH - 1):
                            e.matmul(o, ones_c[:, 0:1], r, start=st, stop=sp),
                            wait=[("vln", vt)] if nh == 0 else (), inc="mm")
            chunk_mm[ci] = mt
        mmtok = chunk_mm[NCH - 1]
        vrow = ST_v[0:1, :]
        p.emit("v", lambda e, o=mrow[0:1, :]: e.tensor_scalar(o, pA[0:1, :], 1.0 / D, None, ALU.mult),
               wait=[("mm", mmtok)], inc="vln")
        p.emit("v", lambda e: e.tensor_scalar(vrow, pB[0:1, :], 1.0 / D, EPS, ALU.mult, ALU.add), inc="vln")
        p.emit("v", lambda e, o=rrow[0:1, :], m=mrow[0:1, :]: e.tensor_mul(o, m, m), inc="vln")
        v1 = p.emit("v", lambda e, r=rrow[0:1, :]: e.tensor_sub(vrow, vrow, r), inc="vln")
        s1_ = p.emit("s", lambda e, o=rrow[0:1, :]: e.activation(out=o, in_=vrow, func=AF.Sqrt),
                     wait=[("vln", v1)], inc="sln")
        v2 = p.emit("v", lambda e, r=rrow[0:1, :]: e.reciprocal(r, r), wait=[("sln", s1_)], inc="vln")
        ln0.stats_rd = p.tok("vln")
        b1 = bcast_row(mrow[0:1, :], mb[:, 0:T], T, pC, [("vln", v2)], "vln")
        b2 = bcast_row(rrow[0:1, :], mb[:, T:2 * T], T, pC, [("vln", b1)], "vln")
        last = None
        for ci in range(NCH):
            xv = XF[:, ci * T:(ci + 1) * T]
            xc = xn[:, ci * 2 * T + which * T: ci * 2 * T + (which + 1) * T]
            p.emit("v", lambda e, a=xv: e.tensor_sub(a, a, mb[:, 0:T]),
                   wait=[("vln", b2)] if ci == 0 else (), inc="vln")
            last = p.emit("v", lambda e, o=xc, a=xv: e.tensor_mul(o, a, mb[:, T:2 * T]), inc="vln")
        return last, ln0.stats_rd

    ln_own, stats_rd0 = ln0(x_own, 0, [], ST_mo, ST_ro)
    ln_part, _ = ln0(x_part, 1, [("vln", ln_own)], ST_mp, ST_rp)
    xn_ready = ln_part

    def dump_dbg(get_src, n_tiles, wait_sem, wait_tok, bounce=None):
        bounce = bounce if bounce is not None else sT
        for i in range(n_tiles):
            src = get_src(i)
            tk2 = p.emit("v", lambda e, o=bounce[:, 0:512], i_=src[:, 0:512]: e.tensor_copy(o, i_),
                         wait=[(wait_sem, wait_tok), ("st", p.tok("st"))], inc="vdbg")
            p.dma("y", dbg[:, i * T:i * T + 512], bounce[:, 0:512], inc="st", wait=[("vdbg", tk2)])
            tk3 = p.emit("v", lambda e, o=bounce[:, 0:512], i_=src[:, 512:1024]: e.tensor_copy(o, i_),
                         wait=[("st", p.tok("st"))], inc="vdbg")
            p.dma("y", dbg[:, i * T + 512:(i + 1) * T], bounce[:, 0:512], inc="st", wait=[("vdbg", tk3)])
        p.emit("y", None, wait=[("st", p.tok("st"))])

    if dbg_mode == "xn0":
        dump_dbg(lambda i: xn[:, (i // 2) * 2 * T + (i % 2) * T:(i // 2) * 2 * T + (i % 2 + 1) * T],
                 32, "vln", xn_ready)
        p.replay()
        return nc

    tkbv = p.dma("y", SRQ[0:1, :], bv0[0:1, 0:1024], inc="xc", wait=[("vln", xn_ready)])
    tkbv2 = p.dma("y", SRK[0:1, :], bv0[0:1, 1024:2048], inc="xc")
    b1 = bcast_row(SRQ[0:1, :], bvb[:, 0:1024], 1024, pC, [("xc", p.tok("xc")), ("vln", xn_ready)], "vln")
    vb_ready = bcast_row(SRK[0:1, :], bvb[:, 1024:2048], 1024, pC, [("vln", b1)], "vln")
    vg4 = vg_t.rearrange("p (t h d) -> p t h d", t=16, h=4)
    cst2 = p.emit("v", lambda e: e.memset(vg4[:, :, :, 128:129], 1.0), inc="vat")

    vgrp_evac = {}
    vglast = {}

    def emit_vgroup(g):
        wtoks = []
        for fc in range(NCH):
            dst = wvg[:, fc * 512:(fc + 1) * 512]
            w = [("mm", vglast[g - 1])] if fc == 0 and g >= 1 else ()
            wtoks.append(p.dma("g", dst, wv0[g, fc, :, :], inc="wv", wait=w))
        ev_last = None
        for tc in range(16):
            which, tloc = tc // 8, tc % 8
            for fc in range(NCH):
                lhs = xn[:, fc * 2 * T + which * T + tloc * 128: fc * 2 * T + which * T + (tloc + 1) * 128]
                rhs = wvg[:, fc * 512:(fc + 1) * 512]
                w = []
                if fc == 0:
                    w = [("wv", p.tok("wv")), ("vat", p.tok("vat"))]
                    if g == 0 and tc == 0:
                        w.append(("vln", vb_ready))
                p.emit("t", lambda e, o=pC[:, 0:512], l=lhs, r=rhs, st=(fc == 0), sp=(fc == NCH - 1):
                       e.matmul(o, l, r, start=st, stop=sp),
                       wait=w, inc="mm" if fc == NCH - 1 else None)
            mmt = p.tok("mm")
            vg_view = vg4[:, tc, :, 0:128]
            ps_view = pC.ap().rearrange("p (h d) -> p h d", h=8)[:, 0:4, :]
            bb_view = bvb.rearrange("p (h d) -> p h d", h=NCH)[:, 4 * g:4 * g + 4, :]
            ev_last = p.emit("v", lambda e, o=vg_view, i=ps_view, b=bb_view: e.tensor_add(o, i, b),
                             wait=[("mm", mmt)], inc="vat")
            vglast[g] = mmt
        vgrp_evac[g] = ev_last

    att_fin = None
    qlast = {}
    klast = {}
    slot_exp = {}
    import os
    NHRUN = int(os.environ.get('NHRUN', str(NH0)))
    for h in range(1 if dbg_mode == 'qk0' else NHRUN):
        if h % 4 == 0:
            emit_vgroup(h // 4)
        hs = h % 2
        wsb = hs * NCH * 128

        wk_tok = p.dma("g", whead[:, wsb:wsb + NCH * 128], wk0[h, :, :], inc="wk",
                       wait=[("mm", qlast[h - 2])] if h >= 2 else ())
        bk_tok = p.dma("g", biasqk[:, hs * 2 + 1:hs * 2 + 2],
                       bk0[h:h + 1, :].rearrange("a b -> b a"), inc="wbk",
                       wait=[("vqk", p.tok("vqk"))] if h >= 1 else ())
        kevs = []
        for half2 in range(2):
            mmt = None
            for fc in range(NCH):
                lhs = whead[:, wsb + fc * 128:wsb + (fc + 1) * 128]
                for nb in range(2):
                    rhs = xn[:, fc * 2 * T + half2 * T + nb * 512: fc * 2 * T + half2 * T + nb * 512 + 512]
                    w = []
                    if fc == 0 and nb == 0:
                        w = [("wk", p.tok("wk")), ("vqk", p.tok("vqk")), ("vsc", p.tok("vsc"))]
                        if h == 0 and half2 == 0:
                            w.append(("vln", xn_ready))
                    mmt = p.emit("t", lambda e, o=pA[:, nb * 512:nb * 512 + 512], l=lhs, r=rhs,
                                 st=(fc == 0), sp=(fc == NCH - 1):
                                 e.matmul(o, l, r, start=st, stop=sp),
                                 wait=w, inc="mm" if fc == NCH - 1 else None)
            kdst = kh_t[:, hs * 2 * T + half2 * T: hs * 2 * T + (half2 + 1) * T]
            kev = p.emit("v", lambda e, o=kdst, b=biasqk[:, hs * 2 + 1:hs * 2 + 2]:
                         e.tensor_scalar(o, pA[:, :], b, SCH, ALU.add, ALU.mult),
                         wait=[("mm", mmt), ("wbk", p.tok("wbk"))], inc="vqk")
            kevs.append(kev)
            klast[h] = mmt

        if h >= 1 and os.environ.get('STOPAT') == 'k':
            continue
        wq_tok = p.dma("g", whead[:, wsb:wsb + NCH * 128], wq0[h, :, :], inc="wq",
                       wait=[("mm", klast[h])])
        bq_tok = p.dma("g", biasqk[:, hs * 2:hs * 2 + 1],
                       bq0[h:h + 1, :].rearrange("a b -> b a"), inc="wbq",
                       wait=[("vqk", p.tok("vqk"))] if h >= 1 else ())
        mmq = None
        for fc in range(NCH):
            lhs = whead[:, wsb + fc * 128:wsb + (fc + 1) * 128]
            for nb in range(2):
                rhs = xn[:, fc * 2 * T + nb * 512: fc * 2 * T + nb * 512 + 512]
                w = []
                if fc == 0 and nb == 0:
                    w = [("wq", p.tok("wq")), ("vqk", p.tok("vqk")), ("vsc", p.tok("vsc")),
                         ("vst2", p.tok("vst2"))]
                mmq = p.emit("t", lambda e, o=pB[:, nb * 512:nb * 512 + 512], l=lhs, r=rhs,
                             st=(fc == 0), sp=(fc == NCH - 1):
                             e.matmul(o, l, r, start=st, stop=sp),
                             wait=w, inc="mm" if fc == NCH - 1 else None)
        qdst = qh_t[:, hs * T:(hs + 1) * T]
        qlast[h] = mmq
        vq = p.emit("v", lambda e, o=qdst, b=biasqk[:, hs * 2:hs * 2 + 1]:
                    e.tensor_scalar(o, pB[:, :], b, SCH, ALU.add, ALU.mult),
                    wait=[("mm", mmq), ("wbq", p.tok("wbq"))], inc="vqk")

        hh = h % 4
        ksrc = kh_t[:, hs * 2 * T:(hs + 1) * 2 * T]
        for qhf in range(2):
            qsl = qdst[:, qhf * 512:(qhf + 1) * 512]
            exps = {}
            for kc in range(16):
                ps_ = [pA[:, 0:512], pA[:, 512:1024], pB[:, 0:512], pB[:, 512:1024]][kc % 4]
                lhs = ksrc[:, kc * 128:(kc + 1) * 128]
                w = [("sex", slot_exp.get(kc % 4, 0))]
                if kc == 0:
                    w.append(("vqk", vq))
                mm_last = p.emit("t", lambda e, o=ps_, l=lhs, r=qsl:
                                 e.matmul(o, l, r, start=True, stop=True),
                                 wait=w, inc="mm")
                exps[kc] = p.emit("s", lambda e, o=probT[:, kc * 512:(kc + 1) * 512], i=ps_:
                                  e.activation(out=o, in_=i, func=AF.Exp),
                                  wait=[("mm", mm_last)], inc="sex")
                slot_exp[kc % 4] = exps[kc]
            for qc4 in range(4):
                qc = qhf * 4 + qc4
                mmt = None
                for kc in range(16):
                    lhs = probT[:, kc * 512 + qc4 * 128: kc * 512 + (qc4 + 1) * 128]
                    rhs = vg4[:, kc, hh, :]
                    w = [("sex", exps[kc])]
                    if kc == 0:
                        w += [("vat", vgrp_evac[h // 4]), ("vpv", p.tok("vpv"))]
                    mmt = p.emit("t", lambda e, o=pD[:, 0:129], l=lhs, r=rhs, st=(kc == 0), sp=(kc == 15):
                                 e.matmul(o, l, r, start=st, stop=sp),
                                 wait=w, inc="mm" if kc == 15 else None)
                qs_ = qc % 2
                asl = atm[:, qs_ * 128:(qs_ + 1) * 128]
                vr = p.emit("v", lambda e, d=dnorm[:, qs_:qs_ + 1]: e.reciprocal(d, pD[:, 128:129]),
                            wait=[("mm", mmt)], inc="vpv")
                vn = p.emit("v", lambda e, o=asl, d=dnorm[:, qs_:qs_ + 1]:
                            e.tensor_scalar(o, pD[:, 0:128], d, None, ALU.mult),
                            wait=[("vpv", vr), ("sat", p.tok("sat"))], inc="vpv")
                trs = pT[:, (qc % 4) * 128:((qc % 4) + 1) * 128]
                w_tr = [("vpv", vn), ("sat", p.tok("sat"))]
                if h == 0 and qc == 0:
                    w_tr.append(("xc", 16))
                mtt = p.emit("t", lambda e, o=trs, i=asl: e.transpose(o, i, idn[:, :]),
                             wait=w_tr, inc="mm")
                att_fin = p.emit("v", lambda e, o=att0[:, h * T + qc * 128: h * T + (qc + 1) * 128],
                                 i=trs: e.tensor_copy(o, i), wait=[("mm", mtt)], inc="sat")

    if dbg_mode == "qk0":
        tiles = []
        tiles.append((qh_t[:, 0:T], "vqk"))
        tiles.append((kh_t[:, 0:T], "vqk"))
        tiles.append((kh_t[:, T:2 * T], "vqk"))
        tiles.append((mub[:, 0:T], "vst2"))
        vgh0 = vg4[:, :, 0, 0:128]
        fin_tok = {"vqk": p.tok("vqk"), "vst2": p.tok("vst2"), "sex": p.tok("sex"), "vat": p.tok("vat"),
                   "vpv": p.tok("vpv"), "sat": p.tok("sat")}
        idx = 0
        def put(src_ap, sem):
            nonlocal idx
            tk2 = p.emit("v", lambda e, o=sT[:, 0:512], i_=src_ap: e.tensor_copy(o, i_),
                         wait=[(sem, fin_tok[sem]), ("st", p.tok("st"))], inc="vdbg")
            p.dma("y", dbg[:, idx * 512:(idx + 1) * 512], sT[:, 0:512], inc="st", wait=[("vdbg", tk2)])
            idx += 1
        for (ap, sem) in tiles:
            put(ap[:, 0:512], sem)
            put(ap[:, 512:1024], sem)
        for tc2 in range(16):
            src_ap = vgh0[:, tc2, :]
            tk2 = p.emit("v", lambda e, o=sT[:, 0:128], i_=src_ap: e.tensor_copy(o, i_),
                         wait=[("vat", fin_tok["vat"]), ("st", p.tok("st"))], inc="vdbg")
            p.dma("y", dbg[:, idx * 512 + (tc2 % 4) * 128: idx * 512 + (tc2 % 4) * 128 + 128],
                  sT[:, 0:128], inc="st", wait=[("vdbg", tk2)])
            if tc2 % 4 == 3:
                idx += 1
        for kc in range(16):
            put(probT[:, kc * 512:(kc + 1) * 512], "sex")
        for i2 in range(2):
            put(att0[:, i2 * 512:(i2 + 1) * 512], "sat")
        mmt2 = None
        for fc in range(NCH):
            lhs = xn[:, fc * 2 * T + 0 * T + 0 * 128: fc * 2 * T + 0 * T + 128]
            mmt2 = p.emit("t", lambda e, l=lhs, r=wvg[:, fc * 512:(fc + 1) * 512],
                          st_=(fc == 0), sp=(fc == NCH - 1):
                          e.matmul(pD[:, 0:512], l, r, start=st_, stop=sp),
                          wait=[("st", p.tok("st")), ("vpv", p.tok("vpv"))] if fc == 0 else (),
                          inc="mm" if fc == NCH - 1 else None)
        tkr = p.emit("v", lambda e: e.tensor_copy(sT[:, 0:512], pD[:, 0:512]),
                     wait=[("mm", mmt2)], inc="vdbg")
        p.dma("y", dbg[:, idx * 512:(idx + 1) * 512], sT[:, 0:512], inc="st", wait=[("vdbg", tkr)])
        idx += 1
        p.emit("y", None, wait=[("st", p.tok("st"))])
        p.replay()
        return nc

    if dbg_mode == "attnraw":
        dump_dbg(lambda i: att0[:, i * T:(i + 1) * T], NCH, "sat", att_fin, bounce=sT)
        p.replay()
        return nc

    xacc_tks = []
    for ci in range(NCH):
        xacc_tks.append(p.dma("y", x_acc[:, ci * T:(ci + 1) * T],
                              x_own[128 * ci:128 * (ci + 1), :], inc="xc",
                              wait=[("sat", att_fin)]))
    bptk = p.dma("g", bpt[:, :], bp0[:, :].rearrange("c b -> b c"), inc="wpb",
                 wait=[("sat", att_fin)], slow=True)

    proj_last = None
    plast = {}
    for oc in range(NCH):
        osl = (oc % 2) * NCH * 128
        wtk = p.dma("g", wpt[:, osl:osl + NCH * 128], wp0[oc, :, :], inc="wp",
                    wait=[("mm", plast[oc - 2])] if oc >= 2 else [("sat", att_fin)])
        mmt = None
        for ic in range(NCH):
            lhs = wpt[:, osl + ic * 128:osl + (ic + 1) * 128]
            for nb in range(2):
                rhs = att0[:, ic * T + nb * 512: ic * T + nb * 512 + 512]
                w = []
                if ic == 0 and nb == 0:
                    w = [("wp", p.tok("wp")), ("vac", p.tok("vac")), ("vsc", p.tok("vsc"))]
                    if oc == 0:
                        w.append(("sat", att_fin))
                mmt = p.emit("t", lambda e, o=pA[:, nb * 512:nb * 512 + 512], l=lhs, r=rhs,
                             st=(ic == 0), sp=(ic == NCH - 1):
                             e.matmul(o, l, r, start=st, stop=sp),
                             wait=w, inc="mm" if ic == NCH - 1 else None)
        plast[oc] = mmt
        xa = x_acc[:, oc * T:(oc + 1) * T]
        w = [("mm", mmt), ("xc", p.tok("xc"))]
        if oc == 0:
            w.append(("wpb", bptk))
        proj_last = p.emit("v", lambda e, o=xa, b=bpt[:, oc:oc + 1]:
                           e.scalar_tensor_tensor(o, pA[:, :], b, o, ALU.add, ALU.add),
                           wait=w, inc="vac")

    if dbg_mode == "attn0":
        dump_dbg(lambda i: x_acc[:, i * T:(i + 1) * T], NCH, "vac", proj_last, bounce=ffa)
        p.replay()
        return nc

    rrowf = rbc[0:1, 0:1024]
    for nb in range(2):
        mmt = None
        for ci in range(NCH):
            rhs = xn[:, ci * 2 * T + nb * 512: ci * 2 * T + nb * 512 + 512]
            w = []
            if ci == 0:
                w = [("vr", p.tok("vr")), ("vpv", p.tok("vpv")), ("vst2", p.tok("vst2"))]
            mmt = p.emit("t", lambda e, r=rhs, st=(ci == 0), sp=(ci == NCH - 1):
                         e.matmul(pD[0:1, 0:512], ones_cb[:, 0:1], r, start=st, stop=sp),
                         wait=w, inc="mm" if ci == NCH - 1 else None)
        p.emit("v", lambda e, o=rbc[0:1, nb * 512:nb * 512 + 512]:
               e.tensor_copy(o, pD[0:1, 0:512]), wait=[("mm", mmt)], inc="vr")
    v128 = p.emit("v", lambda e: e.tensor_scalar(rrowf, rrowf, 128.0, None, ALU.mult),
                  wait=[("vr", p.tok("vr"))], inc="vr")
    brt = bcast_row(rrowf, rbc, T, pB, [("vr", v128), ("vsc", p.tok("vsc"))], "vr")
    dst_tk = p.dma("g", fsc[:, 4:4 + NCH], dsc[:, :].rearrange("c b -> b c"), inc="wds",
                   wait=[("sat", att_fin)], slow=True)
    nds_tok = p.emit("v", lambda e: e.tensor_scalar(fsc[:, 4 + NCH:4 + 2 * NCH],
                                                    fsc[:, 4:4 + NCH], -128.0, None, ALU.mult),
                     wait=[("wds", dst_tk)], inc="vr")

    dv = None
    glastd = {}
    ulastd = {}
    dlastd = {}
    for grp in range(4):
        hlast = None
        for il in range(16):
            ic = grp * 16 + il
            wslot = ic % 2
            wgt_d = wgt[:, wslot * NCH * 128 // 2:wslot * NCH * 128 // 2 + NCH * 128]
            wut_d = wut[:, wslot * NCH * 128 // 2:wslot * NCH * 128 // 2 + NCH * 128]
            wgt_d = wgt[:, wslot * 2048:wslot * 2048 + 2048]
            wut_d = wut[:, wslot * 2048:wslot * 2048 + 2048]
            wgk = p.dma("g", wgt_d, wgd[ic, :, :], inc="wg",
                        wait=[("mm", glastd[ic - 2])] if ic >= 2 else [("vac", p.tok("vac"))])
            wuk = p.dma("g", wut_d, wud[ic, :, :], inc="wu",
                        wait=[("mm", ulastd[ic - 2])] if ic >= 2 else [("vac", p.tok("vac"))])
            sctk = p.dma("g", fsc[:, wslot * 2:wslot * 2 + 1],
                         gsc[ic:ic + 1, :].rearrange("a b -> b a"), inc="wsg",
                         wait=[("vff", p.tok("vff"))])
            sctk2 = p.dma("g", fsc[:, wslot * 2 + 1:wslot * 2 + 2],
                          usc[ic:ic + 1, :].rearrange("a b -> b a"), inc="wsu",
                          wait=[("vff", p.tok("vff"))])
            mg = mu = None
            for fc in range(NCH):
                lg = wgt_d[:, fc * 128:(fc + 1) * 128]
                for nb in range(2):
                    rhs = xn[:, fc * 2 * T + nb * 512: fc * 2 * T + nb * 512 + 512]
                    w = []
                    if fc == 0 and nb == 0:
                        w = [("wg", p.tok("wg")), ("vff", p.tok("vff")), ("vdn", p.tok("vdn")),
                             ("vr", p.tok("vr"))]
                        if ic == 0:
                            w.append(("vr", brt))
                            w.append(("vac", p.tok("vac")))
                    mg = p.emit("t", lambda e, o=pA[:, nb * 512:nb * 512 + 512], l=lg, r=rhs,
                                st=(fc == 0), sp=(fc == NCH - 1):
                                e.matmul(o, l, r, start=st, stop=sp),
                                wait=w, inc="mm" if fc == NCH - 1 else None)
            for fc in range(NCH):
                lu = wut_d[:, fc * 128:(fc + 1) * 128]
                for nb in range(2):
                    rhs = xn[:, fc * 2 * T + nb * 512: fc * 2 * T + nb * 512 + 512]
                    w = [("wu", p.tok("wu"))] if (fc == 0 and nb == 0) else ()
                    mu = p.emit("t", lambda e, o=pB[:, nb * 512:nb * 512 + 512], l=lu, r=rhs,
                                st=(fc == 0), sp=(fc == NCH - 1):
                                e.matmul(o, l, r, start=st, stop=sp),
                                wait=w, inc="mm" if fc == NCH - 1 else None)
            glastd[ic] = mg
            ulastd[ic] = mu
            ag = ffa[:, 0:T]
            au = ffa[:, T:2 * T]
            v1 = p.emit("v", lambda e, o=ag: e.tensor_sub(o, pA[:, :], rbc[:, :]),
                        wait=[("mm", mg)], inc="vff")
            v2 = p.emit("v", lambda e, o=au: e.tensor_sub(o, pB[:, :], rbc[:, :]),
                        wait=[("mm", mu)], inc="vff")
            s1_ = p.emit("s", lambda e, o=ffm[:, :], i=ag, sc=fsc[:, wslot * 2:wslot * 2 + 1]:
                         e.activation(out=o, in_=i, func=AF.Sigmoid, scale=sc),
                         wait=[("vff", v1), ("wsg", p.tok("wsg"))], inc="sff")
            p.emit("v", lambda e: e.tensor_mul(ffa[:, 0:T], ffa[:, 0:T], ffm[:, :]),
                   wait=[("sff", s1_)], inc="vff")
            v4 = p.emit("v", lambda e: e.tensor_mul(ffa[:, 0:T], ffa[:, 0:T], ffa[:, T:2 * T]),
                        inc="vff")
            v5 = p.emit("v", lambda e, a=wslot: e.tensor_mul(fsc[:, a * 2:a * 2 + 1],
                                                             fsc[:, a * 2:a * 2 + 1],
                                                             fsc[:, a * 2 + 1:a * 2 + 2]),
                        wait=[("wsu", p.tok("wsu")), ("sff", s1_)], inc="vff")
            hlast = p.emit("v", lambda e, o=hbuf[:, il * T:(il + 1) * T], a=wslot:
                           e.tensor_scalar(o, ffa[:, 0:T], fsc[:, a * 2:a * 2 + 1], None, ALU.mult),
                           wait=[("vff", v4), ("vff", v5)], inc="vff")
        for nb in range(2):
            mmt = None
            for il in range(16):
                rhs = hbuf[:, il * T + nb * 512: il * T + nb * 512 + 512]
                w = []
                if il == 0:
                    w = [("vff", hlast), ("vr", p.tok("vr"))]
                mmt = p.emit("t", lambda e, o=pB[0:1, nb * 512:nb * 512 + 512], r=rhs,
                             st=(il == 0), sp=(il == 15):
                             e.matmul(o, ones_cb[:, 0:1], r, start=st, stop=sp),
                             wait=w, inc="mm" if il == 15 else None)
            p.emit("v", lambda e, nb=nb: e.tensor_copy(SBC[0:1, nb * 512:nb * 512 + 512],
                                                       pB[0:1, nb * 512:nb * 512 + 512]),
                   wait=[("mm", mmt)], inc="vr")
        sbc_tok = bcast_row(SBC[0:1, :], SBC, T, pB, [("vr", p.tok("vr"))], "vr")
        for hc in range(NCH):
            wslot = hc % 2
            wdt_d = wdt[:, wslot * 2048:wslot * 2048 + 2048]
            wdk = p.dma("g", wdt_d, wdd[grp, hc, :, :], inc="wd",
                        wait=[("mm", dlastd[(grp, hc - 2)])] if hc >= 2 else (
                            [("mm", dlastd[(grp - 1, NCH - 1)])] if grp >= 1 else [("vac", p.tok("vac"))]))
            ps = pC if hc % 2 == 0 else pA
            mmt = None
            for il in range(16):
                lhs = wdt_d[:, il * 128:(il + 1) * 128]
                for nb in range(2):
                    rhs = hbuf[:, il * T + nb * 512: il * T + nb * 512 + 512]
                    w = []
                    if il == 0 and nb == 0:
                        w = [("wd", p.tok("wd")), ("vff", hlast), ("vdn", p.tok("vdn"))]
                        if grp == 0 and hc <= 1:
                            w.append(("vat", p.tok("vat")))
                    mmt = p.emit("t", lambda e, o=ps[:, nb * 512:nb * 512 + 512], l=lhs, r=rhs,
                                 st=(il == 0), sp=(il == 15):
                                 e.matmul(o, l, r, start=st, stop=sp),
                                 wait=w, inc="mm" if il == 15 else None)
            dlastd[(grp, hc)] = mmt
            xa = x_acc[:, hc * T:(hc + 1) * T]
            w = [("mm", mmt), ("vac", proj_last if grp == 0 else 0)]
            if grp == 0 and hc == 0:
                w.append(("wds", dst_tk))
            dv = p.emit("v", lambda e, o=xa, s=fsc[:, 4 + hc:4 + hc + 1], ps_=ps:
                        e.scalar_tensor_tensor(o, ps_[:, :], s, o, ALU.mult, ALU.add),
                        wait=w, inc="vdn")
            dv = p.emit("v", lambda e, o=xa, s=fsc[:, 4 + NCH + hc:4 + NCH + hc + 1]:
                        e.scalar_tensor_tensor(o, SBC[:, :], s, o, ALU.mult, ALU.add),
                        wait=[("vr", sbc_tok), ("vr", nds_tok)], inc="vdn")


    xn1 = BV(0, 16384)
    qx = BV(32768, 16384)
    QS0 = FV(131072, 1024)
    QS1 = FV(135168, 1024)
    QS2 = FV(139264, 1024)
    QS3 = FV(143360, 1024)
    KVT = BV(147456, 2048)
    CS1 = FV(155648, 2048)
    WPS = BV(163840, 4096)
    W1S = BV(172032, 4096)
    LR_m = FV(180224, 1024)
    LR_r = FV(184320, 1024)
    LR_v = FV(188416, 1024)
    MB1 = FV(192512, 2048)
    SCL = FV(200704, 16)
    RSB = FV(200768, 1024)
    protv = atm[:, 256:384]
    isb = atm[:, 384:385] if False else None

    tokP3 = [("vdn", p.tok("vdn")), ("vr", p.tok("vr")), ("vac", p.tok("vac"))]
    p.dma("y", protv, protd[:, :], inc="xc", wait=tokP3)
    p.dma("y", CS1[:, 0:T], cosd[:, :], inc="xc")
    p.dma("y", CS1[:, T:2 * T], sind[:, :], inc="xc")
    p.dma("y", fsc[:, 2:3], flagd[:, 0:1], inc="xc", slow=True)
    p.dma("y", fsc[:, 3:4], flagd[:, 1:2], inc="xc", slow=True)
    isld = p.dma("y", mb[0:1, 0:1].bitcast(BF16)[0:1, 0:1] if False else qh_t[0:1, 0:1], iscl[0:1, :], inc="xc")
    wsld = p.dma("y", SCL[0:1, 1:2], wscl[0:1, :], inc="xc")

    prev_mm = None
    for ci in range(NCH):
        xv = x_acc[:, ci * T:(ci + 1) * T]
        w0 = list(tokP3) if ci == 0 else ([("mm", prev_mm)] if prev_mm else [])
        vt = p.emit("v", lambda e, a=QS0[:, :], b=xv: e.tensor_mul(a, b, b), wait=w0, inc="vl1")
        for nh in range(2):
            s0, s1 = nh * 512, (nh + 1) * 512
            p.emit("t", lambda e, o=pA[0:1, s0:s1], r=xv[:, s0:s1], st=(ci == 0), sp=(ci == NCH - 1):
                   e.matmul(o, ones_c[:, 0:1], r, start=st, stop=sp),
                   wait=[("vl1", vt)] if nh == 0 else ())
        mt = None
        for nh in range(2):
            s0, s1 = nh * 512, (nh + 1) * 512
            mt = p.emit("t", lambda e, o=pB[0:1, s0:s1], r=QS0[:, s0:s1], st=(ci == 0), sp=(ci == NCH - 1):
                        e.matmul(o, ones_c[:, 0:1], r, start=st, stop=sp),
                        wait=[("vl1", vt)] if nh == 0 else (), inc="mm")
        prev_mm = mt
    p.emit("v", lambda e: e.tensor_scalar(LR_m[0:1, :], pA[0:1, :], 1.0 / D, None, ALU.mult),
           wait=[("mm", prev_mm)], inc="vl1")
    p.emit("v", lambda e: e.tensor_scalar(LR_v[0:1, :], pB[0:1, :], 1.0 / D, EPS, ALU.mult, ALU.add), inc="vl1")
    p.emit("v", lambda e: e.tensor_mul(LR_r[0:1, :], LR_m[0:1, :], LR_m[0:1, :]), inc="vl1")
    v1 = p.emit("v", lambda e: e.tensor_sub(LR_v[0:1, :], LR_v[0:1, :], LR_r[0:1, :]), inc="vl1")
    s1_ = p.emit("s", lambda e: e.activation(out=LR_r[0:1, :], in_=LR_v[0:1, :], func=AF.Sqrt),
                 wait=[("vl1", v1)], inc="sl1")
    v2 = p.emit("v", lambda e: e.reciprocal(LR_r[0:1, :], LR_r[0:1, :]), wait=[("sl1", s1_)], inc="vl1")
    vcv = p.emit("v", lambda e: e.tensor_copy(SCL[0:1, 0:1], qh_t[0:1, 0:1]),
                 wait=[("xc", p.tok("xc"))], inc="vl1")
    vinv = p.emit("v", lambda e: e.reciprocal(SCL[0:1, 2:3], SCL[0:1, 0:1]), wait=[("vl1", vcv)], inc="vl1")
    vsw = p.emit("v", lambda e: e.tensor_mul(SCL[0:1, 3:4], SCL[0:1, 0:1], SCL[0:1, 1:2]),
                 wait=[("vl1", vcv)], inc="vl1")
    vrsi = p.emit("v", lambda e: e.tensor_scalar(LR_v[0:1, 0:T], LR_r[0:1, :], SCL[0:1, 2:3], None, ALU.mult),
                  wait=[("vl1", v2), ("vl1", vinv)], inc="vl1")
    b1 = bcast_row(LR_m[0:1, :], MB1[:, 0:T], T, pC, [("vl1", v2)], "vl1")
    b2 = bcast_row(LR_r[0:1, :], MB1[:, T:2 * T], T, pC, [("vl1", b1)], "vl1")
    rsb_tok = bcast_row(LR_v[0:1, 0:T], RSB, T, pC, [("vl1", vrsi), ("vl1", b2)], "vl1")
    cv2 = p.dma("y", LR_m[0:1, 0:32], convd[0:1, :], inc="xc", wait=[("vl1", b2)])
    bcv = bcast_row(LR_m[0:1, 0:32], fsc[:, 4:36], 32, pD, [("xc", p.tok("xc"))], "vl1")
    msw = p.emit("t", lambda e: e.matmul(pD[:, 0:1], ones_r[0:1, :], SCL[0:1, 3:4], start=True, stop=True),
                 wait=[("vl1", vsw), ("vl1", bcv)], inc="mm")
    bsw = p.emit("v", lambda e: e.tensor_copy(fsc[:, 0:1], pD[:, 0:1]), wait=[("mm", msw)], inc="vl1")

    QC = float(2 ** 20 + 1)
    QG = float(3 * 2 ** 13)
    QTH = float((1.5 * 2.0 ** -6) ** 2)
    qx_done = xn1_done = None
    for ci in range(NCH):
        xv = x_acc[:, ci * T:(ci + 1) * T]
        p.emit("v", lambda e, a=xv: e.tensor_sub(QS0[:, :], a, MB1[:, 0:T]),
               wait=[("vl1", rsb_tok), ("sl1", p.tok("sl1"))] if ci == 0 else (), inc="vl1")
        xn1_done = p.emit("v", lambda e, o=xn1[:, ci * T:(ci + 1) * T]:
                          e.tensor_mul(o, QS0[:, :], MB1[:, T:2 * T]), inc="vl1")
        p.emit("v", lambda e: e.tensor_mul(QS0[:, :], QS0[:, :], RSB[:, :]), inc="vl1")
        p.emit("v", lambda e: e.tensor_scalar(QS0[:, :], QS0[:, :], -448.0, 448.0, ALU.max, ALU.min), inc="vl1")
        p.emit("v", lambda e: e.tensor_scalar(QS1[:, :], QS0[:, :], QG, QG, ALU.add, ALU.subtract), inc="vl1")
        st_ = p.emit("s", lambda e: e.mul(QS2[:, :], QS0[:, :], QC), wait=[("vl1", p.tok("vl1"))], inc="sl1")
        p.emit("v", lambda e: e.tensor_sub(QS3[:, :], QS2[:, :], QS0[:, :]), wait=[("sl1", st_)], inc="vl1")
        p.emit("v", lambda e: e.tensor_sub(QS2[:, :], QS2[:, :], QS3[:, :]), inc="vl1")
        p.emit("v", lambda e: e.tensor_mul(QS3[:, :], QS0[:, :], QS0[:, :]), inc="vl1")
        p.emit("v", lambda e: e.tensor_scalar(QS3[:, :], QS3[:, :], QTH, 1e30, ALU.subtract, ALU.mult), inc="vl1")
        p.emit("v", lambda e: e.tensor_scalar(QS3[:, :], QS3[:, :], 0.0, 1.0, ALU.max, ALU.min), inc="vl1")
        p.emit("v", lambda e: e.tensor_sub(QS2[:, :], QS2[:, :], QS1[:, :]), inc="vl1")
        p.emit("v", lambda e: e.tensor_mul(QS2[:, :], QS2[:, :], QS3[:, :]), inc="vl1")
        qx_done = p.emit("v", lambda e, o=qx[:, ci * T:(ci + 1) * T]:
                         e.tensor_add(o, QS2[:, :], QS1[:, :]), inc="vl1")

    pub_last = None
    for kv in range(8):
        wsl = (kv % 2) * NCH * 128
        wkt = p.dma("g", WPS[:, wsl:wsl + NCH * 128], wpk[kv, :, :], inc="wk1",
                    wait=[("mm", p.tok("mm"))] if kv >= 1 else [("vl1", qx_done)])
        mmt = None
        for fc in range(NCH):
            lhs = WPS[:, wsl + fc * 128:wsl + (fc + 1) * 128]
            for nb in range(2):
                rhs = xn1[:, fc * T + nb * 512: fc * T + nb * 512 + 512]
                w = []
                if fc == 0 and nb == 0:
                    w = [("wk1", p.tok("wk1")), ("vkv", p.tok("vkv"))]
                    if kv == 0:
                        w.append(("vl1", xn1_done))
                mmt = p.emit("t", lambda e, o=pA[:, nb * 512:nb * 512 + 512], l=lhs, r=rhs,
                             st=(fc == 0), sp=(fc == NCH - 1):
                             e.matmul(o, l, r, start=st, stop=sp),
                             wait=w, inc="mm" if fc == NCH - 1 else None)
        kev = p.emit("v", lambda e: e.tensor_copy(KVT[:, 0:T], pA[:, :]),
                     wait=[("mm", mmt), ("pub", p.tok("pub"))], inc="vkv")
        rt = None
        for nb in range(2):
            rt = p.emit("t", lambda e, o=pB[:, nb * 512:nb * 512 + 512],
                        r=KVT[:, nb * 512:nb * 512 + 512]:
                        e.matmul(o, protv, r, start=True, stop=True),
                        wait=[("vkv", kev), ("vrp", p.tok("vrp"))] if nb == 0 else (), inc="mm")
        r1 = p.emit("v", lambda e: e.tensor_mul(QS0[:, :], KVT[:, 0:T], CS1[:, 0:T]),
                    wait=[("vkv", kev)], inc="vrp")
        r2 = p.emit("v", lambda e: e.tensor_mul(QS1[:, :], pB[:, :], CS1[:, T:2 * T]),
                    wait=[("mm", rt)], inc="vrp")
        r3 = p.emit("v", lambda e: e.tensor_add(KVT[:, T:2 * T], QS0[:, :], QS1[:, :]),
                    wait=[("vrp", r2)], inc="vrp")
        pub_last = p.dma("y", kv_out[0, kv, :, :], KVT[:, T:2 * T], inc="pub", wait=[("vrp", r3)])
        wvt_ = p.dma("g", WPS[:, wsl:wsl + NCH * 128], wpv[kv, :, :], inc="wk1",
                     wait=[("mm", p.tok("mm"))])
        mmt = None
        for fc in range(NCH):
            lhs = WPS[:, wsl + fc * 128:wsl + (fc + 1) * 128]
            for nb in range(2):
                rhs = xn1[:, fc * T + nb * 512: fc * T + nb * 512 + 512]
                w = [("wk1", p.tok("wk1")), ("vkv", p.tok("vkv"))] if (fc == 0 and nb == 0) else ()
                mmt = p.emit("t", lambda e, o=pA[:, nb * 512:nb * 512 + 512], l=lhs, r=rhs,
                             st=(fc == 0), sp=(fc == NCH - 1):
                             e.matmul(o, l, r, start=st, stop=sp),
                             wait=w, inc="mm" if fc == NCH - 1 else None)
        vev = p.emit("v", lambda e: e.tensor_copy(KVT[:, 0:T], pA[:, :]),
                     wait=[("mm", mmt), ("pub", p.tok("pub"))], inc="vkv")
        pub_last = p.dma("y", kv_out[1, kv, :, :], KVT[:, 0:T], inc="pub", wait=[("vkv", vev)])

    cc_tok = p.emit("g", lambda e: e.collective_compute(
        "AllGather", ALU.bypass, replica_groups=[[0, 1], [2, 3], [4, 5], [6, 7]],
        ins=[kv_out.ap().opt()], outs=[kv_gth.ap().opt()]),
        wait=[("pub", p.tok("pub"))], inc="cc")

    ffn1_last = None
    for oc in range(NCH):
        wsl = (oc % 2) * NCH * 128
        wft = p.dma("g", W1S[:, wsl:wsl + NCH * 128], w1d[oc, :, :], inc="wf1",
                    wait=[("mm", p.tok("mm"))] if oc >= 2 else [("vl1", qx_done)])
        ps = pA if oc % 2 == 0 else pB
        mmt = None
        for ic in range(NCH):
            lhs = W1S[:, wsl + ic * 128:wsl + (ic + 1) * 128]
            for nb in range(2):
                rhs = qx[:, ic * T + nb * 512: ic * T + nb * 512 + 512]
                w = []
                if ic == 0 and nb == 0:
                    w = [("wf1", p.tok("wf1")), ("vf1", p.tok("vf1")), ("vkv", p.tok("vkv")),
                         ("vrp", p.tok("vrp"))]
                mmt = p.emit("t", lambda e, o=ps[:, nb * 512:nb * 512 + 512], l=lhs, r=rhs,
                             st=(ic == 0), sp=(ic == NCH - 1):
                             e.matmul(o, l, r, start=st, stop=sp),
                             wait=w, inc="mm" if ic == NCH - 1 else None)
        ffn1_last = p.emit("v", lambda e, o=x_acc[:, oc * T:(oc + 1) * T], ps_=ps:
                           e.scalar_tensor_tensor(o, ps_[:, :], fsc[:, 0:1], o, ALU.mult, ALU.add),
                           wait=[("mm", mmt), ("vl1", bsw)], inc="vf1")

    masks = BV(32768, 9216)
    CS2 = FV(51200, 2048)
    ST2 = BV(59392, 1024)
    SS2 = FV(61440, 1024)
    KB = BV(131072, 8192)
    VB = BV(147456, 8256)
    PT1 = BV(164352, 4608)
    WQS = BV(173568, 4096)
    QCUR = BV(181760, 2048)
    WOS = BV(185856, 4096)
    A1Q = BV(194048, 4096)
    KB4 = KB.rearrange("p (v t) -> p v t", v=4)
    VB4 = VB.rearrange("p (v b c) -> p v b c", v=4, b=16)

    mld = p.dma("y", masks[:, :], maskd[:, :], inc="xc",
                wait=[("vf1", p.tok("vf1"))])
    p.dma("y", CS2[:, 0:T], cosd[:, :], inc="xc")
    csl2 = p.dma("y", CS2[:, T:2 * T], sind[:, :], inc="xc")

    att1_fin = None
    for hf in range(2):
        rb0w = [("cc", cc_tok), ("xc", p.tok("xc")), ("vf1", p.tok("vf1"))]
        if hf == 1:
            rb0w.append(("mm", p.tok("mm")))
            rb0w.append(("vb2", p.tok("vb2")))
        for (reg, base_src) in ((0, 0), (1, 1)):
            dst = KB4[:, :, reg * T:(reg + 1) * T]
            d1 = p.dma("y", dst, kv_gth[base_src, 0, 4 * hf:4 * hf + 4, :, :].rearrange("v p t -> p v t"),
                       inc="xr", wait=rb0w if reg == 0 else ())
            b1_ = p.emit("v", lambda e, o=dst: e.tensor_scalar(o, o, fsc[:, 3:4], None, ALU.mult),
                         wait=[("xr", p.tok("xr"))], inc="vb2")
            for kvl in range(4):
                d2 = p.dma("y", ST2[:, :], kv_gth[1 - base_src, 0, 4 * hf + kvl, :, :], inc="xr",
                           wait=[("vb2", p.tok("vb2"))])
                b1_ = p.emit("v", lambda e, o=KB4[:, kvl, reg * T:(reg + 1) * T]:
                             e.scalar_tensor_tensor(o, ST2[:, :], fsc[:, 2:3], o, ALU.mult, ALU.add),
                             wait=[("xr", p.tok("xr"))], inc="vb2")
        for kvl in range(4):
            kvg = 4 * hf + kvl
            w0c = fsc[:, 4 + 2 * kvg:5 + 2 * kvg]
            w1c = fsc[:, 5 + 2 * kvg:6 + 2 * kvg]
            kown = KB4[:, kvl, 0:T]
            kpar = KB4[:, kvl, T:2 * T]
            bb = p.emit("v", lambda e, o=ST2[:, 0:1], i=KB4[:, kvl, 2 * T - 1:2 * T]:
                        e.tensor_scalar(o, i, fsc[:, 2:3], None, ALU.mult),
                        wait=[("vb2", p.tok("vb2"))], inc="vb2")
            p.emit("v", lambda e, o=ST2[:, 1:T], i=kown[:, 0:T - 1], w=w0c:
                   e.tensor_scalar(o, i, w, None, ALU.mult), inc="vb2")
            p.emit("v", lambda e, o=ST2[:, 0:1], w=w0c:
                   e.tensor_scalar(o, ST2[:, 0:1], w, None, ALU.mult), inc="vb2")
            p.emit("v", lambda e, o=kown, w=w1c:
                   e.scalar_tensor_tensor(o, kown, w, ST2[:, 0:T], ALU.mult, ALU.add), inc="vb2")
            p.emit("v", lambda e, o=ST2[:, 1:T], i=kpar[:, 0:T - 1], w=w0c:
                   e.tensor_scalar(o, i, w, None, ALU.mult), inc="vb2")
            p.emit("v", lambda e, o=ST2[:, 0:1]: e.memset(o, 0.0), inc="vb2")
            p.emit("v", lambda e, o=kpar, w=w1c:
                   e.scalar_tensor_tensor(o, kpar, w, ST2[:, 0:T], ALU.mult, ALU.add), inc="vb2")
        VFM = PT1[:, 0:2048]
        on1 = p.emit("v", lambda e: e.memset(VB4[:, :, :, 128:129], 1.0),
                     wait=[("vb2", p.tok("vb2"))] + ([("mm", p.tok("mm"))] if hf == 1 else []), inc="vb2")
        for kvl in range(4):
            kvg = 4 * hf + kvl
            w0c = fsc[:, 4 + 32 + 2 * kvg - 28:5 + 32 + 2 * kvg - 28] if False else fsc[:, 20 + 2 * kvg:21 + 2 * kvg]
            w1c = fsc[:, 21 + 2 * kvg:22 + 2 * kvg]
            dv1 = p.dma("y", VFM[:, 0:T], kv_gth[0, 1, kvg, :, :], inc="xr",
                        wait=[("mm", p.tok("mm")), ("vb2", p.tok("vb2"))] if kvl == 0 else [("vb2", p.tok("vb2"))])
            p.emit("v", lambda e, o=VFM[:, 0:T]: e.tensor_scalar(o, o, fsc[:, 3:4], None, ALU.mult),
                   wait=[("xr", p.tok("xr"))], inc="vb2")
            dv2 = p.dma("y", ST2[:, :], kv_gth[1, 1, kvg, :, :], inc="xr", wait=[("vb2", p.tok("vb2"))])
            p.emit("v", lambda e, o=VFM[:, 0:T]:
                   e.scalar_tensor_tensor(o, ST2[:, :], fsc[:, 2:3], o, ALU.mult, ALU.add),
                   wait=[("xr", p.tok("xr"))], inc="vb2")
            dv3 = p.dma("y", VFM[:, T:2 * T], kv_gth[1, 1, kvg, :, :], inc="xr", wait=[("vb2", p.tok("vb2"))])
            p.emit("v", lambda e, o=VFM[:, T:2 * T]: e.tensor_scalar(o, o, fsc[:, 3:4], None, ALU.mult),
                   wait=[("xr", p.tok("xr"))], inc="vb2")
            dv4 = p.dma("y", ST2[:, :], kv_gth[0, 1, kvg, :, :], inc="xr", wait=[("vb2", p.tok("vb2"))])
            p.emit("v", lambda e, o=VFM[:, T:2 * T]:
                   e.scalar_tensor_tensor(o, ST2[:, :], fsc[:, 2:3], o, ALU.mult, ALU.add),
                   wait=[("xr", p.tok("xr"))], inc="vb2")
            bb = p.emit("v", lambda e, o=ST2[:, 0:1], i=VFM[:, 2 * T - 1:2 * T]:
                        e.tensor_scalar(o, i, fsc[:, 2:3], None, ALU.mult), inc="vb2")
            p.emit("v", lambda e, o=ST2[:, 1:T], i=VFM[:, 0:T - 1], w=w0c:
                   e.tensor_scalar(o, i, w, None, ALU.mult), inc="vb2")
            p.emit("v", lambda e, o=ST2[:, 0:1], w=w0c:
                   e.tensor_scalar(o, ST2[:, 0:1], w, None, ALU.mult), inc="vb2")
            p.emit("v", lambda e, o=VFM[:, 0:T], w=w1c:
                   e.scalar_tensor_tensor(o, VFM[:, 0:T], w, ST2[:, 0:T], ALU.mult, ALU.add), inc="vb2")
            p.emit("v", lambda e, o=ST2[:, 1:T], i=VFM[:, T:2 * T - 1], w=w0c:
                   e.tensor_scalar(o, i, w, None, ALU.mult), inc="vb2")
            p.emit("v", lambda e, o=ST2[:, 0:1]: e.memset(o, 0.0), inc="vb2")
            vcv_ = p.emit("v", lambda e, o=VFM[:, T:2 * T], w=w1c:
                          e.scalar_tensor_tensor(o, VFM[:, T:2 * T], w, ST2[:, 0:T], ALU.mult, ALU.add),
                          inc="vb2")
            for blk in range(16):
                trs = pT[:, (blk % 4) * 128:((blk % 4) + 1) * 128]
                mtt = p.emit("t", lambda e, o=trs, i=VFM[:, blk * 128:(blk + 1) * 128]:
                             e.transpose(o, i, idn[:, :]),
                             wait=[("vb2", vcv_ if blk == 0 else 0), ("vb2", p.tok("vb2"))], inc="mm")
                p.emit("v", lambda e, o=VB4[:, kvl, blk, 0:128], i=trs: e.tensor_copy(o, i),
                       wait=[("mm", mtt)], inc="vb2")
        for qi in range(8):
            qh = 8 * hf + qi
            kvl = qi // 2
            qsl = (qi % 2) * T
            wqt = p.dma("g", WQS[:, (qi % 2) * NCH * 128:(qi % 2) * NCH * 128 + NCH * 128],
                        wpq[qh, :, :], inc="wq2",
                        wait=[("mm", p.tok("mm"))] if (hf or qi) else [("vf1", p.tok("vf1"))])
            mmq = None
            for fc in range(NCH):
                lhs = WQS[:, (qi % 2) * NCH * 128 + fc * 128:(qi % 2) * NCH * 128 + (fc + 1) * 128]
                for nb in range(2):
                    rhs = xn1[:, fc * T + nb * 512: fc * T + nb * 512 + 512]
                    w = [("wq2", p.tok("wq2")), ("vq2", p.tok("vq2"))] if (fc == 0 and nb == 0) else ()
                    mmq = p.emit("t", lambda e, o=pA[:, nb * 512:nb * 512 + 512], l=lhs, r=rhs,
                                 st=(fc == 0), sp=(fc == NCH - 1):
                                 e.matmul(o, l, r, start=st, stop=sp),
                                 wait=w, inc="mm" if fc == NCH - 1 else None)
            qev = p.emit("v", lambda e, o=QCUR[:, qsl:qsl + T]: e.tensor_copy(o, pA[:, :]),
                         wait=[("mm", mmq)], inc="vq2")
            rt = None
            for nb in range(2):
                rt = p.emit("t", lambda e, o=pB[:, nb * 512:nb * 512 + 512],
                            r=QCUR[:, qsl + nb * 512:qsl + nb * 512 + 512]:
                            e.matmul(o, protv, r, start=True, stop=True),
                            wait=[("vq2", qev), ("vq2", p.tok("vq2"))] if nb == 0 else (), inc="mm")
            p.emit("v", lambda e, o=ST2[:, :], i=QCUR[:, qsl:qsl + T]:
                   e.tensor_mul(o, i, CS2[:, 0:T]), wait=[("vq2", qev)], inc="vq2")
            p.emit("v", lambda e, o=QCUR[:, qsl:qsl + T]: e.tensor_mul(o, pB[:, :], CS2[:, T:2 * T]),
                   wait=[("mm", rt)], inc="vq2")
            qrope = p.emit("v", lambda e, o=QCUR[:, qsl:qsl + T]: e.tensor_add(o, o, ST2[:, :]),
                           inc="vq2")
            for qthf in range(2):
                exp_toks = []
                for q4 in range(4):
                    qt = qthf * 4 + q4
                    mml = {}
                    for l in range(9):
                        s = (qt + l + 8) % 16
                        lhs = KB4[:, kvl, s * 128:(s + 1) * 128]
                        o = pA[:, l * 128:(l + 1) * 128] if l < 8 else pB[:, 0:128]
                        w = []
                        if l == 0:
                            w = [("vq2", qrope), ("vsc2", p.tok("vsc2")), ("vb2", p.tok("vb2"))]
                        mml[l] = p.emit("t", lambda e, o=o, l_=lhs, r=QCUR[:, qsl + qt * 128:qsl + (qt + 1) * 128]:
                                        e.matmul(o, l_, r, start=True, stop=True),
                                        wait=w, inc="mm" if l in (7, 8) else None)
                    mw = masks[:, qt * 1152:qt * 1152 + 1024]
                    mn = masks[:, qt * 1152 + 1024:qt * 1152 + 1152]
                    vs1 = p.emit("v", lambda e, m=mw: e.tensor_add(SS2[:, :], pA[:, :], m),
                                 wait=[("mm", mml[7]), ("sx2", p.tok("sx2"))], inc="vsc2")
                    ex1 = p.emit("s", lambda e, o=PT1[:, q4 * 1152:q4 * 1152 + 1024]:
                                 e.activation(out=o, in_=SS2[:, :], func=AF.Exp),
                                 wait=[("vsc2", vs1)], inc="sx2")
                    vs2 = p.emit("v", lambda e, m=mn: e.tensor_add(SS2[:, 0:128], pB[:, 0:128], m),
                                 wait=[("mm", mml[8]), ("sx2", ex1)], inc="vsc2")
                    ex2 = p.emit("s", lambda e, o=PT1[:, q4 * 1152 + 1024:q4 * 1152 + 1152]:
                                 e.activation(out=o, in_=SS2[:, 0:128], func=AF.Exp),
                                 wait=[("vsc2", vs2)], inc="sx2")
                    exp_toks.append(ex2)
                for q4 in range(4):
                    qt = qthf * 4 + q4
                    mmt = None
                    for l in range(9):
                        s = (qt + l + 8) % 16
                        lhs = PT1[:, q4 * 1152 + l * 128:q4 * 1152 + (l + 1) * 128]
                        rhs = VB4[:, kvl, s, :]
                        w = []
                        if l == 0:
                            w = [("sx2", exp_toks[q4]), ("vpv2", p.tok("vpv2"))]
                        mmt = p.emit("t", lambda e, o=pD[:, 0:129], l_=lhs, r=rhs, st=(l == 0), sp=(l == 8):
                                     e.matmul(o, l_, r, start=st, stop=sp),
                                     wait=w, inc="mm" if l == 8 else None)
                    qs_ = qt % 2
                    vr = p.emit("v", lambda e, d=dnorm[:, qs_:qs_ + 1]: e.reciprocal(d, pD[:, 128:129]),
                                wait=[("mm", mmt)], inc="vpv2")
                    vn = p.emit("v", lambda e, o=atm[:, qs_ * 128:(qs_ + 1) * 128], d=dnorm[:, qs_:qs_ + 1]:
                                e.tensor_scalar(o, pD[:, 0:128], d, None, ALU.mult),
                                wait=[("vpv2", vr), ("va2", p.tok("va2"))], inc="vpv2")
                    trs = pT[:, 512 + (qt % 2) * 128:512 + ((qt % 2) + 1) * 128]
                    mtt = p.emit("t", lambda e, o=trs, i=atm[:, qs_ * 128:(qs_ + 1) * 128]:
                                 e.transpose(o, i, idn[:, :]),
                                 wait=[("vpv2", vn), ("va2", p.tok("va2"))], inc="mm")
                    att1_fin = p.emit("v", lambda e, o=A1Q[:, (qi % 4) * T + qt * 128:(qi % 4) * T + (qt + 1) * 128],
                                      i=trs: e.tensor_copy(o, i), wait=[("mm", mtt)], inc="va2")
            if qi % 4 == 3:
                qtr = qh // 4
                for oc in range(NCH):
                    osl = (oc % 2) * NCH * 128
                    wot = p.dma("g", WOS[:, osl:osl + 512], wod[oc, :, 4 * qtr * 128:(4 * qtr + 4) * 128],
                                inc="wo2", wait=[("mm", p.tok("mm"))] if oc >= 2 else [("va2", att1_fin)])
                    mmt = None
                    for icl in range(4):
                        lhs = WOS[:, osl + icl * 128:osl + (icl + 1) * 128]
                        for nb in range(2):
                            rhs = A1Q[:, icl * T + nb * 512: icl * T + nb * 512 + 512]
                            w = []
                            if icl == 0 and nb == 0:
                                w = [("wo2", p.tok("wo2")), ("vo2", p.tok("vo2")), ("va2", att1_fin)]
                            mmt = p.emit("t", lambda e, o=pC[:, nb * 512:nb * 512 + 512], l=lhs, r=rhs,
                                         st=(icl == 0), sp=(icl == 3):
                                         e.matmul(o, l, r, start=st, stop=sp),
                                         wait=w, inc="mm" if icl == 3 else None)
                    p.emit("v", lambda e, o=x_acc[:, oc * T:(oc + 1) * T]:
                           e.tensor_add(o, o, pC[:, :]),
                           wait=[("mm", mmt), ("vf1", p.tok("vf1"))], inc="vo2")

    for qtr in range(4):
        sl = x_acc[:, qtr * 4 * T:(qtr + 1) * 4 * T]
        p.dma("y", y[:, qtr * 4 * T:(qtr + 1) * 4 * T], sl, inc="st",
              wait=[("vo2", p.tok("vo2")), ("vdn", p.tok("vdn"))])
    p.emit("y", None, wait=[("st", p.tok("st"))])

    p.replay()
    return nc


def _tile_lhsT(w):
    K, M = w.shape
    kb, mb_ = K // 128, M // 128
    t = w.reshape(kb, 128, mb_, 128).transpose(2, 1, 0, 3)
    return np.ascontiguousarray(t.reshape(mb_, 128, kb * 128))


def prep_inputs(inp):
    x = np.asarray(inp["L0_attn_x"])
    Wqkv = np.asarray(inp["L0_attn_W_qkv"])
    bqkv = np.asarray(inp["L0_attn_b_qkv"])
    Wproj = np.asarray(inp["L0_attn_W_proj"])
    bproj = np.asarray(inp["L0_attn_b_proj"])
    gw = np.asarray(inp["L0_ffn_gate_w"])
    gs = np.asarray(inp["L0_ffn_gate_scale"])
    uw = np.asarray(inp["L0_ffn_up_w"])
    us = np.asarray(inp["L0_ffn_up_scale"])
    dw = np.asarray(inp["L0_ffn_down_w"])
    ds = np.asarray(inp["L0_ffn_down_scale"])

    wq0 = _tile_lhsT(Wqkv[:, 0:2048])
    wk0 = _tile_lhsT(Wqkv[:, 2048:4096])
    wv0 = np.ascontiguousarray(
        Wqkv[:, 4096:6144].reshape(NCH, 128, 4, 512).transpose(2, 0, 1, 3))
    bq0 = np.ascontiguousarray(bqkv[0:2048].reshape(16, 128))
    bk0 = np.ascontiguousarray(bqkv[2048:4096].reshape(16, 128))
    bv0 = np.ascontiguousarray(bqkv[4096:6144].reshape(1, 2048))
    wp0 = _tile_lhsT(Wproj)
    bp0 = np.ascontiguousarray(bproj.reshape(16, 128))
    wg_ = _tile_lhsT(gw.T)
    wu_ = _tile_lhsT(uw.T)
    wdT = dw.T
    wd_ = wdT.reshape(4, 16, 128, 16, 128)
    wd_ = np.ascontiguousarray(wd_.transpose(0, 3, 2, 1, 4).reshape(4, 16, 128, 16 * 128))
    gsc_ = np.ascontiguousarray(gs.reshape(64, 128))
    usc_ = np.ascontiguousarray(us.reshape(64, 128))
    dsc_ = np.ascontiguousarray(ds.reshape(16, 128))
    ident = np.eye(128, dtype=np.float32).astype(ml_dtypes.bfloat16)

    Wpack = np.asarray(inp["L1_attn_W_pack"])
    Wo = np.asarray(inp["L1_attn_W_o"])
    convk = np.asarray(inp["L1_attn_conv_k"]).astype(np.float32)
    convv = np.asarray(inp["L1_attn_conv_v"]).astype(np.float32)
    w1 = np.asarray(inp["L1_ffn_weight"])
    wscale = np.asarray(inp["L1_ffn_weight_scale"]).astype(np.float32)
    iscale = np.asarray(inp["L1_ffn_input_scale"])

    wpq_ = _tile_lhsT(Wpack[:, 0:2048])
    wpk_ = _tile_lhsT(Wpack[:, 2048:3072])
    wpv_ = _tile_lhsT(Wpack[:, 3072:4096])
    wod_ = _tile_lhsT(Wo)
    w1T = np.ascontiguousarray(w1.T)
    w1d_ = _tile_lhsT(w1T.view(np.uint8)).view(w1.dtype)
    protm = np.zeros((128, 128), np.float32)
    for i in range(64):
        protm[i, i + 64] = 1.0
        protm[i + 64, i] = -1.0
    protd_ = protm.astype(ml_dtypes.bfloat16)
    convd_ = np.concatenate([convk.reshape(-1), convv.reshape(-1)]).reshape(1, 32).astype(np.float32)
    iscl_ = np.asarray(iscale).reshape(1, 1)
    wscl_ = wscale.reshape(1, 1)
    SCHf = np.float32(SCH)
    inv_freq = (1.0 / 100000.0 ** (np.arange(0, 128, 2, dtype=np.float32) / 128.0)).astype(np.float32)

    shared = dict(wq0=wq0, wk0=wk0, wv0=wv0, bq0=bq0, bk0=bk0, bv0=bv0,
                  wp0=wp0, bp0=bp0, wg=wg_, wu=wu_, wd=wd_,
                  gsc=gsc_, usc=usc_, dsc=dsc_, ident=ident,
                  wpq=wpq_, wpk=wpk_, wpv=wpv_, wod=wod_, w1d=w1d_,
                  protd=protd_, convd=convd_, iscl=iscl_, wscl=wscl_)
    in_maps = []
    for c in range(8):
        b, pp = c // 2, c % 2
        xo = np.ascontiguousarray(x[b, 1024 * pp:1024 * (pp + 1), :].T)
        xp = np.ascontiguousarray(x[b, 1024 * (1 - pp):1024 * (2 - pp), :].T)
        pos = (1024 * pp + np.arange(1024, dtype=np.float32))
        fr = np.outer(pos, inv_freq)
        cosf = np.cos(fr).astype(np.float32) * SCHf
        sinf = np.sin(fr).astype(np.float32) * SCHf
        cosd_ = np.ascontiguousarray(np.concatenate([cosf.T, cosf.T], axis=0))
        sind_ = np.ascontiguousarray(np.concatenate([sinf.T, sinf.T], axis=0))
        mask = np.full((8, 9, 128, 128), -1e30, np.float32)
        for t in range(8):
            for l in range(9):
                s = (t + l + 8) % 16
                j0 = 1024 * pp + 128 * s if s < 8 else 1024 * (1 - pp) + 128 * (s - 8)
                i_ = 1024 * pp + 128 * t + np.arange(128)[None, :]
                j_ = j0 + np.arange(128)[:, None]
                valid = (j_ <= i_) & (i_ - j_ < 1024)
                mask[t, l][valid] = 0.0
        maskd_ = np.ascontiguousarray(mask.transpose(2, 0, 1, 3).reshape(128, 9216)).astype(ml_dtypes.bfloat16)
        flagd_ = np.full((128, 2), 0.0, np.float32)
        flagd_[:, 0] = float(pp)
        flagd_[:, 1] = 1.0 - float(pp)
        m = dict(shared)
        m["x_own"] = xo
        m["x_part"] = xp
        m["cosd"] = cosd_
        m["sind"] = sind_
        m["maskd"] = maskd_
        m["flagd"] = flagd_
        in_maps.append(m)
    return in_maps


def assemble_output(results):
    out = np.empty((4, 2048, 2048), dtype=np.float32)
    for c in range(8):
        b, pp = c // 2, c % 2
        sh = results[c]["y"].reshape(128, 16, 1024)
        out[b, 1024 * pp:1024 * (pp + 1), :] = sh.transpose(2, 1, 0).reshape(1024, 2048)
    return out


def kernel(**inputs):
    in_maps = prep_inputs(inputs)
    nc = build_nc(dbg_mode=None)
    res = run_bass_kernel_spmd(nc, in_maps, core_ids=list(range(8)))
    return assemble_output(res.results)
